# version 01
# baseline (speedup 1.0000x reference)
import math
from contextlib import ExitStack

import numpy as np
import concourse.bass as bass
import concourse.mybir as mybir
from concourse.bass_utils import run_bass_kernel_spmd

F32 = mybir.dt.float32
BF16 = mybir.dt.bfloat16
U32 = mybir.dt.uint32
AF = mybir.ActivationFunctionType
ALU = mybir.AluOpType
AXX = mybir.AxisListType.X

S = 2048
D = 1024
NT = S // 128
ND = D // 128
NCORES = 8


def _bf16(x):
    import ml_dtypes
    return np.ascontiguousarray(np.asarray(x, np.float32)).astype(ml_dtypes.bfloat16)


class Prog:
    ENGS = ("tn", "sc", "vc", "go", "sy")

    def __init__(self):
        self.nc = bass.Bass("TRN2", target_bir_lowering=False)
        self.ctx = ExitStack()
        self.sems = {}
        self.cnt = {}
        for name in ("dmi", "dmo", "dmg", "tn", "sc", "vc", "go", "sy"):
            self.sems[name] = self.ctx.enter_context(self.nc.semaphore(name))
            self.cnt[name] = 0
        self.last_instr = {}
        self.pending = {n: {} for n in self.ENGS}

    def sbuf(self, name, shape, dt=F32, stack=None):
        return (stack or self.ctx).enter_context(self.nc.sbuf_tensor(name, shape, dt))

    def psum(self, name, shape, dt=F32, stack=None):
        return (stack or self.ctx).enter_context(self.nc.psum_tensor(name, shape, dt))

    def dram(self, name, shape, dt=F32, kind="Internal"):
        return self.nc.dram_tensor(name, shape, dt, kind=kind).ap()

    def inc(self, instr, sem, k=1):
        amt = 16 * k if sem.startswith("dm") else k
        instr.then_inc(self.sems[sem], amt)
        self.cnt[sem] += amt
        if sem in self.ENGS:
            self.last_instr.pop(sem, None)
        return self.cnt[sem]

    def track(self, instr, eng_sem):
        self.last_instr[eng_sem] = instr
        return instr

    def wait(self, eng, sem, val=None):
        v = self.cnt[sem] if val is None else val
        if v > 0:
            eng.wait_ge(self.sems[sem], v)

    def drain(self, eng, eng_sem):
        pend = self.pending[eng_sem]
        for sem, val in sorted(pend.items()):
            eng.wait_ge(self.sems[sem], val)
        pend.clear()

    def barrier(self):
        for es in self.ENGS:
            ins = self.last_instr.pop(es, None)
            if ins is not None:
                self.inc(ins, es)
        counts = {s: self.cnt[s] for s in ("dmi", "dmo", "dmg") if self.cnt[s] > 0}
        counts.update({es: self.cnt[es] for es in self.ENGS if self.cnt[es] > 0})
        for es in self.ENGS:
            pend = self.pending[es]
            for sem, val in counts.items():
                pend[sem] = max(pend.get(sem, 0), val)



def ph_layernorm(p, block, x_dram, xn_dram, scope, eps_ap, nhalf_ap, keep_var=None):
    xt = [p.sbuf(f"lnx{k}", [128, D], F32, scope) for k in range(2)]
    sq = [p.sbuf(f"lnsq{k}", [128, D], F32, scope) for k in range(2)]
    xn = [p.sbuf(f"lnxn{k}", [128, D], BF16, scope) for k in range(2)]
    s1 = p.sbuf("lns1", [128, 1], F32, scope)
    s2 = p.sbuf("lns2", [128, 1], F32, scope)
    mcol = p.sbuf("lnm", [128, 1], F32, scope)
    msq = p.sbuf("lnmsq", [128, 1], F32, scope)
    var = p.sbuf("lnvar", [128, 1], F32, scope)
    rstd = p.sbuf("lnrstd", [128, 1], F32, scope)

    lnv = p.sbuf("lnlnv", [128, 1], F32, scope)
    dmi0, vc0, sc0, dmo0 = (p.cnt[s] for s in ("dmi", "vc", "sc", "dmo"))
    NV = 6

    @block.sync
    def _(sy):
        p.drain(sy, "sy")
        for i in range(NT):
            if i >= 2:
                sy.wait_ge(p.sems["vc"], vc0 + NV * (i - 2) + NV)
            ins = sy.dma_start(out=xt[i % 2][:, :],
                               in_=x_dram[i * 128:(i + 1) * 128, :])
            p.inc(ins, "dmi")

    @block.scalar
    def _(s):
        p.drain(s, "sc")
        for i in range(NT):
            s.wait_ge(p.sems["dmi"], dmi0 + 16 * (i + 1))
            if i >= 2:
                s.wait_ge(p.sems["vc"], vc0 + NV * (i - 2) + 2)
            ins = s.activation(sq[i % 2][:, :], xt[i % 2][:, :], AF.Square)
            p.inc(ins, "sc")
            s.wait_ge(p.sems["vc"], vc0 + NV * i + 5)
            ins = s.activation(lnv[:, :], var[:, :], AF.Ln, bias=eps_ap)
            p.inc(ins, "sc")
            s.wait_ge(p.sems["sc"], p.cnt["sc"])
            ins = s.activation(rstd[:, :], lnv[:, :], AF.Exp, scale=nhalf_ap)
            p.inc(ins, "sc")

    @block.vector
    def _(v):
        p.drain(v, "vc")
        for i in range(NT):
            v.wait_ge(p.sems["dmi"], dmi0 + 16 * (i + 1))
            ins = v.reduce_sum(s1[:, :], xt[i % 2][:, :], axis=AXX)
            p.inc(ins, "vc")
            v.wait_ge(p.sems["sc"], sc0 + 3 * i + 1)
            ins = v.reduce_sum(s2[:, :], sq[i % 2][:, :], axis=AXX)
            p.inc(ins, "vc")
            v.wait_ge(p.sems["vc"], p.cnt["vc"])
            ins = v.tensor_scalar_mul(mcol[:, :], s1[:, :], 1.0 / D)
            p.inc(ins, "vc")
            v.wait_ge(p.sems["vc"], p.cnt["vc"])
            ins = v.tensor_scalar_mul(msq[:, :], mcol[:, :], mcol[:, :])
            p.inc(ins, "vc")
            v.wait_ge(p.sems["vc"], p.cnt["vc"])
            ins = v.tensor_scalar(var[:, :], s2[:, :], 1.0 / D, msq[:, :],
                                  ALU.mult, ALU.subtract)
            p.inc(ins, "vc")
            if keep_var is not None:
                v.wait_ge(p.sems["vc"], p.cnt["vc"])
                v.tensor_copy(keep_var[:, i:i + 1], var[:, :])
            v.wait_ge(p.sems["sc"], sc0 + 3 * i + 3)
            if i >= 2:
                v.wait_ge(p.sems["dmo"], dmo0 + 16 * (i - 1))
            ins = v.tensor_scalar(xn[i % 2][:, :], xt[i % 2][:, :],
                                  mcol[:, :], rstd[:, :], ALU.subtract, ALU.mult)
            p.inc(ins, "vc")

    @block.gpsimd
    def _(g):
        p.drain(g, "go")
        for i in range(NT):
            g.wait_ge(p.sems["vc"], vc0 + NV * i + 6)
            ins = g.dma_start(out=xn_dram[i * 128:(i + 1) * 128, :],
                              in_=xn[i % 2][:, :])
            p.inc(ins, "dmo")


def ph_load_xnT(p, block, xn_dram, xnT, nchunk=ND, rows=S):
    @block.sync
    def _(sy):
        p.drain(sy, "sy")
        for c in range(nchunk):
            ins = sy.dma_start_transpose(
                xnT[:, c * rows:(c + 1) * rows],
                xn_dram[:, c * 128:(c + 1) * 128])
            p.inc(ins, "dmi")


def ph_load_weight_fm(p, block, scope, w_dram, K, M, name, stage=None, out=None):
    if K < 128:
        if out is None:
            out = p.sbuf(name + "_b", [K, M], BF16, scope)
        if stage is None:
            stage = p.sbuf(name + "_st", [K, M], F32, scope)

        @block.sync
        def _(sy):
            p.drain(sy, "sy")
            ins = sy.dma_start(out=stage[0:K, 0:M], in_=w_dram)
            p.inc(ins, "dmi")
        ld = p.cnt["dmi"]

        @block.vector
        def _(v):
            p.drain(v, "vc")
            v.wait_ge(p.sems["dmi"], ld)
            ins = v.tensor_copy(out[0:K, 0:M], stage[0:K, 0:M])
            p.inc(ins, "vc")
        return out

    nch = K // 128
    if out is None:
        out = p.sbuf(name + "_b", [128, nch * M], BF16, scope)
    if stage is None:
        stage = p.sbuf(name + "_st", [128, nch * M], F32, scope)

    @block.sync
    def _(sy):
        p.drain(sy, "sy")
        ins = sy.dma_start(
            out=stage[:, 0:nch * M].rearrange("p (c m) -> p c m", c=nch),
            in_=w_dram.rearrange("(c p) m -> p c m", p=128))
        p.inc(ins, "dmi")
    ld = p.cnt["dmi"]

    @block.vector
    def _(v):
        p.drain(v, "vc")
        v.wait_ge(p.sems["dmi"], ld)
        ins = v.tensor_copy(out[:, 0:nch * M], stage[:, 0:nch * M])
        p.inc(ins, "vc")
    return out



LAM = 1e-4
SCALE0 = 1.0 / (math.sqrt(1024.0) + 1e-8)


def tile_kind(i, j):
    kmin, kmax = 128 * i, 128 * i + 127
    qmin, qmax = 512 * j, 512 * j + 511
    if kmax < qmin:
        return "lo"
    if kmin > qmax:
        return "hi"
    return "mix"


def l0_consts():
    t = np.arange(S, dtype=np.float64)
    consts = {
        "c_eql": _bf16(np.broadcast_to(SCALE0 * np.exp(-LAM * t), (128, S))),
        "c_eqh": _bf16(np.broadcast_to(SCALE0 * np.exp(+LAM * t), (128, S))),
        "c_ekl": _bf16(np.broadcast_to(np.exp(+LAM * t), (128, S))),
        "c_ekh": _bf16(np.broadcast_to(np.exp(-LAM * t), (128, S))),
        "c_ident": np.eye(128, dtype=np.float32),
        "c_ones1x128": np.ones((1, 128), np.float32),
    }
    corr = []
    for o in (0, 128, 256, 384):
        kt = np.arange(128)[:, None] + o
        qt = np.arange(512)[None, :]
        corr.append(np.minimum(1.0, np.exp(-2 * LAM * (kt - qt))))
    consts["c_corr"] = np.stack(corr).astype(np.float32)
    e = np.zeros((16, 8 * 128), np.float32)
    for c in range(8):
        e[2 * c, c * 128:c * 128 + 64] = 1.0
        e[2 * c + 1, c * 128 + 64:c * 128 + 128] = 1.0
    consts["c_zexp"] = e
    cc = np.zeros((128, 2), np.float32)
    cc[:, 0] = 1e-5
    cc[:, 1] = -0.5
    consts["c_ccol"] = cc
    return consts


def build_l0(debug=False):
    p = Prog()
    nc = p.nc
    x_dram = p.dram("x", [S, D], F32, kind="ExternalInput")
    wq_d = p.dram("Wq", [D, D], F32, kind="ExternalInput")
    wk_d = p.dram("Wk", [D, D], F32, kind="ExternalInput")
    wv_d = p.dram("Wv", [D, D], F32, kind="ExternalInput")
    wo_d = p.dram("Wo", [D, D], F32, kind="ExternalInput")
    wdn_d = p.dram("Wdn", [D, 256], F32, kind="ExternalInput")
    wuk_d = p.dram("Wuk", [256, D], F32, kind="ExternalInput")
    wuv_d = p.dram("Wuv", [256, D], F32, kind="ExternalInput")
    win_d = p.dram("Win", [D, 64], F32, kind="ExternalInput")
    bin_d = p.dram("bin", [64, 1], F32, kind="ExternalInput")
    w1_d = p.dram("W1", [64, 16], F32, kind="ExternalInput")
    b1_d = p.dram("b1", [64, 16], F32, kind="ExternalInput")
    w2_d = p.dram("W2", [64, 16], F32, kind="ExternalInput")
    b2_d = p.dram("b2", [64, 1], F32, kind="ExternalInput")
    wout_d = p.dram("Wout", [64, D], F32, kind="ExternalInput")
    bout_d = p.dram("bout", [1, D], F32, kind="ExternalInput")
    eql_d = p.dram("c_eql", [128, S], BF16, kind="ExternalInput")
    eqh_d = p.dram("c_eqh", [128, S], BF16, kind="ExternalInput")
    ekl_d = p.dram("c_ekl", [128, S], BF16, kind="ExternalInput")
    ekh_d = p.dram("c_ekh", [128, S], BF16, kind="ExternalInput")
    corr_d = p.dram("c_corr", [4, 128, 512], F32, kind="ExternalInput")
    ident_d = p.dram("c_ident", [128, 128], F32, kind="ExternalInput")
    ones_d = p.dram("c_ones1x128", [1, 128], F32, kind="ExternalInput")
    zexp_d = p.dram("c_zexp", [16, 8 * 128], F32, kind="ExternalInput")

    x1_dram = p.dram("x1", [S, D], F32, kind="ExternalOutput")
    dk = "ExternalOutput" if debug else "Internal"
    xn_dram = p.dram("xn_scr", [S, D], BF16, kind=dk)
    g_dram = p.dram("g_scr", [S, D], BF16, kind=dk)
    z_dram = p.dram("z_scr", [1, 16 * S], F32, kind=dk)
    at_dram = p.dram("at_scr", [128, ND * S], BF16, kind=dk) if debug else None
    st_dram = p.dram("st_scr", [64, S], F32, kind=dk) if debug else None
    acc_dram = p.dram("acc_scr", [64, S], F32, kind=dk) if debug else None
    vx_dram = p.dram("vx_scr", [128, NT * 1040], BF16, kind=dk) if debug else None
    lat_dram = p.dram("lat_scr", [128, 2 * S], BF16, kind=dk) if debug else None

    xnT = p.sbuf("xnT", [128, ND * S], BF16)
    ones128 = p.sbuf("ones128", [1, 128], F32)
    ident = p.sbuf("ident", [128, 128], F32)
    latT = p.sbuf("latT", [128, 2 * S], BF16)
    attnT = p.sbuf("attnT", [128, ND * S], BF16)

    ccol = p.sbuf("ccol", [128, 2], F32)
    ccol_d = p.dram("c_ccol", [128, 2], F32, kind="ExternalInput")

    with nc.Block() as block:
        @block.sync
        def _(sy):
            ins = sy.dma_start(out=ccol[:, :], in_=ccol_d)
            p.inc(ins, "dmi")
        p.barrier()

        with ExitStack() as sc0:
            ph_layernorm(p, block, x_dram, xn_dram, sc0, ccol[:, 0:1], ccol[:, 1:2])
            p.barrier()

        ph_load_xnT(p, block, xn_dram, xnT)

        @block.sync
        def _(sy):
            ins = sy.dma_start(out=ones128[:, :], in_=ones_d)
            p.inc(ins, "dmi")
            ins = sy.dma_start(out=ident[:, :], in_=ident_d)
            p.inc(ins, "dmi")
        p.barrier()

        with ExitStack() as sc2:
            winb = ph_load_weight_fm(p, block, sc2, win_d, D, 64, "win")
            woutb = ph_load_weight_fm(p, block, sc2, wout_d, 64, D, "wout")
            small = p.sbuf("ffn_small", [64, 52], F32, sc2)
            bout_row = p.sbuf("bout_row", [1, D], F32, sc2)

            @block.sync
            def _(sy):
                p.drain(sy, "sy")
                for dram_, c0, w in ((w1_d, 0, 16), (b1_d, 16, 16), (w2_d, 32, 16),
                                     (bin_d, 48, 1), (b2_d, 49, 1)):
                    ins = sy.dma_start(out=small[:, c0:c0 + w], in_=dram_)
                    p.inc(ins, "dmi")
                ins = sy.dma_start(out=bout_row[:, :], in_=bout_d)
                p.inc(ins, "dmi")
            p.barrier()

            sT = p.sbuf("sT", [64, S], F32, sc2)
            acc = p.sbuf("acc", [64, S], F32, sc2)
            tmp = p.sbuf("ffntmp", [64, S], F32, sc2)
            gTb = p.sbuf("gTb", [64, S], BF16, sc2)
            bout_bc = p.sbuf("bout_bc", [128, D], F32, sc2)
            ps_s = p.psum("ps_s", [64, 512], F32, sc2)
            ps_bc = p.psum("ps_bc", [128, 512], F32, sc2)
            ps_g = [p.psum(f"ps_g{k}", [128, 512], F32, sc2) for k in range(2)]

            for j in range(4):
                @block.tensor
                def _(t, j=j):
                    p.drain(t, "tn")
                    if j > 0:
                        p.wait(t, "sc")
                    ins = None
                    for c in range(ND):
                        ins = t.matmul(ps_s[:, :], lhsT=winb[:, c * 64:(c + 1) * 64],
                                       rhs=xnT[:, c * S + j * 512: c * S + (j + 1) * 512],
                                       start=(c == 0), stop=(c == ND - 1))
                    p.inc(ins, "tn")

                @block.scalar
                def _(s, j=j):
                    p.drain(s, "sc")
                    p.wait(s, "tn")
                    ins = s.activation(sT[:, j * 512:(j + 1) * 512], ps_s[:, :],
                                       AF.Identity, bias=small[:, 48:49])
                    p.inc(ins, "sc")

            for dh in range(2):
                @block.tensor
                def _(t, dh=dh):
                    p.drain(t, "tn")
                    if dh > 0:
                        p.wait(t, "vc")
                    ins = t.matmul(ps_bc[:, :], lhsT=ones128[:, :],
                                   rhs=bout_row[:, dh * 512:(dh + 1) * 512],
                                   start=True, stop=True)
                    p.inc(ins, "tn")

                @block.vector
                def _(v, dh=dh):
                    p.drain(v, "vc")
                    p.wait(v, "tn")
                    ins = v.tensor_copy(bout_bc[:, dh * 512:(dh + 1) * 512],
                                        ps_bc[:, :])
                    p.inc(ins, "vc")
            p.barrier()

            tmp2 = p.sbuf("ffntmp2", [64, S], F32, sc2)
            sc_i = p.cnt["sc"]
            vc_i = p.cnt["vc"]

            @block.scalar
            def _(s):
                p.drain(s, "sc")
                for i in range(16):
                    buf = tmp if i % 2 == 0 else tmp2
                    if i >= 2:
                        s.wait_ge(p.sems["vc"], vc_i + i - 1)
                    ins = s.activation(buf[:, :], sT[:, :], AF.Relu,
                                       bias=small[:, 16 + i:17 + i],
                                       scale=small[:, i:i + 1])
                    p.inc(ins, "sc")

            @block.vector
            def _(v):
                p.drain(v, "vc")
                for i in range(16):
                    buf = tmp if i % 2 == 0 else tmp2
                    v.wait_ge(p.sems["sc"], sc_i + i + 1)
                    if i == 0:
                        ins = v.tensor_scalar_mul(acc[:, :], buf[:, :],
                                                  small[:, 32:33])
                    else:
                        v.wait_ge(p.sems["vc"], p.cnt["vc"])
                        ins = v.scalar_tensor_tensor(acc[:, :], buf[:, :],
                                                     small[:, 32 + i:33 + i],
                                                     acc[:, :], ALU.mult, ALU.add)
                    p.inc(ins, "vc")
                v.wait_ge(p.sems["vc"], p.cnt["vc"])
                ins = v.tensor_scalar(gTb[:, :], acc[:, :], small[:, 49:50], None,
                                      ALU.add)
                p.inc(ins, "vc")
            p.barrier()

            gt = [p.sbuf(f"gt{k}", [128, D], BF16, sc2) for k in range(2)]
            gtmp = p.sbuf("gtmp", [128, D], F32, sc2)
            vc_b, sc_b, tn_b, dmo_b = (p.cnt[s] for s in ("vc", "sc", "tn", "dmo"))

            @block.tensor
            def _(t):
                p.drain(t, "tn")
                for i in range(NT):
                    for dh in range(2):
                        n = 2 * i + dh
                        if n >= 2:
                            t.wait_ge(p.sems["vc"], vc_b + n - 1)
                        ins = t.matmul(ps_g[dh][:, :],
                                       lhsT=gTb[:, i * 128:(i + 1) * 128],
                                       rhs=woutb[:, dh * 512:(dh + 1) * 512],
                                       start=True, stop=True)
                        p.inc(ins, "tn")

            @block.vector
            def _(v):
                p.drain(v, "vc")
                for i in range(NT):
                    for dh in range(2):
                        n = 2 * i + dh
                        v.wait_ge(p.sems["tn"], tn_b + n + 1)
                        if i >= 1 and dh == 0:
                            v.wait_ge(p.sems["sc"], sc_b + i)
                        ins = v.tensor_add(gtmp[:, dh * 512:(dh + 1) * 512],
                                           ps_g[dh][:, :],
                                           bout_bc[:, dh * 512:(dh + 1) * 512])
                        p.inc(ins, "vc")

            @block.scalar
            def _(s):
                p.drain(s, "sc")
                for i in range(NT):
                    s.wait_ge(p.sems["vc"], vc_b + 2 * i + 2)
                    if i >= 2:
                        s.wait_ge(p.sems["dmo"], dmo_b + 16 * (i - 1))
                    ins = s.activation(gt[i % 2][:, :], gtmp[:, :], AF.Silu)
                    p.inc(ins, "sc")

            @block.gpsimd
            def _(g):
                p.drain(g, "go")
                for i in range(NT):
                    g.wait_ge(p.sems["sc"], sc_b + i + 1)
                    ins = g.dma_start(out=g_dram[i * 128:(i + 1) * 128, :],
                                      in_=gt[i % 2][:, :])
                    p.inc(ins, "dmo")
                    p.barrier()

        with ExitStack() as sc3:
            kvT = p.sbuf("kvT", [128, ND * S], BF16, sc3)
            stage = p.sbuf("wstage", [128, 8192], F32, sc3)
            ps_p = [p.psum(f"ps_p{k}", [128, 512], F32, sc3) for k in range(2)]

            def proj_evac(wb, rhs_buf, rhs_len, nch_r, ngrp, dst, dst_len,
                          mode, wM):
                tn_b = p.cnt["tn"]
                ev_b = p.cnt["sc" if mode == "copy" else "vc"]
                ev_sem = "sc" if mode == "copy" else "vc"

                @block.tensor
                def _(t):
                    p.drain(t, "tn")
                    n = 0
                    for g8 in range(ngrp):
                        for j in range(4):
                            bk = n % 2
                            if n >= 2:
                                t.wait_ge(p.sems[ev_sem], ev_b + n - 1)
                            ins = None
                            for c in range(nch_r):
                                ins = t.matmul(
                                    ps_p[bk][:, :],
                                    lhsT=wb[:, c * wM + g8 * 128: c * wM + (g8 + 1) * 128],
                                    rhs=rhs_buf[:, c * rhs_len + j * 512:
                                                c * rhs_len + (j + 1) * 512],
                                    start=(c == 0), stop=(c == nch_r - 1))
                            p.inc(ins, "tn")
                            n += 1

                if mode == "copy":
                    @block.scalar
                    def _(s):
                        p.drain(s, "sc")
                        n = 0
                        for g8 in range(ngrp):
                            for j in range(4):
                                bk = n % 2
                                s.wait_ge(p.sems["tn"], tn_b + n + 1)
                                ins = s.copy(
                                    dst[:, g8 * dst_len + j * 512:
                                        g8 * dst_len + (j + 1) * 512],
                                    ps_p[bk][:, :])
                                p.inc(ins, "sc")
                                n += 1
                else:
                    @block.vector
                    def _(v):
                        p.drain(v, "vc")
                        n = 0
                        for g8 in range(ngrp):
                            for j in range(4):
                                bk = n % 2
                                v.wait_ge(p.sems["tn"], tn_b + n + 1)
                                sl = dst[:, g8 * dst_len + j * 512:
                                         g8 * dst_len + (j + 1) * 512]
                                ins = v.tensor_add(sl, sl, ps_p[bk][:, :])
                                p.inc(ins, "vc")
                                n += 1

            wkb = ph_load_weight_fm(p, block, sc3, wk_d, D, D, "wk", stage=stage)
            p.barrier()
            proj_evac(wkb, xnT, S, ND, ND, kvT, S, "copy", D)
            p.barrier()
            wvb = ph_load_weight_fm(p, block, sc3, wv_d, D, D, "wv", stage=stage,
                                    out=wkb)
            p.barrier()
            proj_evac(wvb, xnT, S, ND, ND, kvT, S, "add", D)
            p.barrier()
            wdnb = ph_load_weight_fm(p, block, sc3, wdn_d, D, 256, "wdn",
                                     stage=stage)
            p.barrier()
            proj_evac(wdnb, kvT, S, ND, 2, latT, S, "copy", 256)
            p.barrier()

        vext = p.sbuf("vext", [128, NT * 1040], BF16)
        with ExitStack() as sc3b:
            wuvb = ph_load_weight_fm(p, block, sc3b, wuv_d, 256, D, "wuv")
            p.barrier()
            ps_p2 = [p.psum(f"ps_q{k}", [128, 512], F32, sc3b) for k in range(2)]
            tn_b = p.cnt["tn"]
            sc_b = p.cnt["sc"]

            @block.vector
            def _(v):
                p.drain(v, "vc")
                for i in range(NT):
                    for h in range(16):
                        off = i * 1040 + h * 65 + 64
                        ins = v.memset(vext[:, off:off + 1], 1.0)
                        p.inc(ins, "vc")

            @block.tensor
            def _(t):
                p.drain(t, "tn")
                n = 0
                for i in range(NT):
                    for dh in range(2):
                        bk = n % 2
                        if n >= 2:
                            t.wait_ge(p.sems["sc"], sc_b + n - 1)
                        ins = None
                        for c2 in range(2):
                            ins = t.matmul(
                                ps_p2[bk][:, :],
                                lhsT=latT[:, c2 * S + i * 128: c2 * S + (i + 1) * 128],
                                rhs=wuvb[:, c2 * D + dh * 512: c2 * D + (dh + 1) * 512],
                                start=(c2 == 0), stop=(c2 == 1))
                        p.inc(ins, "tn")
                        n += 1

            @block.scalar
            def _(s):
                p.drain(s, "sc")
                p.wait(s, "vc")
                n = 0
                for i in range(NT):
                    for dh in range(2):
                        bk = n % 2
                        s.wait_ge(p.sems["tn"], tn_b + n + 1)
                        last = None
                        for h8 in range(8):
                            h = dh * 8 + h8
                            last = s.copy(
                                vext[:, i * 1040 + h * 65: i * 1040 + h * 65 + 64],
                                ps_p2[bk][:, h8 * 64:(h8 + 1) * 64])
                        p.inc(last, "sc")
                        n += 1
            p.barrier()

        with ExitStack() as sc4:
            wqb = p.sbuf("wq_b", [128, ND * D], BF16, sc4)
            wukb = p.sbuf("wuk_b", [128, 2 * D], BF16, sc4)
            with ExitStack() as scw:
                stage2 = p.sbuf("wstage2", [128, 8192], F32, scw)
                ph_load_weight_fm(p, block, scw, wq_d, D, D, "wq",
                                  stage=stage2, out=wqb)
                p.barrier()
                ph_load_weight_fm(p, block, scw, wuk_d, 256, D, "wuk",
                                  stage=stage2, out=wukb)
                p.barrier()
            esc = {}
            for nm, dr in (("eql", eql_d), ("eqh", eqh_d), ("ekl", ekl_d),
                           ("ekh", ekh_d)):
                esc[nm] = p.sbuf("e_" + nm, [128, S], BF16, sc4)
            corr = p.sbuf("corr", [128, 4 * 512], F32, sc4)
            zexp = p.sbuf("zexp", [16, 8 * 128], F32, sc4)

            @block.sync
            def _(sy):
                p.drain(sy, "sy")
                for nm, dr in (("eql", eql_d), ("eqh", eqh_d), ("ekl", ekl_d),
                               ("ekh", ekh_d)):
                    ins = sy.dma_start(out=esc[nm][:, :], in_=dr)
                    p.inc(ins, "dmi")
                for o in range(4):
                    ins = sy.dma_start(out=corr[:, o * 512:(o + 1) * 512],
                                       in_=corr_d[o, :, :])
                    p.inc(ins, "dmi")
                ins = sy.dma_start(out=zexp[:, :], in_=zexp_d)
                p.inc(ins, "dmi")
            p.barrier()

            qlo = p.sbuf("qlo", [128, S], BF16, sc4)
            qhi = p.sbuf("qhi", [128, S], BF16, sc4)
            klo = p.sbuf("klo", [128, S], BF16, sc4)
            khi = p.sbuf("khi", [128, S], BF16, sc4)
            zrow = p.sbuf("zrow", [1, 512], F32, sc4)
            acol = p.sbuf("acol", [128, 1], F32, sc4)
            bcol = p.sbuf("bcol", [128, 1], F32, sc4)
            abmax = p.sbuf("abmax", [128, 1], F32, sc4)
            m_q = p.sbuf("m_q", [1, 1], F32, sc4)
            m_k = p.sbuf("m_k", [1, 1], F32, sc4)
            negM = p.sbuf("negM", [1, 1], F32, sc4)
            biascol = p.sbuf("biascol", [128, 1], F32, sc4)
            pt = [p.sbuf(f"pt{k}", [128, 512], BF16, sc4) for k in range(2)]
            stmp = p.sbuf("stmp", [128, 512], F32, sc4)

            with ExitStack() as scp:
                ps_pr = [p.psum(f"ps_pr{k}", [128, 512], F32, scp) for k in range(2)]
                ps_sc = [p.psum(f"ps_sc{k}", [128, 512], F32, scp) for k in range(2)]
                ps_o = p.psum("ps_o", [65, 512], F32, scp)
                ps_t = p.psum("ps_t", [1, 128], F32, scp)
                ps_b = p.psum("ps_b", [128, 1], F32, scp)

                for g8 in range(8):
                    tn_b = p.cnt["tn"]
                    vc_b = p.cnt["vc"]

                    @block.tensor
                    def _(t, g8=g8, tn_b=tn_b, vc_b=vc_b):
                        p.drain(t, "tn")
                        n = 0
                        for j in range(4):
                            bk = n % 2
                            if n >= 2:
                                t.wait_ge(p.sems["vc"], vc_b + 2 * (n - 1))
                            ins = None
                            for c in range(ND):
                                ins = t.matmul(
                                    ps_pr[bk][:, :],
                                    lhsT=wqb[:, c * D + g8 * 128: c * D + (g8 + 1) * 128],
                                    rhs=xnT[:, c * S + j * 512: c * S + (j + 1) * 512],
                                    start=(c == 0), stop=(c == ND - 1))
                            p.inc(ins, "tn")
                            n += 1
                        for j in range(4):
                            bk = n % 2
                            t.wait_ge(p.sems["vc"], vc_b + 2 * (n - 1))
                            ins = None
                            for c2 in range(2):
                                ins = t.matmul(
                                    ps_pr[bk][:, :],
                                    lhsT=wukb[:, c2 * D + g8 * 128: c2 * D + (g8 + 1) * 128],
                                    rhs=latT[:, c2 * S + j * 512: c2 * S + (j + 1) * 512],
                                    start=(c2 == 0), stop=(c2 == 1))
                            p.inc(ins, "tn")
                            n += 1

                    @block.vector
                    def _(v, tn_b=tn_b):
                        p.drain(v, "vc")
                        n = 0
                        for dst_lo, dst_hi, e_lo, e_hi in (
                                (qlo, qhi, "eql", "eqh"), (klo, khi, "ekl", "ekh")):
                            for j in range(4):
                                bk = n % 2
                                v.wait_ge(p.sems["tn"], tn_b + n + 1)
                                sl = slice(j * 512, (j + 1) * 512)
                                v.tensor_mul(dst_lo[:, sl], ps_pr[bk][:, :],
                                             esc[e_lo][:, sl])
                                ins = v.tensor_mul(dst_hi[:, sl], ps_pr[bk][:, :],
                                                   esc[e_hi][:, sl])
                                p.inc(ins, "vc", 2)
                                n += 1
                    p.barrier()

                    for side, (t_lo, t_hi) in (("q", (qlo, qhi)), ("k", (klo, khi))):
                        @block.vector
                        def _(v, t_lo=t_lo, t_hi=t_hi):
                            p.drain(v, "vc")
                            ins = v.reduce_max(acol[:, :], t_lo[:, :], axis=AXX,
                                               apply_absolute_value=True)
                            p.inc(ins, "vc")
                            ins = v.reduce_max(bcol[:, :], t_hi[:, :], axis=AXX,
                                               apply_absolute_value=True)
                            p.inc(ins, "vc")
                            v.wait_ge(p.sems["vc"], p.cnt["vc"])
                            ins = v.tensor_max(abmax[:, :], acol[:, :], bcol[:, :])
                            p.inc(ins, "vc")

                        @block.tensor
                        def _(t):
                            p.drain(t, "tn")
                            p.wait(t, "vc")
                            ins = t.transpose(ps_t[0:1, :], abmax[:, :], ident[:, :])
                            p.inc(ins, "tn")

                        @block.vector
                        def _(v, side=side):
                            p.wait(v, "tn")
                            if side == "q":
                                ins = v.reduce_max(m_q[:, :], ps_t[0:1, 0:128],
                                                   axis=AXX)
                                p.inc(ins, "vc")
                            else:
                                ins = v.reduce_max(m_k[:, :], ps_t[0:1, 0:128],
                                                   axis=AXX)
                                p.inc(ins, "vc")
                                v.wait_ge(p.sems["vc"], p.cnt["vc"])
                                ins = v.tensor_scalar(negM[:, :], m_q[:, :],
                                                      m_k[0:1, 0:1], -64.0,
                                                      ALU.mult, ALU.mult)
                                p.inc(ins, "vc")
                        p.barrier()

                    @block.tensor
                    def _(t):
                        p.drain(t, "tn")
                        ins = t.matmul(ps_b[:, :], lhsT=ones128[:, :],
                                       rhs=negM[:, :], start=True, stop=True)
                        p.inc(ins, "tn")

                    @block.vector
                    def _(v):
                        p.wait(v, "tn")
                        ins = v.tensor_copy(biascol[:, :], ps_b[:, :])
                        p.inc(ins, "vc")
                    p.barrier()

                    for hh in range(2):
                        h = 2 * g8 + hh
                        rb = hh * 64
                        for j in range(4):
                            kinds = [tile_kind(i, j) for i in range(16)]
                            mix_idx = [i for i in range(16) if kinds[i] == "mix"]
                            mix_no = {i: k + 1 for k, i in enumerate(mix_idx)}
                            tn_b2 = p.cnt["tn"]
                            sc_b2 = p.cnt["sc"]
                            vc_b2 = p.cnt["vc"]

                            @block.tensor
                            def _(t, h=h, rb=rb, j=j, kinds=kinds, mix_no=mix_no,
                                  sc_b2=sc_b2, vc_b2=vc_b2):
                                p.drain(t, "tn")
                                for i in range(16):
                                    bk = i % 2
                                    if i >= 2:
                                        if kinds[i - 2] == "mix":
                                            t.wait_ge(p.sems["vc"],
                                                      vc_b2 + mix_no[i - 2])
                                        else:
                                            t.wait_ge(p.sems["sc"], sc_b2 + i - 1)
                                    qv = qlo if kinds[i] != "hi" else qhi
                                    kv = klo if kinds[i] != "hi" else khi
                                    ins = t.matmul(
                                        ps_sc[bk][:, :],
                                        lhsT=kv[rb:rb + 64, i * 128:(i + 1) * 128],
                                        rhs=qv[rb:rb + 64, j * 512:(j + 1) * 512],
                                        start=True, stop=True)
                                    p.inc(ins, "tn")
                                    t.wait_ge(p.sems["sc"], sc_b2 + i + 1)
                                    ins = t.matmul(
                                        ps_o[:, :],
                                        lhsT=vext[:, i * 1040 + h * 65:
                                                  i * 1040 + (h + 1) * 65],
                                        rhs=pt[bk][:, :],
                                        start=(i == 0), stop=(i == 15))
                                    p.inc(ins, "tn")

                            @block.vector
                            def _(v, j=j, kinds=kinds, mix_no=mix_no,
                                  tn_b2=tn_b2, sc_b2=sc_b2):
                                p.drain(v, "vc")
                                for i in range(16):
                                    if kinds[i] != "mix":
                                        continue
                                    bk = i % 2
                                    v.wait_ge(p.sems["tn"], tn_b2 + 2 * i + 1)
                                    prev = [ii for ii in mix_idx if ii < i]
                                    if prev:
                                        v.wait_ge(p.sems["sc"], sc_b2 + prev[-1] + 1)
                                    o = (128 * i - 512 * j) // 128
                                    ins = v.tensor_mul(stmp[:, :], ps_sc[bk][:, :],
                                                       corr[:, o * 512:(o + 1) * 512])
                                    p.inc(ins, "vc")

                            @block.scalar
                            def _(s, h=h, j=j, kinds=kinds, mix_no=mix_no,
                                  tn_b2=tn_b2, vc_b2=vc_b2):
                                p.drain(s, "sc")
                                for i in range(16):
                                    bk = i % 2
                                    if kinds[i] == "mix":
                                        s.wait_ge(p.sems["vc"], vc_b2 + mix_no[i])
                                        src = stmp[:, :]
                                    else:
                                        s.wait_ge(p.sems["tn"], tn_b2 + 2 * i + 1)
                                        src = ps_sc[bk][:, :]
                                    ins = s.activation(pt[bk][:, :], src, AF.Exp,
                                                       bias=biascol[:, :])
                                    p.inc(ins, "sc")
                                s.wait_ge(p.sems["tn"], p.cnt["tn"])
                                s.copy(zrow[:, :], ps_o[64:65, :])
                                c = h // 2
                                rbase = (h % 2) * 64
                                ins = s.copy(
                                    attnT[rbase:rbase + 64,
                                          c * S + j * 512: c * S + (j + 1) * 512],
                                    ps_o[0:64, :])
                                p.inc(ins, "sc")

                            @block.gpsimd
                            def _(g, h=h, j=j):
                                p.drain(g, "go")
                                p.wait(g, "sc")
                                ins = g.dma_start(
                                    out=z_dram[0:1, h * S + j * 512:
                                               h * S + (j + 1) * 512],
                                    in_=zrow[:, :])
                                p.inc(ins, "dmo")
                            p.barrier()

            zall = p.sbuf("zall", [16, S], F32, sc4)

            @block.sync
            def _(sy):
                p.drain(sy, "sy")
                p.wait(sy, "dmo")
                ins = sy.dma_start(out=zall[:, :],
                                   in_=z_dram.rearrange("o (h t) -> (o h) t", h=16))
                p.inc(ins, "dmi")
    
            @block.vector
            def _(v):
                p.drain(v, "vc")
                p.wait(v, "dmi")
                ins = v.reciprocal(zall[:, :], zall[:, :])
                p.inc(ins, "vc")
            p.barrier()

            ps_zb = [p.psum(f"ps_zb{k}", [128, 512], F32, sc4) for k in range(2)]
            tn_b = p.cnt["tn"]
            vc_b = p.cnt["vc"]

            @block.tensor
            def _(t):
                p.drain(t, "tn")
                n = 0
                for c in range(ND):
                    for j in range(4):
                        bk = n % 2
                        if n >= 2:
                            t.wait_ge(p.sems["vc"], vc_b + n - 1)
                        ins = t.matmul(ps_zb[bk][:, :],
                                       lhsT=zexp[:, c * 128:(c + 1) * 128],
                                       rhs=zall[:, j * 512:(j + 1) * 512],
                                       start=True, stop=True)
                        p.inc(ins, "tn")
                        n += 1

            @block.vector
            def _(v):
                p.drain(v, "vc")
                n = 0
                for c in range(ND):
                    for j in range(4):
                        bk = n % 2
                        v.wait_ge(p.sems["tn"], tn_b + n + 1)
                        sl = attnT[:, c * S + j * 512: c * S + (j + 1) * 512]
                        ins = v.tensor_mul(sl, sl, ps_zb[bk][:, :])
                        p.inc(ins, "vc")
                        n += 1
            p.barrier()

        if debug:
            @block.gpsimd
            def _(g):
                p.drain(g, "go")
                ins = g.dma_start(out=at_dram, in_=attnT[:, :])
                p.inc(ins, "dmo")
                ins = g.dma_start(out=vx_dram, in_=vext[:, :])
                p.inc(ins, "dmo")
                ins = g.dma_start(out=lat_dram, in_=latT[:, :])
                p.inc(ins, "dmo")
            p.barrier()

        with ExitStack() as sc5:
            wob = ph_load_weight_fm(p, block, sc5, wo_d, D, D, "wo")
            p.barrier()
            xt2 = [p.sbuf(f"xt2{k}", [128, D], F32, sc5) for k in range(2)]
            gt2 = [p.sbuf(f"gt2{k}", [128, D], BF16, sc5) for k in range(2)]
            ftmp = p.sbuf("ftmp", [128, 512], F32, sc5)
            ps_f = [p.psum(f"ps_f{k}", [128, 512], F32, sc5) for k in range(2)]
            dmi_b, tn_b, vc_b, dmo_b = (p.cnt[s] for s in ("dmi", "tn", "vc", "dmo"))

            @block.sync
            def _(sy):
                p.drain(sy, "sy")
                for i in range(NT):
                    if i >= 2:
                        sy.wait_ge(p.sems["vc"], vc_b + 4 * (i - 1))
                        sy.wait_ge(p.sems["dmo"], dmo_b + 16 * (i - 1))
                    ins = sy.dma_start(out=xt2[i % 2][:, :],
                                       in_=x_dram[i * 128:(i + 1) * 128, :])
                    p.inc(ins, "dmi")
                    ins = sy.dma_start(out=gt2[i % 2][:, :],
                                       in_=g_dram[i * 128:(i + 1) * 128, :])
                    p.inc(ins, "dmi")
        
            @block.tensor
            def _(t):
                p.drain(t, "tn")
                n = 0
                for i in range(NT):
                    for dh in range(2):
                        bk = n % 2
                        if n >= 2:
                            t.wait_ge(p.sems["vc"], vc_b + 2 * (n - 1))
                        ins = None
                        for c in range(ND):
                            ins = t.matmul(
                                ps_f[bk][:, :],
                                lhsT=attnT[:, c * S + i * 128: c * S + (i + 1) * 128],
                                rhs=wob[:, c * D + dh * 512: c * D + (dh + 1) * 512],
                                start=(c == 0), stop=(c == ND - 1))
                        p.inc(ins, "tn")
                        n += 1

            @block.vector
            def _(v):
                p.drain(v, "vc")
                n = 0
                for i in range(NT):
                    v.wait_ge(p.sems["dmi"], dmi_b + 32 * (i + 1))
                    if i >= 2:
                        v.wait_ge(p.sems["dmo"], dmo_b + 16 * (i - 1))
                    for dh in range(2):
                        bk = n % 2
                        v.wait_ge(p.sems["tn"], tn_b + n + 1)
                        sl = slice(dh * 512, (dh + 1) * 512)
                        ins = v.tensor_mul(ftmp[:, :], ps_f[bk][:, :],
                                           gt2[i % 2][:, sl])
                        p.inc(ins, "vc")
                        v.wait_ge(p.sems["vc"], p.cnt["vc"])
                        ins = v.tensor_add(xt2[i % 2][:, sl], xt2[i % 2][:, sl],
                                           ftmp[:, :])
                        p.inc(ins, "vc")
                        n += 1

            @block.gpsimd
            def _(g):
                p.drain(g, "go")
                for i in range(NT):
                    g.wait_ge(p.sems["vc"], vc_b + 4 * (i + 1))
                    ins = g.dma_start(out=x1_dram[i * 128:(i + 1) * 128, :],
                                      in_=xt2[i % 2][:, :])
                    p.inc(ins, "dmo")
                g.wait_ge(p.sems["dmo"], p.cnt["dmo"])

    p.ctx.close()
    return p



_CACHE = {}


def run_l0(inputs):
    if "l0" not in _CACHE:
        _CACHE["l0"] = build_l0()
    p = _CACHE["l0"]
    x = np.ascontiguousarray(np.asarray(inputs["L0_attn_x"], np.float32))
    base = {
        "Wq": inputs["L0_attn_Wq"], "Wk": inputs["L0_attn_Wk"],
        "Wv": inputs["L0_attn_Wv"], "Wo": inputs["L0_attn_Wo"],
        "Wdn": inputs["L0_attn_W_down_kv"], "Wuk": inputs["L0_attn_W_up_k"],
        "Wuv": inputs["L0_attn_W_up_v"], "Win": inputs["L0_ffn_Win"],
        "bin": np.asarray(inputs["L0_ffn_bin_"]).reshape(64, 1),
        "W1": inputs["L0_ffn_W1"], "b1": inputs["L0_ffn_b1"],
        "W2": inputs["L0_ffn_W2"],
        "b2": np.asarray(inputs["L0_ffn_b2"]).reshape(64, 1),
        "Wout": inputs["L0_ffn_Wout"],
        "bout": np.asarray(inputs["L0_ffn_bout"]).reshape(1, D),
    }
    base = {k: np.ascontiguousarray(np.asarray(v, np.float32))
            for k, v in base.items()}
    base.update(l0_consts())
    in_maps = []
    for c in range(NCORES):
        m = dict(base)
        m["x"] = np.ascontiguousarray(x[c])
        in_maps.append(m)
    res = run_bass_kernel_spmd(p.nc, in_maps, core_ids=list(range(NCORES)))
    return np.stack([res.results[c]["x1"] for c in range(NCORES)])





L1_CLOC = 1024
BIGIDX = 1 << 20


def l1_consts():
    idx = np.linspace(0.0, S - 1, 200).astype(np.int32)
    ia = np.zeros((128, 1), np.uint32); ia[:, 0] = idx[:128]
    ib = np.zeros((72, 1), np.uint32); ib[:, 0] = idx[128:]
    iota8 = np.broadcast_to(np.arange(8, dtype=np.float32), (128, 8)).copy()
    headsel = np.zeros((128, 2), np.float32)
    headsel[0:64, 0] = 1.0; headsel[64:128, 1] = 1.0
    expand2 = np.zeros((2, 128), np.float32)
    expand2[0, 0:64] = 1.0; expand2[1, 64:128] = 1.0
    cc = np.zeros((128, 4), np.float32)
    cc[:, 0] = 1e-5; cc[:, 1] = -0.5; cc[:, 2] = 1e-12; cc[:, 3] = 1.0
    return {
        "c_idxa": ia, "c_idxb": ib, "c_iota8": iota8,
        "c_headsel": _bf16(headsel), "c_expand2": expand2,
        "c_ones1x128": np.ones((1, 128), np.float32),
        "c_ccol4": cc,
        "c_ident": np.eye(128, dtype=np.float32),
        "c_ones8": np.ones((8, 1), np.float32),
    }


def build_l1a():
    p = Prog()
    nc = p.nc
    x_dram = p.dram("x", [S, D], F32, kind="ExternalInput")
    wq_d = p.dram("Wq", [D, D], F32, kind="ExternalInput")
    bq_d = p.dram("bq", [128, ND], F32, kind="ExternalInput")
    wk_d = p.dram("Wk", [D, D], F32, kind="ExternalInput")
    bk_d = p.dram("bk", [128, ND], F32, kind="ExternalInput")
    wel_d = p.dram("WeL", [16 * 2 * 128, 20], F32, kind="ExternalInput")
    wrl_d = p.dram("WrL", [16 * 2 * 128, 20], F32, kind="ExternalInput")
    wct_d = p.dram("Wct", [20, 64], F32, kind="ExternalInput")
    wcb_d = p.dram("Wcb", [20, 64], F32, kind="ExternalInput")
    wo_d = p.dram("Wo", [D, D], F32, kind="ExternalInput")
    bcrep_d = p.dram("bcrep", [128, ND], F32, kind="ExternalInput")
    bo_d = p.dram("bo", [1, D], F32, kind="ExternalInput")
    wsw_d = p.dram("Wsw", [D, 8], F32, kind="ExternalInput")
    bsw_d = p.dram("bsw", [1, 8], F32, kind="ExternalInput")
    idxa_d = p.dram("c_idxa", [128, 1], U32, kind="ExternalInput")
    idxb_d = p.dram("c_idxb", [72, 1], U32, kind="ExternalInput")
    iota8_d = p.dram("c_iota8", [128, 8], F32, kind="ExternalInput")
    headsel_d = p.dram("c_headsel", [128, 2], BF16, kind="ExternalInput")
    expand2_d = p.dram("c_expand2", [2, 128], F32, kind="ExternalInput")
    ones_d = p.dram("c_ones1x128", [1, 128], F32, kind="ExternalInput")
    ccol_d = p.dram("c_ccol4", [128, 4], F32, kind="ExternalInput")
    identl_d = p.dram("c_ident", [128, 128], F32, kind="ExternalInput")

    s1_dram = p.dram("s1", [S, D], F32, kind="ExternalOutput")
    xn_dram = p.dram("xn1", [S, D], BF16, kind="ExternalOutput")
    aidx_dram = p.dram("aidx1", [128, NT], U32, kind="ExternalOutput")
    aidxrow_dram = p.dram("aidxrow1", [NT, 128], F32, kind="ExternalOutput")
    counts_dram = p.dram("counts1", [1, 8], F32, kind="ExternalOutput")

    ccol = p.sbuf("ccol", [128, 4], F32)
    ones128 = p.sbuf("ones128", [1, 128], F32)
    xnT = p.sbuf("xnT", [128, ND * S], BF16)
    qnT = p.sbuf("qnT", [128, ND * S], BF16)
    knT = p.sbuf("knT", [128, ND * S], BF16)
    NEb = p.sbuf("NEb", [128, ND * D], BF16)
    NRb = p.sbuf("NRb", [128, ND * D], BF16)

    with nc.Block() as block:
        @block.sync
        def _(sy):
            ins = sy.dma_start(out=ccol[:, :], in_=ccol_d)
            p.inc(ins, "dmi")
            ins = sy.dma_start(out=ones128[:, :], in_=ones_d)
            p.inc(ins, "dmi")
        p.barrier()

        with ExitStack() as sc0:
            ph_layernorm(p, block, x_dram, xn_dram, sc0, ccol[:, 0:1], ccol[:, 1:2])
            p.barrier()
        ph_load_xnT(p, block, xn_dram, xnT)
        p.barrier()

        def qk_norm(w_d, b_d, dstT, tag):
            with ExitStack() as scq:
                wb = p.sbuf("w" + tag + "_b", [128, ND * D], BF16, scq)
                with ExitStack() as scst:
                    stage = p.sbuf("wst" + tag, [128, 8192], F32, scst)
                    ph_load_weight_fm(p, block, scst, w_d, D, D, "w" + tag,
                                      stage=stage, out=wb)
                    p.barrier()
                bcols = p.sbuf("bc_" + tag, [128, ND], F32, scq)
                hsel = p.sbuf("hs" + tag, [128, 2], BF16, scq)
                exp2 = p.sbuf("ex" + tag, [2, 128], F32, scq)

                @block.sync
                def _(sy):
                    p.drain(sy, "sy")
                    ins = sy.dma_start(out=bcols[:, :], in_=b_d)
                    p.inc(ins, "dmi")
                    ins = sy.dma_start(out=hsel[:, :], in_=headsel_d)
                    p.inc(ins, "dmi")
                    ins = sy.dma_start(out=exp2[:, :], in_=expand2_d)
                    p.inc(ins, "dmi")
                p.barrier()

                sqv = p.sbuf("sq" + tag, [128, S], BF16, scq)
                nsq = p.sbuf("nq" + tag, [2, S], F32, scq)
                rn = p.sbuf("rn" + tag, [2, S], F32, scq)
                ps_q = [p.psum(f"pq{tag}{k}", [128, 512], F32, scq) for k in range(2)]
                ps_n = [p.psum(f"pn{tag}{k}", [2, 512], F32, scq) for k in range(2)]
                ps_bc = [p.psum(f"pb{tag}{k}", [128, 512], F32, scq) for k in range(2)]

                for c in range(ND):
                    tn_b, sc_b = p.cnt["tn"], p.cnt["sc"]

                    @block.tensor
                    def _(t, c=c, sc_b=sc_b):
                        p.drain(t, "tn")
                        for j in range(4):
                            bk = j % 2
                            if j >= 2:
                                t.wait_ge(p.sems["sc"], sc_b + 2 * (j - 1))
                            ins = None
                            for cc in range(ND):
                                ins = t.matmul(
                                    ps_q[bk][:, :],
                                    lhsT=wb[:, cc * D + c * 128: cc * D + (c + 1) * 128],
                                    rhs=xnT[:, cc * S + j * 512: cc * S + (j + 1) * 512],
                                    start=(cc == 0), stop=(cc == ND - 1))
                            p.inc(ins, "tn")

                    @block.scalar
                    def _(s, c=c, tn_b=tn_b):
                        p.drain(s, "sc")
                        for j in range(4):
                            bk = j % 2
                            s.wait_ge(p.sems["tn"], tn_b + j + 1)
                            sl = slice(c * S + j * 512, c * S + (j + 1) * 512)
                            ins = s.activation(dstT[:, sl], ps_q[bk][:, :],
                                               AF.Identity, bias=bcols[:, c:c + 1])
                            p.inc(ins, "sc")
                            s.wait_ge(p.sems["sc"], p.cnt["sc"])
                            ins = s.activation(sqv[:, j * 512:(j + 1) * 512],
                                               dstT[:, sl], AF.Square)
                            p.inc(ins, "sc")
                    p.barrier()

                    tn_b, vc_b = p.cnt["tn"], p.cnt["vc"]

                    @block.tensor
                    def _(t, vc_b=vc_b):
                        p.drain(t, "tn")
                        for j in range(4):
                            bk = j % 2
                            if j >= 2:
                                t.wait_ge(p.sems["vc"], vc_b + j - 1)
                            ins = t.matmul(ps_n[bk][:, :], lhsT=hsel[:, :],
                                           rhs=sqv[:, j * 512:(j + 1) * 512],
                                           start=True, stop=True)
                            p.inc(ins, "tn")

                    @block.vector
                    def _(v, tn_b=tn_b):
                        p.drain(v, "vc")
                        for j in range(4):
                            bk = j % 2
                            v.wait_ge(p.sems["tn"], tn_b + j + 1)
                            ins = v.tensor_copy(
                                nsq[:, j * 512:(j + 1) * 512], ps_n[bk][:, :])
                            p.inc(ins, "vc")
                    p.barrier()

                    @block.scalar
                    def _(s):
                        p.drain(s, "sc")
                        ins = s.activation(nsq[:, :], nsq[:, :], AF.Sqrt)
                        p.inc(ins, "sc")

                    @block.vector
                    def _(v):
                        p.drain(v, "vc")
                        p.wait(v, "sc")
                        ins = v.tensor_scalar_max(nsq[:, :], nsq[:, :],
                                                  ccol[0:2, 2:3])
                        p.inc(ins, "vc")
                        v.wait_ge(p.sems["vc"], p.cnt["vc"])
                        ins = v.reciprocal(rn[:, :], nsq[:, :])
                        p.inc(ins, "vc")
                    p.barrier()

                    tn_b, vc_b = p.cnt["tn"], p.cnt["vc"]

                    @block.tensor
                    def _(t, vc_b=vc_b):
                        p.drain(t, "tn")
                        for j in range(4):
                            bk = j % 2
                            if j >= 2:
                                t.wait_ge(p.sems["vc"], vc_b + j - 1)
                            ins = t.matmul(
                                ps_bc[bk][:, :], lhsT=exp2[:, :],
                                rhs=rn[:, j * 512:(j + 1) * 512],
                                start=True, stop=True)
                            p.inc(ins, "tn")

                    @block.vector
                    def _(v, c=c, tn_b=tn_b):
                        p.drain(v, "vc")
                        for j in range(4):
                            bk = j % 2
                            v.wait_ge(p.sems["tn"], tn_b + j + 1)
                            sl = dstT[:, c * S + j * 512: c * S + (j + 1) * 512]
                            ins = v.tensor_mul(sl, sl, ps_bc[bk][:, :])
                            p.inc(ins, "vc")
                    p.barrier()

        qk_norm(wq_d, bq_d, qnT, "q")
        qk_norm(wk_d, bk_d, knT, "k")

        with ExitStack() as scn:
            xg = [p.sbuf(f"xg{k}", [128, D], BF16, scn) for k in range(2)]
            wel = p.sbuf("wel_b", [128, 32 * 20], BF16, scn)
            wrl = p.sbuf("wrl_b", [128, 32 * 20], BF16, scn)
            wct = p.sbuf("wct_b", [20, 64], BF16, scn)
            wcb = p.sbuf("wcb_b", [20, 64], BF16, scn)
            wob = p.sbuf("wo_b", [128, ND * D], BF16, scn)
            with ExitStack() as scst:
                stage = p.sbuf("wstage1", [128, 8192], F32, scst)
                for dram_, K_, M_, nm_, out_ in (
                        (wel_d, 4096, 20, "wel", wel), (wrl_d, 4096, 20, "wrl", wrl),
                        (wct_d, 20, 64, "wct", wct), (wcb_d, 20, 64, "wcb", wcb),
                        (wo_d, D, D, "wo", wob)):
                    ph_load_weight_fm(p, block, scst, dram_, K_, M_, nm_,
                                      stage=stage, out=out_)
                    p.barrier()
            idxa = p.sbuf("idxa", [128, 1], U32, scn)
            idxb = p.sbuf("idxb", [72, 1], U32, scn)

            @block.sync
            def _(sy):
                p.drain(sy, "sy")
                ins = sy.dma_start(out=idxa[:, :], in_=idxa_d)
                p.inc(ins, "dmi")
                ins = sy.dma_start(out=idxb[:, :], in_=idxb_d)
                p.inc(ins, "dmi")
            p.barrier()

            @block.gpsimd
            def _(g):
                p.drain(g, "go")
                ins = g.indirect_dma_start(
                    out=xg[0][:, :], out_offset=None, in_=xn_dram,
                    in_offset=bass.IndirectOffsetOnAxis(idxa[:, :], 0))
                p.inc(ins, "dmg")
                ins = g.indirect_dma_start(
                    out=xg[1][0:72, :], out_offset=None, in_=xn_dram,
                    in_offset=bass.IndirectOffsetOnAxis(idxb[:, :], 0))
                p.inc(ins, "dmg")
            p.barrier()

            weT = p.sbuf("weT", [20, 64], BF16, scn)
            MTb = p.sbuf("MTb", [64, 64], BF16, scn)
            wo_h = p.sbuf("wo_h", [64, D], BF16, scn)
            ps_we = p.psum("ps_we", [20, 64], F32, scn)
            ps_mt = p.psum("ps_mt", [64, 64], F32, scn)
            ps_ne = p.psum("ps_ne", [64, 512], F32, scn)

            for h in range(16):
                for wl, wc, dstN in ((wel, wct, NEb), (wrl, wcb, NRb)):
                    @block.tensor
                    def _(t, h=h, wl=wl):
                        p.drain(t, "tn")
                        ins = t.matmul(ps_we[:, :],
                                       lhsT=wl[:, (2 * h) * 20:(2 * h + 1) * 20],
                                       rhs=xg[0][:, h * 64:(h + 1) * 64],
                                       start=True, stop=False)
                        ins = t.matmul(ps_we[:, :],
                                       lhsT=wl[:, (2 * h + 1) * 20:(2 * h + 2) * 20],
                                       rhs=xg[1][:, h * 64:(h + 1) * 64],
                                       start=False, stop=True)
                        p.inc(ins, "tn")

                    @block.vector
                    def _(v):
                        p.drain(v, "vc")
                        p.wait(v, "tn")
                        ins = v.tensor_copy(weT[:, :], ps_we[:, :])
                        p.inc(ins, "vc")

                    @block.tensor
                    def _(t, wc=wc):
                        p.drain(t, "tn")
                        p.wait(t, "vc")
                        ins = t.matmul(ps_mt[:, :], lhsT=wc[0:20, 0:64],
                                       rhs=weT[:, :], start=True, stop=True)
                        p.inc(ins, "tn")

                    @block.vector
                    def _(v):
                        p.drain(v, "vc")
                        p.wait(v, "tn")
                        ins = v.tensor_copy(MTb[:, :], ps_mt[:, :])
                        p.inc(ins, "vc")

                    @block.vector
                    def _(v, h=h):
                        p.drain(v, "vc")
                        rb = (h % 2) * 64
                        ins = v.tensor_copy(
                            wo_h[:, :],
                            wob[rb:rb + 64, (h // 2) * D:(h // 2 + 1) * D])
                        p.inc(ins, "vc")

                    @block.tensor
                    def _(t, h=h):
                        p.drain(t, "tn")
                        for dh in range(2):
                            p.wait(t, "vc")
                            ins = t.matmul(
                                ps_ne[:, :], lhsT=MTb[:, :],
                                rhs=wo_h[0:64, dh * 512:(dh + 1) * 512],
                                start=True, stop=True)
                            p.inc(ins, "tn")

                            @block.vector
                            def _(v, h=h, dh=dh, dstN=dstN):
                                p.drain(v, "vc")
                                p.wait(v, "tn")
                                rb = (h % 2) * 64
                                ins = v.tensor_copy(
                                    dstN[rb:rb + 64,
                                         (h // 2) * D + dh * 512:
                                         (h // 2) * D + (dh + 1) * 512],
                                    ps_ne[:, :])
                                p.inc(ins, "vc")
                    p.barrier()

            srow = p.sbuf("srow", [1, D], F32, scn)
            srow_bc = p.sbuf("srow_bc", [128, D], F32, scn)
            bcrepc = p.sbuf("bcrepc", [128, ND], F32, scn)
            bcrepb = p.sbuf("bcrepb", [128, ND], BF16, scn)
            borow = p.sbuf("borow", [1, D], F32, scn)
            ps_r = p.psum("ps_r", [1, 512], F32, scn)
            ps_rb = p.psum("ps_rb", [128, 512], F32, scn)

            @block.sync
            def _(sy):
                p.drain(sy, "sy")
                ins = sy.dma_start(out=bcrepc[:, :], in_=bcrep_d)
                p.inc(ins, "dmi")
                ins = sy.dma_start(out=borow[:, :], in_=bo_d)
                p.inc(ins, "dmi")
            p.barrier()

            @block.vector
            def _(v):
                p.drain(v, "vc")
                ins = v.tensor_copy(bcrepb[:, :], bcrepc[:, :])
                p.inc(ins, "vc")
            p.barrier()

            for dh in range(2):
                @block.tensor
                def _(t, dh=dh):
                    p.drain(t, "tn")
                    if dh:
                        p.wait(t, "vc")
                    ins = None
                    for c in range(ND):
                        ins = t.matmul(ps_r[:, :], lhsT=bcrepb[:, c:c + 1],
                                       rhs=wob[:, c * D + dh * 512:
                                               c * D + (dh + 1) * 512],
                                       start=(c == 0), stop=(c == ND - 1))
                    p.inc(ins, "tn")

                @block.vector
                def _(v, dh=dh):
                    p.drain(v, "vc")
                    p.wait(v, "tn")
                    sl = slice(dh * 512, (dh + 1) * 512)
                    ins = v.tensor_add(srow[:, sl], ps_r[:, :], borow[:, sl])
                    p.inc(ins, "vc")
            p.barrier()
            for dh in range(2):
                @block.tensor
                def _(t, dh=dh):
                    p.drain(t, "tn")
                    if dh:
                        p.wait(t, "vc")
                    ins = t.matmul(ps_rb[:, :], lhsT=ones128[:, :],
                                   rhs=srow[:, dh * 512:(dh + 1) * 512],
                                   start=True, stop=True)
                    p.inc(ins, "tn")

                @block.vector
                def _(v, dh=dh):
                    p.drain(v, "vc")
                    p.wait(v, "tn")
                    ins = v.tensor_copy(srow_bc[:, dh * 512:(dh + 1) * 512],
                                        ps_rb[:, :])
                    p.inc(ins, "vc")
            p.barrier()

            st = [p.sbuf(f"st{k}", [128, D], F32, scn) for k in range(2)]
            ps_f = [p.psum(f"ps_f{k}", [128, 512], F32, scn) for k in range(2)]
            tn_b, vc_b, dmo_b = p.cnt["tn"], p.cnt["vc"], p.cnt["dmo"]

            @block.tensor
            def _(t):
                p.drain(t, "tn")
                n = 0
                for i in range(NT):
                    for dh in range(2):
                        bk = n % 2
                        if n >= 2:
                            t.wait_ge(p.sems["vc"], vc_b + n - 1)
                        ins = None
                        for c in range(ND):
                            ins = t.matmul(
                                ps_f[bk][:, :],
                                lhsT=qnT[:, c * S + i * 128: c * S + (i + 1) * 128],
                                rhs=NEb[:, c * D + dh * 512: c * D + (dh + 1) * 512],
                                start=(c == 0), stop=False)
                        for c in range(ND):
                            ins = t.matmul(
                                ps_f[bk][:, :],
                                lhsT=knT[:, c * S + i * 128: c * S + (i + 1) * 128],
                                rhs=NRb[:, c * D + dh * 512: c * D + (dh + 1) * 512],
                                start=False, stop=(c == ND - 1))
                        p.inc(ins, "tn")
                        n += 1

            @block.vector
            def _(v):
                p.drain(v, "vc")
                n = 0
                for i in range(NT):
                    if i >= 2:
                        v.wait_ge(p.sems["dmo"], dmo_b + 16 * (i - 1))
                    for dh in range(2):
                        bk = n % 2
                        v.wait_ge(p.sems["tn"], tn_b + n + 1)
                        ins = v.tensor_add(st[i % 2][:, dh * 512:(dh + 1) * 512],
                                           ps_f[bk][:, :],
                                           srow_bc[:, dh * 512:(dh + 1) * 512])
                        p.inc(ins, "vc")
                        n += 1

            @block.gpsimd
            def _(g):
                p.drain(g, "go")
                for i in range(NT):
                    g.wait_ge(p.sems["vc"], vc_b + 2 * (i + 1))
                    ins = g.dma_start(out=s1_dram[i * 128:(i + 1) * 128, :],
                                      in_=st[i % 2][:, :])
                    p.inc(ins, "dmo")
            p.barrier()

        with ExitStack() as scr:
            wswb = ph_load_weight_fm(p, block, scr, wsw_d, D, 8, "wsw")
            iota8 = p.sbuf("iota8", [128, 8], F32, scr)
            bswrow = p.sbuf("bswrow", [1, 8], F32, scr)
            bsw_bc = p.sbuf("bsw_bc", [128, 8], F32, scr)
            logits = p.sbuf("logits", [128, 8], F32, scr)
            aidxf_tm = p.sbuf("aidxf_tm", [128, NT], F32, scr)
            rowbuf = p.sbuf("rowbuf", [NT, 128], F32, scr)
            identr = p.sbuf("identr", [128, 128], F32, scr)
            ps_tr = p.psum("ps_tr", [NT, 128], F32, scr)
            amax8 = p.sbuf("amax8", [128, 8], F32, scr)
            aidx8 = p.sbuf("aidx8", [128, 8], U32, scr)
            aidxf = p.sbuf("aidxf", [128, 1], F32, scr)
            onehot = p.sbuf("onehot", [128, 8], F32, scr)
            aidx_tm = p.sbuf("aidx_tm", [128, NT], U32, scr)
            cnt_sb = p.sbuf("cnt_sb", [1, 8], F32, scr)
            ps_l = [p.psum(f"ps_l{k}", [128, 8], F32, scr) for k in range(2)]
            ps_c = p.psum("ps_c", [1, 8], F32, scr)
            ps_bb = p.psum("ps_bb", [128, 8], F32, scr)

            @block.sync
            def _(sy):
                p.drain(sy, "sy")
                ins = sy.dma_start(out=iota8[:, :], in_=iota8_d)
                p.inc(ins, "dmi")
                ins = sy.dma_start(out=bswrow[:, :], in_=bsw_d)
                p.inc(ins, "dmi")
                ins = sy.dma_start(out=identr[:, :], in_=identl_d)
                p.inc(ins, "dmi")
            p.barrier()

            @block.tensor
            def _(t):
                p.drain(t, "tn")
                ins = t.matmul(ps_bb[:, :], lhsT=ones128[:, :], rhs=bswrow[:, :],
                               start=True, stop=True)
                p.inc(ins, "tn")

            @block.vector
            def _(v):
                p.drain(v, "vc")
                p.wait(v, "tn")
                ins = v.tensor_copy(bsw_bc[:, :], ps_bb[:, :])
                p.inc(ins, "vc")
            p.barrier()

            tn_b = p.cnt["tn"]
            for i in range(NT):
                @block.tensor
                def _(t, i=i):
                    p.drain(t, "tn") if i == 0 else None
                    if i >= 2:
                        t.wait_ge(p.sems["vc"], p.cnt["vc"] - 0)
                    ins = None
                    for c in range(ND):
                        ins = t.matmul(
                            ps_l[i % 2][:, :],
                            lhsT=xnT[:, c * S + i * 128: c * S + (i + 1) * 128],
                            rhs=wswb[:, c * 8:(c + 1) * 8],
                            start=(c == 0), stop=(c == ND - 1))
                    p.inc(ins, "tn")

                @block.vector
                def _(v, i=i):
                    p.drain(v, "vc") if i == 0 else None
                    p.wait(v, "tn")
                    ins = v.tensor_add(logits[:, :], ps_l[i % 2][:, :], bsw_bc[:, :])
                    p.inc(ins, "vc")
                    v.wait_ge(p.sems["vc"], p.cnt["vc"])
                    ins = v.max(amax8[:, :], logits[:, :])
                    p.inc(ins, "vc")
                    v.wait_ge(p.sems["vc"], p.cnt["vc"])
                    ins = v.max_index(aidx8[:, :], amax8[:, :], logits[:, :])
                    p.inc(ins, "vc")
                    v.wait_ge(p.sems["vc"], p.cnt["vc"])
                    ins = v.tensor_copy(aidx_tm[:, i:i + 1], aidx8[:, 0:1])
                    p.inc(ins, "vc")
                    ins = v.tensor_copy(aidxf[:, :], aidx8[:, 0:1])
                    p.inc(ins, "vc")
                    ins = v.tensor_copy(aidxf_tm[:, i:i + 1], aidx8[:, 0:1])
                    p.inc(ins, "vc")
                    v.wait_ge(p.sems["vc"], p.cnt["vc"])
                    ins = v.tensor_scalar(onehot[:, :], iota8[:, :], aidxf[:, :],
                                          None, ALU.is_equal)
                    p.inc(ins, "vc")

                @block.tensor
                def _(t, i=i):
                    p.wait(t, "vc")
                    ins = t.matmul(ps_c[:, :], lhsT=ccol[:, 3:4], rhs=onehot[:, :],
                                   start=(i == 0), stop=(i == NT - 1))
                    p.inc(ins, "tn")
                p.barrier()

            @block.vector
            def _(v):
                p.drain(v, "vc")
                ins = v.tensor_copy(cnt_sb[:, :], ps_c[:, :])
                p.inc(ins, "vc")

            @block.tensor
            def _(t):
                p.drain(t, "tn")
                ins = t.transpose(ps_tr[:, :], aidxf_tm[:, :], identr[:, :])
                p.inc(ins, "tn")

            @block.vector
            def _(v):
                p.drain(v, "vc")
                p.wait(v, "tn")
                ins = v.tensor_copy(rowbuf[:, :], ps_tr[:, :])
                p.inc(ins, "vc")

            @block.gpsimd
            def _(g):
                p.drain(g, "go")
                p.wait(g, "vc")
                ins = g.dma_start(out=counts_dram, in_=cnt_sb[:, :])
                p.inc(ins, "dmo")
                ins = g.dma_start(out=aidx_dram, in_=aidx_tm[:, :])
                p.inc(ins, "dmo")
                ins = g.dma_start(out=aidxrow_dram, in_=rowbuf[:, :])
                p.inc(ins, "dmo")
                g.wait_ge(p.sems["dmo"], p.cnt["dmo"])
    p.ctx.close()
    return p




def run_l1a(inputs, x1):
    if "l1a" not in _CACHE:
        _CACHE["l1a"] = build_l1a()
    p = _CACHE["l1a"]
    We = np.asarray(inputs["L1_attn_We"], np.float32)
    Wr = np.asarray(inputs["L1_attn_Wr"], np.float32)
    Wep = np.zeros((16, 256, 20), np.float32); Wep[:, :200] = We
    Wrp = np.zeros((16, 256, 20), np.float32); Wrp[:, :200] = Wr
    Wc = np.asarray(inputs["L1_attn_Wc"], np.float32)
    bc = np.asarray(inputs["L1_attn_bc"], np.float32)
    base = {
        "Wq": inputs["L1_attn_Wq"],
        "bq": np.asarray(inputs["L1_attn_bq"]).reshape(ND, 128).T,
        "Wk": inputs["L1_attn_Wk"],
        "bk": np.asarray(inputs["L1_attn_bk"]).reshape(ND, 128).T,
        "WeL": Wep.reshape(16 * 256, 20), "WrL": Wrp.reshape(16 * 256, 20),
        "Wct": Wc[:20], "Wcb": Wc[20:],
        "Wo": inputs["L1_attn_Wo"],
        "bcrep": np.tile(bc, 16).reshape(ND, 128).T,
        "bo": np.asarray(inputs["L1_attn_bo"]).reshape(1, D),
        "Wsw": inputs["L1_ffn_W_sw"],
        "bsw": np.asarray(inputs["L1_ffn_b_sw"]).reshape(1, 8),
    }
    base = {k: np.ascontiguousarray(np.asarray(v, np.float32))
            for k, v in base.items()}
    base.update(l1_consts())
    in_maps = []
    for c in range(NCORES):
        m = dict(base)
        m["x"] = np.ascontiguousarray(np.asarray(x1[c], np.float32))
        in_maps.append(m)
    res = run_bass_kernel_spmd(p.nc, in_maps, core_ids=list(range(NCORES)))
    return res.results




def ph_load_weight_cast(p, block, w_dram, K, M, out, name, col_block=None,
                        row0=0):
    nch = K // 128
    cb = col_block or M
    @block.gpsimd
    def _(g):
        p.drain(g, "go")
        for b0 in range(0, M, cb):
            src = w_dram[row0:row0 + K, b0:b0 + cb]
            ins = g.dma_start(
                out=out[:, 0:nch * M].rearrange("p (c m) -> p c m", c=nch)
                    [:, :, b0:b0 + cb],
                in_=src.rearrange("(c p) m -> p c m", p=128))
            p.inc(ins, "dmg")



def l1b_consts(core_id):
    pm = np.zeros((8, 1), np.float32)
    pm[:core_id, 0] = 1.0
    t = np.arange(S, dtype=np.float32)
    return {
        "c_prevmask": pm,
        "c_iota8col": np.arange(8, dtype=np.float32).reshape(8, 1),
        "c_ones8col": np.ones((8, 1), np.float32),
        "c_ones1x8": np.ones((1, 8), np.float32),
        "c_bigiota": (t + float(BIGIDX)).reshape(1, S),
        "c_ident": np.eye(128, dtype=np.float32),
    }




def ph_moe_experts(p, block, moe_in, moe_out, w1_d, b1_d, w2_d, b2_d, ones128,
                   cloc, gain_col=None, gbias_col=None):
    with ExitStack() as sce:
        w1eb = p.sbuf("w1eb", [128, ND * 4096], BF16, sce)
        w2eb = p.sbuf("w2eb", [128, 32 * D], BF16, sce)
        b1c = p.sbuf("b1c", [128, 32], F32, sce)
        b2row = p.sbuf("b2row", [1, D], F32, sce)
        b2bc = p.sbuf("b2bc", [128, D], F32, sce)
        xsg = p.sbuf("xsg", [128, ND * 512], BF16, sce)
        hsg = p.sbuf("hsg", [128, 32 * 512], BF16, sce)
        otile = [p.sbuf(f"otile{k}", [128, D], BF16, sce) for k in range(2)]
        ps_h = [p.psum(f"ps_h{k}", [128, 512], F32, sce) for k in range(2)]
        ps_o = [p.psum(f"ps_o{k}", [128, 512], F32, sce) for k in range(2)]
        ps_b2 = p.psum("ps_b2", [128, 512], F32, sce)

        for e in range(8):
            ph_load_weight_cast(p, block, w1_d, D, 4096, w1eb, f"w1e{e}",
                                col_block=1024, row0=e * D)
            ph_load_weight_cast(p, block, w2_d, 4096, D, w2eb, f"w2e{e}",
                                col_block=256, row0=e * 4096)

            @block.sync
            def _(sy, e=e):
                p.drain(sy, "sy")
                ins = sy.dma_start(out=b1c[:, :],
                                   in_=b1_d[e * 128:(e + 1) * 128, :])
                p.inc(ins, "dmi")
                ins = sy.dma_start(out=b2row[:, :], in_=b2_d[e:e + 1, :])
                p.inc(ins, "dmi")
            p.barrier()

            for dh in range(2):
                @block.tensor
                def _(t, dh=dh):
                    p.drain(t, "tn")
                    if dh:
                        p.wait(t, "vc")
                    ins = t.matmul(ps_b2[:, :], lhsT=ones128[:, :],
                                   rhs=b2row[:, dh * 512:(dh + 1) * 512],
                                   start=True, stop=True)
                    p.inc(ins, "tn")

                @block.vector
                def _(v, dh=dh):
                    p.drain(v, "vc")
                    p.wait(v, "tn")
                    ins = v.tensor_copy(b2bc[:, dh * 512:(dh + 1) * 512],
                                        ps_b2[:, :])
                    p.inc(ins, "vc")
            p.barrier()

            for sg in range(cloc // 512):
                @block.sync
                def _(sy, e=e, sg=sg):
                    p.drain(sy, "sy")
                    for c in range(ND):
                        ins = sy.dma_start_transpose(
                            xsg[:, c * 512:(c + 1) * 512],
                            moe_in[e * cloc + sg * 512:
                                   e * cloc + (sg + 1) * 512,
                                   c * 128:(c + 1) * 128])
                        p.inc(ins, "dmi")
                p.barrier()
                if gain_col is not None:
                    @block.vector
                    def _(v):
                        p.drain(v, "vc")
                        for c in range(ND):
                            ins = v.tensor_scalar(
                                xsg[:, c * 512:(c + 1) * 512],
                                xsg[:, c * 512:(c + 1) * 512],
                                gain_col[:, c:c + 1], gbias_col[:, c:c + 1],
                                ALU.mult, ALU.add)
                            p.inc(ins, "vc")
                    p.barrier()

                tn_b, sc_b = p.cnt["tn"], p.cnt["sc"]

                @block.tensor
                def _(t, sc_b=sc_b):
                    p.drain(t, "tn")
                    for fc in range(32):
                        bk = fc % 2
                        if fc >= 2:
                            t.wait_ge(p.sems["sc"], sc_b + fc - 1)
                        ins = None
                        for c in range(ND):
                            ins = t.matmul(
                                ps_h[bk][:, :],
                                lhsT=w1eb[:, c * 4096 + fc * 128:
                                          c * 4096 + (fc + 1) * 128],
                                rhs=xsg[:, c * 512:(c + 1) * 512],
                                start=(c == 0), stop=(c == ND - 1))
                        p.inc(ins, "tn")

                @block.scalar
                def _(s, tn_b=tn_b):
                    p.drain(s, "sc")
                    for fc in range(32):
                        bk = fc % 2
                        s.wait_ge(p.sems["tn"], tn_b + fc + 1)
                        ins = s.activation(hsg[:, fc * 512:(fc + 1) * 512],
                                           ps_h[bk][:, :], AF.Gelu,
                                           bias=b1c[:, fc:fc + 1])
                        p.inc(ins, "sc")
                p.barrier()

                tn_b, vc_b, dmo_b = p.cnt["tn"], p.cnt["vc"], p.cnt["dmo"]

                @block.tensor
                def _(t, vc_b=vc_b):
                    p.drain(t, "tn")
                    n = 0
                    for sc4 in range(4):
                        for dh in range(2):
                            bk = n % 2
                            if n >= 2:
                                t.wait_ge(p.sems["vc"], vc_b + n - 1)
                            ins = None
                            for fc in range(32):
                                ins = t.matmul(
                                    ps_o[bk][:, :],
                                    lhsT=hsg[:, fc * 512 + sc4 * 128:
                                             fc * 512 + (sc4 + 1) * 128],
                                    rhs=w2eb[:, fc * D + dh * 512:
                                             fc * D + (dh + 1) * 512],
                                    start=(fc == 0), stop=(fc == 31))
                                p.inc(ins, "tn") if fc == 31 else None
                            n += 1

                @block.vector
                def _(v, tn_b=tn_b, dmo_b=dmo_b):
                    p.drain(v, "vc")
                    n = 0
                    for sc4 in range(4):
                        if sc4 >= 2:
                            v.wait_ge(p.sems["dmo"], dmo_b + 16 * (sc4 - 1))
                        for dh in range(2):
                            bk = n % 2
                            v.wait_ge(p.sems["tn"], tn_b + n + 1)
                            ins = v.tensor_add(
                                otile[sc4 % 2][:, dh * 512:(dh + 1) * 512],
                                ps_o[bk][:, :],
                                b2bc[:, dh * 512:(dh + 1) * 512])
                            p.inc(ins, "vc")
                            n += 1

                @block.gpsimd
                def _(g, e=e, sg=sg, vc_b=vc_b):
                    p.drain(g, "go")
                    for sc4 in range(4):
                        g.wait_ge(p.sems["vc"], vc_b + 2 * (sc4 + 1))
                        r0 = e * cloc + sg * 512 + sc4 * 128
                        ins = g.dma_start(out=moe_out[r0:r0 + 128, :],
                                          in_=otile[sc4 % 2][:, :])
                        p.inc(ins, "dmo")
                p.barrier()


def build_l1b(upto=4, skip_gather=False, debug=False):
    p = Prog()
    nc = p.nc
    x_dram = p.dram("x", [S, D], F32, kind="ExternalInput")
    s1_dram = p.dram("s1", [S, D], F32, kind="ExternalInput")
    xn_dram = p.dram("xn1", [S, D], BF16, kind="ExternalInput")
    aidxrow_d = p.dram("aidxrow1", [NT, 128], F32, kind="ExternalInput")
    counts_d = p.dram("countsall", [8, 8], F32, kind="ExternalInput")
    w1_d = p.dram("W1L", [8 * D, 4096], F32, kind="ExternalInput")
    b1_d = p.dram("b1L", [8 * 128, 32], F32, kind="ExternalInput")
    w2_d = p.dram("W2L", [8 * 4096, D], F32, kind="ExternalInput")
    b2_d = p.dram("b2L", [8, D], F32, kind="ExternalInput")
    pm_d = p.dram("c_prevmask", [8, 1], F32, kind="ExternalInput")
    i8c_d = p.dram("c_iota8col", [8, 1], F32, kind="ExternalInput")
    o8c_d = p.dram("c_ones8col", [8, 1], F32, kind="ExternalInput")
    o18_d = p.dram("c_ones1x8", [1, 8], F32, kind="ExternalInput")
    bigi_d = p.dram("c_bigiota", [1, S], F32, kind="ExternalInput")
    ident_d = p.dram("c_ident", [128, 128], F32, kind="ExternalInput")
    ones_d = p.dram("c_ones1x128", [1, 128], F32, kind="ExternalInput")

    x2_dram = p.dram("x2", [S, D], F32, kind="ExternalOutput")
    dk1 = "ExternalOutput" if debug else "Internal"
    moe_in = p.dram("moe_in", [8 * L1_CLOC, D], BF16, kind=dk1)
    moe_out = p.dram("moe_out", [8 * L1_CLOC, D], BF16, kind=dk1)

    sidx_tm = p.sbuf("sidx_tm", [128, NT], U32)
    gidx_tm = p.sbuf("gidx_tm", [128, NT], U32)
    ones128 = p.sbuf("ones128", [1, 128], F32)
    ident = p.sbuf("identb", [128, 128], F32)

    with nc.Block() as block:
        with ExitStack() as scr:
            arow = p.sbuf("arow", [1, S], F32, scr)
            cnts = p.sbuf("cnts", [8, 8], F32, scr)
            pmcol = p.sbuf("pmcol", [8, 1], F32, scr)
            i8col = p.sbuf("i8col", [8, 1], F32, scr)
            o8col = p.sbuf("o8col", [8, 1], F32, scr)
            o18 = p.sbuf("o18", [1, 8], F32, scr)
            bigi = p.sbuf("bigi", [1, S], F32, scr)
            abc = p.sbuf("abc", [8, S], F32, scr)
            onehotT = p.sbuf("onehotT", [8, S], F32, scr)
            ones8t = p.sbuf("ones8t", [8, S], F32, scr)
            pos = p.sbuf("pos", [8, S], F32, scr)
            mp = p.sbuf("mp", [8, S], F32, scr)
            offs = p.sbuf("offs", [8, 1], F32, scr)
            pos_row = p.sbuf("pos_row", [1, S], F32, scr)
            offs_row = p.sbuf("offs_row", [1, S], F32, scr)
            keep = p.sbuf("keep", [1, S], F32, scr)
            keep2 = p.sbuf("keep2", [1, S], F32, scr)
            notk = p.sbuf("notk", [1, S], F32, scr)
            slot = p.sbuf("slot", [1, S], F32, scr)
            srow2 = p.sbuf("srow2", [1, S], F32, scr)
            tcol = p.sbuf("tcol", [128, 2], F32, scr)
            ps_a = [p.psum(f"ps_a{k}", [8, 512], F32, scr) for k in range(2)]
            ps_p = [p.psum(f"ps_p{k}", [1, 512], F32, scr) for k in range(2)]
            ps_o8 = p.psum("ps_o8", [8, 1], F32, scr)
            ps_t2 = p.psum("ps_t2", [128, 1], F32, scr)

            @block.sync
            def _(sy):
                p.drain(sy, "sy")
                ins = sy.dma_start(out=arow[:, :],
                                   in_=aidxrow_d.rearrange("i p -> (i p)"))
                p.inc(ins, "dmi")
                for dst, src in ((cnts, counts_d), (pmcol, pm_d), (i8col, i8c_d),
                                 (o8col, o8c_d), (o18, o18_d), (bigi, bigi_d),
                                 (ones128, ones_d), (ident, ident_d)):
                    ins = sy.dma_start(out=dst[:, :], in_=src)
                    p.inc(ins, "dmi")
            p.barrier()

            tn_b = p.cnt["tn"]

            vc_b0 = p.cnt["vc"]

            @block.tensor
            def _(t, vc_b0=vc_b0):
                p.drain(t, "tn")
                for j in range(4):
                    if j >= 2:
                        t.wait_ge(p.sems["vc"], vc_b0 + j - 1)
                    ins = t.matmul(ps_a[j % 2][:, :], lhsT=o18[:, :],
                                   rhs=arow[:, j * 512:(j + 1) * 512],
                                   start=True, stop=True)
                    p.inc(ins, "tn")

            @block.vector
            def _(v, tn_b=tn_b):
                p.drain(v, "vc")
                for j in range(4):
                    v.wait_ge(p.sems["tn"], tn_b + j + 1)
                    ins = v.tensor_copy(abc[:, j * 512:(j + 1) * 512],
                                        ps_a[j % 2][:, :])
                    p.inc(ins, "vc")
                v.wait_ge(p.sems["vc"], p.cnt["vc"])
                ins = v.tensor_scalar(onehotT[:, :], abc[:, :], i8col[:, :], None,
                                      ALU.is_equal)
                p.inc(ins, "vc")
                ins = v.memset(ones8t[:, :], 1.0)
                p.inc(ins, "vc")
                v.wait_ge(p.sems["vc"], p.cnt["vc"])
                ins = v.tensor_tensor_scan(pos[:, :], ones8t[:, :], onehotT[:, :],
                                           0.0, ALU.mult, ALU.add)
                p.inc(ins, "vc")
                v.wait_ge(p.sems["vc"], p.cnt["vc"])
                ins = v.tensor_mul(mp[:, :], onehotT[:, :], pos[:, :])
                p.inc(ins, "vc")
            p.barrier()

            @block.tensor
            def _(t):
                p.drain(t, "tn")
                ins = t.matmul(ps_o8[:, :], lhsT=cnts[:, :], rhs=pmcol[:, :],
                               start=True, stop=True)
                p.inc(ins, "tn")

            @block.vector
            def _(v):
                p.drain(v, "vc")
                p.wait(v, "tn")
                ins = v.tensor_copy(offs[:, :], ps_o8[:, :])
                p.inc(ins, "vc")
                v.wait_ge(p.sems["vc"], p.cnt["vc"])
                ins = v.tensor_scalar_mul(abc[:, :], onehotT[:, :], offs[:, :])
                p.inc(ins, "vc")
            p.barrier()

            tn_b = p.cnt["tn"]

            vc_b1 = p.cnt["vc"]

            @block.tensor
            def _(t, vc_b1=vc_b1):
                p.drain(t, "tn")
                for n in range(8):
                    if n >= 2:
                        t.wait_ge(p.sems["vc"], vc_b1 + n - 1)
                    src = mp if n < 4 else abc
                    j = n % 4
                    ins = t.matmul(ps_p[n % 2][:, :], lhsT=o8col[:, :],
                                   rhs=src[:, j * 512:(j + 1) * 512],
                                   start=True, stop=True)
                    p.inc(ins, "tn")

            @block.vector
            def _(v, tn_b=tn_b):
                p.drain(v, "vc")
                for n in range(8):
                    v.wait_ge(p.sems["tn"], tn_b + n + 1)
                    dst = pos_row if n < 4 else offs_row
                    j = n % 4
                    ins = v.tensor_copy(dst[:, j * 512:(j + 1) * 512],
                                        ps_p[n % 2][:, :])
                    p.inc(ins, "vc")
            p.barrier()

            @block.vector
            def _(v):
                p.drain(v, "vc")
                ins = v.tensor_scalar(pos_row[:, :], pos_row[:, :], -1.0, None,
                                      ALU.add)
                p.inc(ins, "vc")
                v.wait_ge(p.sems["vc"], p.cnt["vc"])
                ins = v.tensor_add(keep2[:, :], pos_row[:, :], offs_row[:, :])
                p.inc(ins, "vc")
                v.wait_ge(p.sems["vc"], p.cnt["vc"])
                ins = v.tensor_scalar(keep2[:, :], keep2[:, :], 2048.0, None,
                                      ALU.is_lt)
                p.inc(ins, "vc")
                ins = v.tensor_scalar(keep[:, :], pos_row[:, :], float(L1_CLOC),
                                      None, ALU.is_lt)
                p.inc(ins, "vc")
                v.wait_ge(p.sems["vc"], p.cnt["vc"])
                ins = v.tensor_mul(keep[:, :], keep[:, :], keep2[:, :])
                p.inc(ins, "vc")
                ins = v.tensor_scalar_mul(slot[:, :], arow[:, :], float(L1_CLOC))
                p.inc(ins, "vc")
                v.wait_ge(p.sems["vc"], p.cnt["vc"])
                ins = v.tensor_add(slot[:, :], slot[:, :], pos_row[:, :])
                p.inc(ins, "vc")
                ins = v.tensor_scalar(notk[:, :], keep[:, :], -1.0, 1.0,
                                      ALU.mult, ALU.add)
                p.inc(ins, "vc")
                v.wait_ge(p.sems["vc"], p.cnt["vc"])
                ins = v.tensor_mul(srow2[0:1, :], slot[:, :], keep[:, :])
                p.inc(ins, "vc")
                ins = v.tensor_mul(notk[:, :], notk[:, :], bigi[:, :])
                p.inc(ins, "vc")
                v.wait_ge(p.sems["vc"], p.cnt["vc"])
                ins = v.tensor_add(srow2[0:1, :], srow2[0:1, :], notk[:, :])
                p.inc(ins, "vc")
            p.barrier()

            tn_b, vc_b = p.cnt["tn"], p.cnt["vc"]

            @block.tensor
            def _(t, vc_b=vc_b):
                p.drain(t, "tn")
                for i in range(NT):
                    if i >= 1:
                        t.wait_ge(p.sems["vc"], vc_b + 2 * i)
                    ins = t.transpose(ps_t2[:, :],
                                      srow2[0:1, i * 128:(i + 1) * 128],
                                      ident[0:1, 0:1])
                    p.inc(ins, "tn")

            @block.vector
            def _(v, tn_b=tn_b):
                p.drain(v, "vc")
                for i in range(NT):
                    v.wait_ge(p.sems["tn"], tn_b + i + 1)
                    ins = v.tensor_copy(sidx_tm[:, i:i + 1], ps_t2[:, 0:1])
                    p.inc(ins, "vc")
                    ins = v.tensor_copy(gidx_tm[:, i:i + 1], ps_t2[:, 0:1])
                    p.inc(ins, "vc")
            p.barrier()

        if upto < 2:
            p.ctx.close()
            return p
        with ExitStack() as scs:
            xnt = [p.sbuf(f"xnt{k}", [128, D], BF16, scs) for k in range(2)]
            dmi_b, dmg_b = p.cnt["dmi"], p.cnt["dmg"]

            @block.sync
            def _(sy):
                p.drain(sy, "sy")
                for i in range(NT):
                    if i >= 2:
                        sy.wait_ge(p.sems["dmg"], dmg_b + 16 * (i - 1))
                    ins = sy.dma_start(out=xnt[i % 2][:, :],
                                       in_=xn_dram[i * 128:(i + 1) * 128, :])
                    p.inc(ins, "dmi")

            @block.gpsimd
            def _(g):
                p.drain(g, "go")
                for i in range(NT):
                    g.wait_ge(p.sems["dmi"], dmi_b + 16 * (i + 1))
                    ins = g.indirect_dma_start(
                        out=moe_in, out_offset=bass.IndirectOffsetOnAxis(
                            sidx_tm[:, i:i + 1], 0),
                        in_=xnt[i % 2][:, :], in_offset=None,
                        bounds_check=8 * L1_CLOC - 1, oob_is_err=False)
                    p.inc(ins, "dmg")
            p.barrier()

        if upto < 3:
            p.ctx.close()
            return p
        ph_moe_experts(p, block, moe_in, moe_out, w1_d, b1_d, w2_d, b2_d,
                       ones128, L1_CLOC)
        p.barrier()

        if upto < 4:
            p.ctx.close()
            return p
        with ExitStack() as scg:
            gt = [p.sbuf(f"gt{k}", [128, D], BF16, scg) for k in range(2)]
            xt = [p.sbuf(f"xt{k}", [128, D], F32, scg) for k in range(2)]
            s1t = [p.sbuf(f"s1t{k}", [128, D], F32, scg) for k in range(2)]
            gsil = [p.sbuf(f"gsil{k}", [128, D], F32, scg) for k in range(2)]
            dmi_b, dmg_b, sc_b, vc_b, dmo_b = (p.cnt[s] for s in
                                               ("dmi", "dmg", "sc", "vc", "dmo"))

            @block.sync
            def _(sy):
                p.drain(sy, "sy")
                for i in range(NT):
                    if i >= 2:
                        sy.wait_ge(p.sems["sc"], sc_b + i - 1)
                        sy.wait_ge(p.sems["vc"], vc_b + 2 * (i - 1))
                        sy.wait_ge(p.sems["dmo"], dmo_b + 16 * (i - 1))
                    ins = sy.dma_start(out=gt[i % 2][:, :],
                                       in_=xn_dram[i * 128:(i + 1) * 128, :])
                    p.inc(ins, "dmi")
                    ins = sy.dma_start(out=xt[i % 2][:, :],
                                       in_=x_dram[i * 128:(i + 1) * 128, :])
                    p.inc(ins, "dmi")
                    ins = sy.dma_start(out=s1t[i % 2][:, :],
                                       in_=s1_dram[i * 128:(i + 1) * 128, :])
                    p.inc(ins, "dmi")



            @block.scalar
            def _(s, dmg_b=dmg_b, vc_b=vc_b):
                p.drain(s, "sc")
                for i in range(NT):
                    s.wait_ge(p.sems["dmg"], dmg_b + 16 * (i + 1))
                    if i >= 2:
                        s.wait_ge(p.sems["vc"], vc_b + 2 * (i - 1))
                    ins = s.activation(gsil[i % 2][:, :], gt[i % 2][:, :], AF.Silu)
                    p.inc(ins, "sc")

            @block.vector
            def _(v):
                p.drain(v, "vc")
                for i in range(NT):
                    v.wait_ge(p.sems["sc"], sc_b + i + 1)
                    v.wait_ge(p.sems["dmi"], dmi_b + 48 * (i + 1))
                    ins = v.tensor_mul(gsil[i % 2][:, :], gsil[i % 2][:, :],
                                       s1t[i % 2][:, :])
                    p.inc(ins, "vc")
                    v.wait_ge(p.sems["vc"], p.cnt["vc"])
                    ins = v.tensor_add(xt[i % 2][:, :], xt[i % 2][:, :],
                                       gsil[i % 2][:, :])
                    p.inc(ins, "vc")

            @block.gpsimd
            def _(g, vc_b=vc_b):
                p.drain(g, "go")
                def store(k):
                    g.wait_ge(p.sems["vc"], vc_b + 2 * (k + 1))
                    ins = g.dma_start(out=x2_dram[k * 128:(k + 1) * 128, :],
                                      in_=xt[k % 2][:, :])
                    p.inc(ins, "dmo")
                for i in range(NT):
                    if i >= 2:
                        store(i - 2)
                    g.wait_ge(p.sems["dmi"], dmi_b + 48 * i + 16)
                    if skip_gather:
                        ins = g.dma_start(out=gt[i % 2][0:1, 0:1],
                                          in_=moe_out[0:1, 0:1])
                    else:
                        ins = g.indirect_dma_start(
                            out=gt[i % 2][:, :], out_offset=None,
                            in_=moe_out, in_offset=bass.IndirectOffsetOnAxis(
                                gidx_tm[:, i:i + 1], 0),
                            bounds_check=8 * L1_CLOC - 1, oob_is_err=False)
                    p.inc(ins, "dmg")
                store(NT - 2)
                store(NT - 1)
                g.wait_ge(p.sems["dmo"], p.cnt["dmo"])
                g.wait_ge(p.sems["dmg"], p.cnt["dmg"])
    p.ctx.close()
    return p


def run_l1b(inputs, x1, l1a_results):
    if "l1b" not in _CACHE:
        _CACHE["l1b"] = build_l1b()
    p = _CACHE["l1b"]
    counts_all = np.concatenate([l1a_results[c]["counts1"] for c in range(NCORES)],
                                axis=0)
    W1 = np.asarray(inputs["L1_ffn_W1"], np.float32)
    W2 = np.asarray(inputs["L1_ffn_W2"], np.float32)
    b1 = np.asarray(inputs["L1_ffn_b1"], np.float32)
    base = {
        "W1L": W1.reshape(8 * D, 4096),
        "W2L": W2.reshape(8 * 4096, D),
        "b1L": b1.reshape(8, 32, 128).transpose(0, 2, 1).reshape(8 * 128, 32),
        "b2L": np.asarray(inputs["L1_ffn_b2"], np.float32),
        "countsall": counts_all,
        "c_ones1x128": np.ones((1, 128), np.float32),
    }
    base = {k: np.ascontiguousarray(np.asarray(v, np.float32))
            for k, v in base.items()}
    in_maps = []
    for c in range(NCORES):
        m = dict(base)
        m.update(l1b_consts(c))
        m["x"] = np.ascontiguousarray(np.asarray(x1[c], np.float32))
        m["s1"] = np.ascontiguousarray(l1a_results[c]["s1"])
        m["xn1"] = np.ascontiguousarray(l1a_results[c]["xn1"])
        m["aidxrow1"] = np.ascontiguousarray(l1a_results[c]["aidxrow1"])
        in_maps.append(m)
    res = run_bass_kernel_spmd(p.nc, in_maps, core_ids=list(range(NCORES)))
    return np.stack([res.results[c]["x2"] for c in range(NCORES)])





L2_CLOC = 1024


def l2_consts():
    hd, half = 64, 32
    inv_freq = 1.0 / 10000.0 ** (np.arange(0, hd, 2, dtype=np.float64) / hd)
    ang = np.arange(S, dtype=np.float64)[:, None] * inv_freq
    cos = np.cos(ang).T
    sin = np.sin(ang).T
    cos128 = np.concatenate([cos, cos, cos, cos], axis=0)
    sins128 = np.concatenate([-sin, sin, -sin, sin], axis=0)
    masks = []
    for o in (0, 128, 256, 384):
        kt = np.arange(128)[:, None] + o
        qt = np.arange(512)[None, :]
        masks.append(np.where(kt <= qt, 0.0, -1e9))
    e = np.zeros((16, 8 * 128), np.float32)
    for c in range(8):
        e[2 * c, c * 128:c * 128 + 64] = 1.0
        e[2 * c + 1, c * 128 + 64:c * 128 + 128] = 1.0
    cc = np.zeros((128, 4), np.float32)
    cc[:, 0] = 1e-5; cc[:, 1] = -0.5; cc[:, 2] = 1e-12; cc[:, 3] = 1.0
    return {
        "c_cosq": _bf16(cos128 / 8.0), "c_sinq": _bf16(sins128 / 8.0),
        "c_cosk": _bf16(cos128), "c_sink": _bf16(sins128),
        "c_maskc": np.stack(masks).astype(np.float32),
        "c_zexp": e,
        "c_ident": np.eye(128, dtype=np.float32),
        "c_ones1x128": np.ones((1, 128), np.float32),
        "c_iota8": np.broadcast_to(np.arange(8, dtype=np.float32), (128, 8)).copy(),
        "c_ccol4": cc,
        "c_headsel": _bf16(np.concatenate([
            np.concatenate([np.ones((64, 1)), np.zeros((64, 1))], 0),
            np.concatenate([np.zeros((64, 1)), np.ones((64, 1))], 0)], 1)),
    }


def build_l2a():
    p = Prog()
    nc = p.nc
    x_dram = p.dram("x", [S, D], F32, kind="ExternalInput")
    wql_d = p.dram("Wql", [D, D], F32, kind="ExternalInput")
    wkl_d = p.dram("Wkl", [D, D], F32, kind="ExternalInput")
    wvl_d = p.dram("Wvl", [D, D], F32, kind="ExternalInput")
    wpl_d = p.dram("Wpl", [D, D], F32, kind="ExternalInput")
    bpl_d = p.dram("bpl", [1, D], F32, kind="ExternalInput")
    ln1g_d = p.dram("ln1g", [128, ND], F32, kind="ExternalInput")
    ln1b_d = p.dram("ln1b", [128, ND], F32, kind="ExternalInput")
    ln2g_d = p.dram("ln2g", [128, ND], F32, kind="ExternalInput")
    ln2b_d = p.dram("ln2b", [128, ND], F32, kind="ExternalInput")
    wq2_d = p.dram("Wq2", [D, D], F32, kind="ExternalInput")
    wk2_d = p.dram("Wk2", [D, D], F32, kind="ExternalInput")
    wv2_d = p.dram("Wv2", [D, D], F32, kind="ExternalInput")
    wp2_d = p.dram("Wp2", [D, D], F32, kind="ExternalInput")
    wg_d = p.dram("Wg", [D, 8], F32, kind="ExternalInput")
    cosq_d = p.dram("c_cosq", [128, S], BF16, kind="ExternalInput")
    sinq_d = p.dram("c_sinq", [128, S], BF16, kind="ExternalInput")
    cosk_d = p.dram("c_cosk", [128, S], BF16, kind="ExternalInput")
    sink_d = p.dram("c_sink", [128, S], BF16, kind="ExternalInput")
    maskc_d = p.dram("c_maskc", [4, 128, 512], F32, kind="ExternalInput")
    zexp_d = p.dram("c_zexp", [16, 8 * 128], F32, kind="ExternalInput")
    ident_d = p.dram("c_ident", [128, 128], F32, kind="ExternalInput")
    ones_d = p.dram("c_ones1x128", [1, 128], F32, kind="ExternalInput")
    iota8_d = p.dram("c_iota8", [128, 8], F32, kind="ExternalInput")
    ccol_d = p.dram("c_ccol4", [128, 4], F32, kind="ExternalInput")
    headsel_d = p.dram("c_headsel", [128, 2], BF16, kind="ExternalInput")

    s2_dram = p.dram("s2", [S, D], F32, kind="ExternalOutput")
    x2i_dram = p.dram("x2i", [S, D], F32, kind="ExternalOutput")
    h2_dram = p.dram("h2", [S, D], BF16, kind="ExternalOutput")
    counts_dram = p.dram("counts2", [1, 8], F32, kind="ExternalOutput")
    e1r_dram = p.dram("e1row", [NT, 128], F32, kind="ExternalOutput")
    e2r_dram = p.dram("e2row", [NT, 128], F32, kind="ExternalOutput")
    g1_dram = p.dram("g1tm", [128, NT], F32, kind="ExternalOutput")
    g2_dram = p.dram("g2tm", [128, NT], F32, kind="ExternalOutput")

    xn_dram = p.dram("xn_scr", [S, D], BF16)
    at2_dram = p.dram("at2_scr", [S, D], BF16)
    crow_dram = p.dram("crow_scr", [NT, 128], F32)
    z2_dram = p.dram("z2_scr", [1, 16 * S], F32)
    nrm_dram = p.dram("nrm_scr", [16, 128], F32)

    ccol = p.sbuf("ccol", [128, 4], F32)
    ones128 = p.sbuf("ones128", [1, 128], F32)
    ident = p.sbuf("ident", [128, 128], F32)
    varc = p.sbuf("varc", [128, NT], F32)
    h1T = p.sbuf("h1T", [128, ND * S], BF16)

    with nc.Block() as block:
        @block.sync
        def _(sy):
            for dst, src in ((ccol, ccol_d), (ones128, ones_d), (ident, ident_d)):
                ins = sy.dma_start(out=dst[:, :], in_=src)
                p.inc(ins, "dmi")
        p.barrier()

        with ExitStack() as sc0:
            ph_layernorm(p, block, x_dram, xn_dram, sc0, ccol[:, 0:1],
                         ccol[:, 1:2], keep_var=varc)
            p.barrier()
        scx0 = ExitStack()
        xnT = p.sbuf("xnT", [128, ND * S], BF16, scx0)
        ph_load_xnT(p, block, xn_dram, xnT)
        p.barrier()

        with ExitStack() as scc:
            tcol = p.sbuf("tcol", [128, NT], F32, scc)
            crow = p.sbuf("crow", [1, S], F32, scc)
            c_bc = p.sbuf("c_bc", [128, S], F32, scc)
            g1col = p.sbuf("g1col", [128, ND], F32, scc)
            b1col = p.sbuf("b1col", [128, ND], F32, scc)
            ps_ct = p.psum("ps_ct", [NT, 128], F32, scc)
            ps_cb = [p.psum(f"ps_cb{k}", [128, 512], F32, scc) for k in range(2)]
            rowbuf = p.sbuf("rowbufc", [NT, 128], F32, scc)

            @block.sync
            def _(sy):
                p.drain(sy, "sy")
                ins = sy.dma_start(out=g1col[:, :], in_=ln1g_d)
                p.inc(ins, "dmi")
                ins = sy.dma_start(out=b1col[:, :], in_=ln1b_d)
                p.inc(ins, "dmi")
            p.barrier()

            @block.vector
            def _(v):
                p.drain(v, "vc")
                ins = v.tensor_scalar(tcol[:, :], varc[:, :], 1e-5, None, ALU.add)
                p.inc(ins, "vc")
                v.wait_ge(p.sems["vc"], p.cnt["vc"])
                ins = v.reciprocal(tcol[:, :], tcol[:, :])
                p.inc(ins, "vc")
                v.wait_ge(p.sems["vc"], p.cnt["vc"])
                ins = v.tensor_mul(tcol[:, :], tcol[:, :], varc[:, :])
                p.inc(ins, "vc")
                v.wait_ge(p.sems["vc"], p.cnt["vc"])
                ins = v.tensor_scalar(tcol[:, :], tcol[:, :], 1e-5, None, ALU.add)
                p.inc(ins, "vc")

            @block.scalar
            def _(s):
                p.drain(s, "sc")
                p.wait(s, "vc")
                ins = s.activation(tcol[:, :], tcol[:, :], AF.Ln)
                p.inc(ins, "sc")
                s.wait_ge(p.sems["sc"], p.cnt["sc"])
                ins = s.activation(tcol[:, :], tcol[:, :], AF.Exp,
                                   scale=ccol[:, 1:2])
                p.inc(ins, "sc")

            @block.tensor
            def _(t):
                p.drain(t, "tn")
                p.wait(t, "sc")
                ins = t.transpose(ps_ct[:, :], tcol[:, :], ident[:, :])
                p.inc(ins, "tn")

            @block.vector
            def _(v):
                p.drain(v, "vc")
                p.wait(v, "tn")
                ins = v.tensor_copy(rowbuf[:, :], ps_ct[:, :])
                p.inc(ins, "vc")

            @block.gpsimd
            def _(g):
                p.drain(g, "go")
                p.wait(g, "vc")
                ins = g.dma_start(out=crow_dram, in_=rowbuf[:, :])
                p.inc(ins, "dmo")

            @block.sync
            def _(sy):
                p.drain(sy, "sy")
                p.wait(sy, "dmo")
                ins = sy.dma_start(out=crow[:, :],
                                   in_=crow_dram.rearrange("i p -> (i p)"))
                p.inc(ins, "dmi")
            p.barrier()

            tn_b, vc_b = p.cnt["tn"], p.cnt["vc"]

            @block.tensor
            def _(t, vc_b=vc_b):
                p.drain(t, "tn")
                for j in range(4):
                    bk = j % 2
                    if j >= 2:
                        t.wait_ge(p.sems["vc"], vc_b + j - 1)
                    ins = t.matmul(ps_cb[bk][:, :], lhsT=ones128[:, :],
                                   rhs=crow[:, j * 512:(j + 1) * 512],
                                   start=True, stop=True)
                    p.inc(ins, "tn")

            @block.vector
            def _(v, tn_b=tn_b):
                p.drain(v, "vc")
                for j in range(4):
                    bk = j % 2
                    v.wait_ge(p.sems["tn"], tn_b + j + 1)
                    ins = v.tensor_copy(c_bc[:, j * 512:(j + 1) * 512],
                                        ps_cb[bk][:, :])
                    p.inc(ins, "vc")
            p.barrier()

            @block.vector
            def _(v):
                p.drain(v, "vc")
                for c in range(ND):
                    sl = slice(c * S, (c + 1) * S)
                    ins = v.tensor_mul(h1T[:, sl], xnT[:, sl], c_bc[:, :])
                    p.inc(ins, "vc")
                    v.wait_ge(p.sems["vc"], p.cnt["vc"])
                    ins = v.tensor_scalar(h1T[:, sl], h1T[:, sl],
                                          g1col[:, c:c + 1], b1col[:, c:c + 1],
                                          ALU.mult, ALU.add)
                    p.inc(ins, "vc")
            p.barrier()

        with ExitStack() as scl:
            qT = p.sbuf("qTl", [128, ND * S], BF16, scl)
            vext2 = p.sbuf("vext2", [128, NT * 1040], BF16, scl)
            kvx = p.sbuf("kvx", [128, 16 * 65], BF16, scl)
            emin = p.sbuf("emin", [128, 512], F32, scl)
            emax = p.sbuf("emax", [128, 512], F32, scl)
            dcol = p.sbuf("dcol", [128, 1], F32, scl)
            ps_q = [p.psum(f"ps_ql{k}", [128, 512], F32, scl) for k in range(2)]
            ps_kv = p.psum("ps_kv", [128, 65], F32, scl)
            ps_ln = [p.psum(f"ps_ln{k}", [128, 65], F32, scl) for k in range(2)]

            @block.vector
            def _(v):
                p.drain(v, "vc")
                for i in range(NT):
                    for h in range(16):
                        off = i * 1040 + h * 65 + 64
                        ins = v.memset(vext2[:, off:off + 1], 1.0)
                        p.inc(ins, "vc")
            p.barrier()

            def proj_elu(wb, fm, dst, dst_len, gslice=None):
                rng = ND if fm else NT
                for g8 in range(rng):
                    tn_b, vc_b, sc_b = p.cnt["tn"], p.cnt["vc"], p.cnt["sc"]
                    nj = 4 if fm else 2

                    @block.tensor
                    def _(t, g8=g8, vc_b=vc_b, nj=nj):
                        p.drain(t, "tn")
                        for j in range(nj):
                            bk = j % 2
                            if j >= 2:
                                t.wait_ge(p.sems["vc"], vc_b + 3 * (j - 1))
                            ins = None
                            for cc in range(ND):
                                if fm:
                                    lh = wb[:, cc * D + g8 * 128:
                                            cc * D + (g8 + 1) * 128]
                                    rh = xnT[:, cc * S + j * 512:
                                             cc * S + (j + 1) * 512]
                                else:
                                    lh = xnT[:, cc * S + g8 * 128:
                                             cc * S + (g8 + 1) * 128]
                                    rh = wb[:, cc * D + j * 512:
                                            cc * D + (j + 1) * 512]
                                ins = t.matmul(ps_q[bk][:, :], lhsT=lh, rhs=rh,
                                               start=(cc == 0),
                                               stop=(cc == ND - 1))
                            p.inc(ins, "tn")

                    @block.vector
                    def _(v, g8=g8, tn_b=tn_b, sc_b=sc_b, nj=nj):
                        p.drain(v, "vc")
                        for j in range(nj):
                            bk = j % 2
                            v.wait_ge(p.sems["tn"], tn_b + j + 1)
                            ins = v.tensor_scalar_min(emin[:, :], ps_q[bk][:, :],
                                                      0.0)
                            p.inc(ins, "vc")
                            ins = v.tensor_scalar_max(emax[:, :], ps_q[bk][:, :],
                                                      0.0)
                            p.inc(ins, "vc")
                            v.wait_ge(p.sems["sc"], sc_b + j + 1)
                            out = dst[:, g8 * dst_len + j * 512:
                                      g8 * dst_len + (j + 1) * 512]
                            ins = v.tensor_add(out, emin[:, :], emax[:, :])
                            p.inc(ins, "vc")

                    @block.scalar
                    def _(s, vc_b=vc_b, nj=nj):
                        p.drain(s, "sc")
                        for j in range(nj):
                            s.wait_ge(p.sems["vc"], vc_b + 3 * j + 1)
                            ins = s.activation(emin[:, :], emin[:, :], AF.Exp)
                            p.inc(ins, "sc")
                    p.barrier()

            with ExitStack() as scv:
                wvlb = p.sbuf("wvl_b", [128, ND * D], BF16, scv)
                ph_load_weight_cast(p, block, wvl_d, D, D, wvlb, "wvl",
                                    col_block=512)
                p.barrier()
                for i in range(NT):
                    tn_b = p.cnt["tn"]

                    @block.tensor
                    def _(t, i=i):
                        p.drain(t, "tn")
                        for j in range(2):
                            bk = j % 2
                            ins = None
                            for cc in range(ND):
                                ins = t.matmul(
                                    ps_q[bk][:, :],
                                    lhsT=xnT[:, cc * S + i * 128: cc * S + (i + 1) * 128],
                                    rhs=wvlb[:, cc * D + j * 512: cc * D + (j + 1) * 512],
                                    start=(cc == 0), stop=(cc == ND - 1))
                            p.inc(ins, "tn")

                    @block.scalar
                    def _(s, i=i, tn_b=tn_b):
                        p.drain(s, "sc")
                        for j in range(2):
                            bk = j % 2
                            s.wait_ge(p.sems["tn"], tn_b + j + 1)
                            last = None
                            for h8 in range(8):
                                h = j * 8 + h8
                                last = s.copy(
                                    vext2[:, i * 1040 + h * 65: i * 1040 + h * 65 + 64],
                                    ps_q[bk][:, h8 * 64:(h8 + 1) * 64])
                            p.inc(last, "sc")
                    p.barrier()

            with ExitStack() as sck:
                wklb = p.sbuf("wkl_b", [128, ND * D], BF16, sck)
                ktok = p.sbuf("ktok", [128, NT * D], BF16, sck)
                ph_load_weight_cast(p, block, wkl_d, D, D, wklb, "wkl",
                                    col_block=512)
                p.barrier()
                proj_elu(wklb, False, ktok, D)
                for h in range(16):
                    rb = (h % 2) * 64
                    tn_b = p.cnt["tn"]

                    @block.tensor
                    def _(t, h=h):
                        p.drain(t, "tn")
                        ins = None
                        for i in range(NT):
                            ins = t.matmul(
                                ps_kv[0:64, :],
                                lhsT=ktok[:, i * D + h * 64: i * D + h * 64 + 64],
                                rhs=vext2[:, i * 1040 + h * 65: i * 1040 + (h + 1) * 65],
                                start=(i == 0), stop=(i == NT - 1))
                        p.inc(ins, "tn")

                    @block.vector
                    def _(v, h=h, rb=rb):
                        p.drain(v, "vc")
                        p.wait(v, "tn")
                        ins = v.tensor_copy(kvx[rb:rb + 64, h * 65:(h + 1) * 65],
                                            ps_kv[0:64, :])
                        p.inc(ins, "vc")
                    p.barrier()

            with ExitStack() as scq2:
                wqlb = p.sbuf("wql_b", [128, ND * D], BF16, scq2)
                ph_load_weight_cast(p, block, wql_d, D, D, wqlb, "wql",
                                    col_block=512)
                p.barrier()
                proj_elu(wqlb, True, qT, S)

            with ExitStack() as scd:
                attn2 = p.sbuf("attn2", [128, NT * D], BF16, scd)
                for h in range(16):
                    rb = (h % 2) * 64
                    c = h // 2
                    tn_b, vc_b = p.cnt["tn"], p.cnt["vc"]

                    @block.tensor
                    def _(t, h=h, rb=rb, c=c, vc_b=vc_b):
                        p.drain(t, "tn")
                        for i in range(NT):
                            bk = i % 2
                            if i >= 2:
                                t.wait_ge(p.sems["vc"], vc_b + 3 * (i - 1))
                            ins = t.matmul(
                                ps_ln[bk][:, :],
                                lhsT=qT[rb:rb + 64, c * S + i * 128: c * S + (i + 1) * 128],
                                rhs=kvx[rb:rb + 64, h * 65:(h + 1) * 65],
                                start=True, stop=True)
                            p.inc(ins, "tn")

                    @block.vector
                    def _(v, h=h, tn_b=tn_b):
                        p.drain(v, "vc")
                        for i in range(NT):
                            bk = i % 2
                            v.wait_ge(p.sems["tn"], tn_b + i + 1)
                            ins = v.tensor_scalar(dcol[:, :], ps_ln[bk][:, 64:65],
                                                  1e-6, None, ALU.add)
                            p.inc(ins, "vc")
                            v.wait_ge(p.sems["vc"], p.cnt["vc"])
                            ins = v.reciprocal(dcol[:, :], dcol[:, :])
                            p.inc(ins, "vc")
                            v.wait_ge(p.sems["vc"], p.cnt["vc"])
                            ins = v.tensor_scalar_mul(
                                attn2[:, i * D + h * 64: i * D + (h + 1) * 64],
                                ps_ln[bk][:, 0:64], dcol[:, :])
                            p.inc(ins, "vc")
                    p.barrier()

                @block.gpsimd
                def _(g):
                    p.drain(g, "go")
                    for i in range(NT):
                        ins = g.dma_start(out=at2_dram[i * 128:(i + 1) * 128, :],
                                          in_=attn2[:, i * D:(i + 1) * D])
                        p.inc(ins, "dmo")
                p.barrier()

            @block.gpsimd
            def _(g):
                p.drain(g, "go")
                for i in range(NT):
                    ins = g.dma_start(out=at2_dram[i * 128:(i + 1) * 128, :],
                                      in_=attn2[:, i * D:(i + 1) * D])
                    p.inc(ins, "dmo")
            p.barrier()

        with ExitStack() as scp:
            at2T = p.sbuf("at2T", [128, ND * S], BF16, scp)
            wplb = p.sbuf("wpl_b", [128, ND * D], BF16, scp)
            bplrow = p.sbuf("bplrow", [1, D], F32, scp)
            bpl_bc = p.sbuf("bpl_bc", [128, D], F32, scp)
            s2t = [p.sbuf(f"s2t{k}", [128, D], F32, scp) for k in range(2)]
            ps_s2 = [p.psum(f"ps_s2{k}", [128, 512], F32, scp) for k in range(2)]
            ph_load_xnT(p, block, at2_dram, at2T)
            ph_load_weight_cast(p, block, wpl_d, D, D, wplb, "wpl", col_block=512)

            @block.sync
            def _(sy):
                p.drain(sy, "sy")
                ins = sy.dma_start(out=bplrow[:, :], in_=bpl_d)
                p.inc(ins, "dmi")
            p.barrier()

            for dh in range(2):
                @block.tensor
                def _(t, dh=dh):
                    p.drain(t, "tn")
                    if dh:
                        p.wait(t, "vc")
                    ins = t.matmul(ps_s2[0][:, :], lhsT=ones128[:, :],
                                   rhs=bplrow[:, dh * 512:(dh + 1) * 512],
                                   start=True, stop=True)
                    p.inc(ins, "tn")

                @block.vector
                def _(v, dh=dh):
                    p.drain(v, "vc")
                    p.wait(v, "tn")
                    ins = v.tensor_copy(bpl_bc[:, dh * 512:(dh + 1) * 512],
                                        ps_s2[0][:, :])
                    p.inc(ins, "vc")
            p.barrier()

            tn_b, vc_b, dmo_b = p.cnt["tn"], p.cnt["vc"], p.cnt["dmo"]

            @block.tensor
            def _(t, vc_b=vc_b):
                p.drain(t, "tn")
                n = 0
                for i in range(NT):
                    for dh in range(2):
                        bk = n % 2
                        if n >= 2:
                            t.wait_ge(p.sems["vc"], vc_b + n - 1)
                        ins = None
                        for cc in range(ND):
                            ins = t.matmul(
                                ps_s2[bk][:, :],
                                lhsT=at2T[:, cc * S + i * 128: cc * S + (i + 1) * 128],
                                rhs=wplb[:, cc * D + dh * 512: cc * D + (dh + 1) * 512],
                                start=(cc == 0), stop=(cc == ND - 1))
                        p.inc(ins, "tn")
                        n += 1

            @block.vector
            def _(v, tn_b=tn_b, dmo_b=dmo_b):
                p.drain(v, "vc")
                n = 0
                for i in range(NT):
                    if i >= 2:
                        v.wait_ge(p.sems["dmo"], dmo_b + 16 * (i - 1))
                    for dh in range(2):
                        bk = n % 2
                        v.wait_ge(p.sems["tn"], tn_b + n + 1)
                        ins = v.tensor_add(s2t[i % 2][:, dh * 512:(dh + 1) * 512],
                                           ps_s2[bk][:, :],
                                           bpl_bc[:, dh * 512:(dh + 1) * 512])
                        p.inc(ins, "vc")
                        n += 1

            @block.gpsimd
            def _(g, vc_b=vc_b):
                p.drain(g, "go")
                for i in range(NT):
                    g.wait_ge(p.sems["vc"], vc_b + 2 * (i + 1))
                    ins = g.dma_start(out=s2_dram[i * 128:(i + 1) * 128, :],
                                      in_=s2t[i % 2][:, :])
                    p.inc(ins, "dmo")
            p.barrier()

        scx0.close()
        env = dict(wq2_d=wq2_d, wk2_d=wk2_d, wv2_d=wv2_d, wp2_d=wp2_d,
                   cosq_d=cosq_d, sinq_d=sinq_d, cosk_d=cosk_d, sink_d=sink_d,
                   maskc_d=maskc_d, zexp_d=zexp_d, z2_dram=z2_dram,
                   xn_dram=xn_dram, x2i_dram=x2i_dram, h2pre_dram=h2_dram,
                   ln2g_d=ln2g_d, ln2b_d=ln2b_d, nrm_dram=nrm_dram,
                   wg_d=wg_d, iota8_d=iota8_d, counts_dram=counts_dram,
                   headsel_d=headsel_d,
                   e1r_dram=e1r_dram, e2r_dram=e2r_dram, g1_dram=g1_dram,
                   g2_dram=g2_dram)
        build_l2a_inner(p, block, xnT, h1T, ident, ones128, ccol, env)
    p.ctx.close()
    return p




def build_l2a_inner(p, block, xnT, h1T, ident, ones128, ccol, env):
    nc = p.nc
    S_, D_ = S, D

    with ExitStack() as sca:
        maskc = p.sbuf("maskc", [128, 4 * 512], F32, sca)

        @block.sync
        def _(sy):
            p.drain(sy, "sy")
            for o in range(4):
                ins = sy.dma_start(out=maskc[:, o * 512:(o + 1) * 512],
                                   in_=env["maskc_d"][o, :, :])
                p.inc(ins, "dmi")
        p.barrier()

        att3T = p.sbuf("att3T", [128, ND * S], BF16, sca)
        scqk = ExitStack()
        qrT = p.sbuf("qrT", [128, ND * S], BF16, scqk)
        krT = p.sbuf("krT", [128, ND * S], BF16, scqk)
        vext3 = p.sbuf("vext3", [128, NT * 1040], BF16, scqk)
        qsw = p.sbuf("qsw", [128, S], BF16, scqk)
        scpr = ExitStack()
        ps_pr = [p.psum(f"ps_pr2{k}", [128, 512], F32, scpr) for k in range(2)]

        @block.vector
        def _(v):
            p.drain(v, "vc")
            for i in range(NT):
                for h in range(16):
                    off = i * 1040 + h * 65 + 64
                    ins = v.memset(vext3[:, off:off + 1], 1.0)
                    p.inc(ins, "vc")
        p.barrier()

        def proj_rope(wb, dst, cosT, sinT):
            for g8 in range(ND):
                tn_b, vc_b = p.cnt["tn"], p.cnt["vc"]

                @block.tensor
                def _(t, g8=g8, vc_b=vc_b):
                    p.drain(t, "tn")
                    for j in range(4):
                        bk = j % 2
                        if j >= 2:
                            t.wait_ge(p.sems["vc"], vc_b + j - 1)
                        ins = None
                        for cc in range(ND):
                            ins = t.matmul(
                                ps_pr[bk][:, :],
                                lhsT=wb[:, cc * D + g8 * 128: cc * D + (g8 + 1) * 128],
                                rhs=h1T[:, cc * S + j * 512: cc * S + (j + 1) * 512],
                                start=(cc == 0), stop=(cc == ND - 1))
                        p.inc(ins, "tn")

                @block.vector
                def _(v, g8=g8, tn_b=tn_b):
                    p.drain(v, "vc")
                    for j in range(4):
                        bk = j % 2
                        v.wait_ge(p.sems["tn"], tn_b + j + 1)
                        sl = slice(g8 * S + j * 512, g8 * S + (j + 1) * 512)
                        ins = v.tensor_copy(dst[:, sl], ps_pr[bk][:, :])
                        p.inc(ins, "vc")
                p.barrier()
            for g8 in range(ND):
                sl = slice(g8 * S, (g8 + 1) * S)
                dmg_b = p.cnt["dmg"]

                @block.gpsimd
                def _(g, g8=g8, sl=sl):
                    p.drain(g, "go")
                    for a, b in ((0, 32), (32, 0), (64, 96), (96, 64)):
                        ins = g.dma_start(out=qsw[a:a + 32, 0:S],
                                          in_=dst[b:b + 32, sl])
                        p.inc(ins, "dmg")

                @block.vector
                def _(v, g8=g8, sl=sl, dmg_b=dmg_b):
                    p.drain(v, "vc")
                    v.wait_ge(p.sems["dmg"], dmg_b + 64)
                    ins = v.tensor_mul(qsw[:, :], qsw[:, :], sinT[:, :])
                    p.inc(ins, "vc")
                    v.wait_ge(p.sems["vc"], p.cnt["vc"])
                    ins = v.tensor_mul(dst[:, sl], dst[:, sl], cosT[:, :])
                    p.inc(ins, "vc")
                    v.wait_ge(p.sems["vc"], p.cnt["vc"])
                    ins = v.tensor_add(dst[:, sl], dst[:, sl], qsw[:, :])
                    p.inc(ins, "vc")
                p.barrier()

        for wi_, (w_d, dst, cq_d, sq_d) in enumerate((
                (env["wq2_d"], qrT, env["cosq_d"], env["sinq_d"]),
                (env["wk2_d"], krT, env["cosk_d"], env["sink_d"]))):
            with ExitStack() as scw:
                wb2 = p.sbuf(f"wqk2_b{wi_}", [128, ND * D], BF16, scw)
                cosT = p.sbuf(f"cosT{wi_}", [128, S], BF16, scw)
                sinT = p.sbuf(f"sinT{wi_}", [128, S], BF16, scw)
                ph_load_weight_cast(p, block, w_d, D, D, wb2, "wqk2",
                                    col_block=512)

                @block.sync
                def _(sy, cq_d=cq_d, sq_d=sq_d):
                    p.drain(sy, "sy")
                    ins = sy.dma_start(out=cosT[:, :], in_=cq_d)
                    p.inc(ins, "dmi")
                    ins = sy.dma_start(out=sinT[:, :], in_=sq_d)
                    p.inc(ins, "dmi")
                p.barrier()
                proj_rope(wb2, dst, cosT, sinT)

        with ExitStack() as scwv:
            wv2b = p.sbuf("wv2_b", [128, ND * D], BF16, scwv)
            ph_load_weight_cast(p, block, env["wv2_d"], D, D, wv2b, "wv2",
                                col_block=512)
            p.barrier()
            for i in range(NT):
                tn_b = p.cnt["tn"]

                @block.tensor
                def _(t, i=i):
                    p.drain(t, "tn")
                    for j in range(2):
                        bk = j % 2
                        ins = None
                        for cc in range(ND):
                            ins = t.matmul(
                                ps_pr[bk][:, :],
                                lhsT=h1T[:, cc * S + i * 128: cc * S + (i + 1) * 128],
                                rhs=wv2b[:, cc * D + j * 512: cc * D + (j + 1) * 512],
                                start=(cc == 0), stop=(cc == ND - 1))
                        p.inc(ins, "tn")

                @block.scalar
                def _(s, i=i, tn_b=tn_b):
                    p.drain(s, "sc")
                    for j in range(2):
                        bk = j % 2
                        s.wait_ge(p.sems["tn"], tn_b + j + 1)
                        last = None
                        for h8 in range(8):
                            h = j * 8 + h8
                            last = s.copy(
                                vext3[:, i * 1040 + h * 65: i * 1040 + h * 65 + 64],
                                ps_pr[bk][:, h8 * 64:(h8 + 1) * 64])
                        p.inc(last, "sc")
                p.barrier()

        scpr.close()

        nrm_dram = env["nrm_dram"]
        biasc = p.sbuf("biasc2", [128, 16], F32, scqk)
        scnp = ExitStack()
        hselx = p.sbuf("hselx", [128, 2], BF16, scnp)
        sqx2 = p.sbuf("sqx2", [128, 512], BF16, scnp)
        nmax_q = p.sbuf("nmax_q", [2, 8], F32, scnp)
        nmax_k = p.sbuf("nmax_k", [2, 8], F32, scnp)
        ncur = p.sbuf("ncur", [2, 1], F32, scnp)
        negM = p.sbuf("negM2", [2, 8], F32, scnp)
        ngrid = p.sbuf("ngrid", [8, 2], F32, scnp)
        nrow = p.sbuf("nrow", [1, 16], F32, scnp)
        ps_nn = [p.psum(f"ps_nn{k}", [2, 512], F32, scnp) for k in range(2)]
        ps_ng = p.psum("ps_ng", [8, 2], F32, scnp)
        ps_bb = p.psum("ps_bb2", [128, 16], F32, scnp)

        @block.sync
        def _(sy):
            p.drain(sy, "sy")
            ins = sy.dma_start(out=hselx[:, :], in_=env["headsel_d"])
            p.inc(ins, "dmi")
        p.barrier()

        for src, dstm in ((qrT, nmax_q), (krT, nmax_k)):
            for g8 in range(ND):
                sc_b, tn_b, vc_b = p.cnt["sc"], p.cnt["tn"], p.cnt["vc"]

                @block.scalar
                def _(s, src=src, g8=g8, vc_b=vc_b):
                    p.drain(s, "sc")
                    for j in range(4):
                        if j >= 1:
                            s.wait_ge(p.sems["vc"], vc_b + 2 * j)
                        ins = s.activation(
                            sqx2[:, :],
                            src[:, g8 * S + j * 512: g8 * S + (j + 1) * 512],
                            AF.Square)
                        p.inc(ins, "sc")

                @block.tensor
                def _(t, sc_b=sc_b, vc_b=vc_b):
                    p.drain(t, "tn")
                    for j in range(4):
                        t.wait_ge(p.sems["sc"], sc_b + j + 1)
                        if j >= 2:
                            t.wait_ge(p.sems["vc"], vc_b + 2 * (j - 1) + 1)
                        ins = t.matmul(ps_nn[j % 2][:, :], lhsT=hselx[:, :],
                                       rhs=sqx2[:, :], start=True, stop=True)
                        p.inc(ins, "tn")

                @block.vector
                def _(v, g8=g8, dstm=dstm, tn_b=tn_b):
                    p.drain(v, "vc")
                    for j in range(4):
                        v.wait_ge(p.sems["tn"], tn_b + j + 1)
                        ins = v.reduce_max(ncur[:, :], ps_nn[j % 2][:, :],
                                           axis=AXX)
                        p.inc(ins, "vc")
                        v.wait_ge(p.sems["vc"], p.cnt["vc"])
                        if j == 0:
                            ins = v.tensor_copy(dstm[:, g8:g8 + 1], ncur[:, :])
                        else:
                            ins = v.tensor_max(dstm[:, g8:g8 + 1],
                                               dstm[:, g8:g8 + 1], ncur[:, :])
                        p.inc(ins, "vc")
                p.barrier()
        @block.scalar
        def _(s):
            p.drain(s, "sc")
            ins = s.activation(nmax_q[:, :], nmax_q[:, :], AF.Sqrt)
            p.inc(ins, "sc")
            s.wait_ge(p.sems["sc"], p.cnt["sc"])
            ins = s.activation(nmax_k[:, :], nmax_k[:, :], AF.Sqrt)
            p.inc(ins, "sc")

        @block.vector
        def _(v):
            p.drain(v, "vc")
            p.wait(v, "sc")
            ins = v.tensor_mul(negM[:, :], nmax_q[:, :], nmax_k[:, :])
            p.inc(ins, "vc")
            v.wait_ge(p.sems["vc"], p.cnt["vc"])
            ins = v.tensor_scalar_mul(negM[:, :], negM[:, :], -1.0)
            p.inc(ins, "vc")

        @block.tensor
        def _(t):
            p.drain(t, "tn")
            p.wait(t, "vc")
            ins = t.transpose(ps_ng[:, :], negM[:, :], ident[0:2, 0:2])
            p.inc(ins, "tn")

        @block.vector
        def _(v):
            p.drain(v, "vc")
            p.wait(v, "tn")
            ins = v.tensor_copy(ngrid[:, :], ps_ng[:, :])
            p.inc(ins, "vc")

        @block.gpsimd
        def _(g):
            p.drain(g, "go")
            p.wait(g, "vc")
            ins = g.dma_start(
                out=nrm_dram[0:1, 0:16].rearrange("o (c par) -> (o c) par", c=8),
                in_=ngrid[:, :])
            p.inc(ins, "dmg")
        p.barrier()

        @block.sync
        def _(sy):
            p.drain(sy, "sy")
            ins = sy.dma_start(out=nrow[:, :], in_=nrm_dram[0:1, 0:16])
            p.inc(ins, "dmi")
        p.barrier()

        @block.tensor
        def _(t):
            p.drain(t, "tn")
            ins = t.matmul(ps_bb[:, :], lhsT=ones128[:, :], rhs=nrow[:, :],
                           start=True, stop=True)
            p.inc(ins, "tn")

        @block.vector
        def _(v):
            p.drain(v, "vc")
            p.wait(v, "tn")
            ins = v.tensor_copy(biasc[:, :], ps_bb[:, :])
            p.inc(ins, "vc")
        p.barrier()

        scnp.close()

        pt = [p.sbuf(f"pt2{k}", [128, 512], BF16, scqk) for k in range(2)]
        stmp = p.sbuf("stmp2", [128, 512], F32, scqk)
        zrow = p.sbuf("zrow2", [1, 512], F32, scqk)
        ps_sc = [p.psum(f"ps_sc2{k}", [128, 512], F32, scqk) for k in range(2)]
        ps_o = p.psum("ps_o2", [65, 512], F32, scqk)

        for h in range(16):
            rb = (h % 2) * 64
            c = h // 2
            for j in range(4):
                ni = 4 * j + 4
                tn_b2, sc_b2, vc_b2 = p.cnt["tn"], p.cnt["sc"], p.cnt["vc"]

                @block.tensor
                def _(t, h=h, rb=rb, c=c, j=j, ni=ni, sc_b2=sc_b2, vc_b2=vc_b2):
                    p.drain(t, "tn")
                    nmask = 0
                    for i in range(ni):
                        bk = i % 2
                        masked = (i >= 4 * j)
                        if i >= 2:
                            if (i - 2) >= 4 * j:
                                t.wait_ge(p.sems["vc"], vc_b2 + (i - 2) - 4 * j + 1)
                            else:
                                t.wait_ge(p.sems["sc"], sc_b2 + i - 1)
                        ins = t.matmul(
                            ps_sc[bk][:, :],
                            lhsT=krT[rb:rb + 64, c * S + i * 128: c * S + (i + 1) * 128],
                            rhs=qrT[rb:rb + 64, c * S + j * 512: c * S + (j + 1) * 512],
                            start=True, stop=True)
                        p.inc(ins, "tn")
                        t.wait_ge(p.sems["sc"], sc_b2 + i + 1)
                        ins = t.matmul(
                            ps_o[:, :],
                            lhsT=vext3[:, i * 1040 + h * 65: i * 1040 + (h + 1) * 65],
                            rhs=pt[bk][:, :],
                            start=(i == 0), stop=(i == ni - 1))
                        p.inc(ins, "tn")

                @block.vector
                def _(v, j=j, ni=ni, tn_b2=tn_b2, sc_b2=sc_b2):
                    p.drain(v, "vc")
                    for i in range(4 * j, ni):
                        bk = i % 2
                        v.wait_ge(p.sems["tn"], tn_b2 + 2 * i + 1)
                        if i - 4 * j >= 1:
                            v.wait_ge(p.sems["sc"], sc_b2 + i)
                        o = i - 4 * j
                        ins = v.tensor_add(stmp[:, :], ps_sc[bk][:, :],
                                           maskc[:, o * 512:(o + 1) * 512])
                        p.inc(ins, "vc")

                @block.scalar
                def _(s, h=h, rb=rb, c=c, j=j, ni=ni, tn_b2=tn_b2, vc_b2=vc_b2):
                    p.drain(s, "sc")
                    for i in range(ni):
                        bk = i % 2
                        if i >= 4 * j:
                            s.wait_ge(p.sems["vc"], vc_b2 + (i - 4 * j) + 1)
                            src = stmp[:, :]
                        else:
                            s.wait_ge(p.sems["tn"], tn_b2 + 2 * i + 1)
                            src = ps_sc[bk][:, :]
                        ins = s.activation(pt[bk][:, :], src, AF.Exp,
                                           bias=biasc[:, h:h + 1])
                        p.inc(ins, "sc")
                    s.wait_ge(p.sems["tn"], p.cnt["tn"])
                    s.copy(zrow[:, :], ps_o[64:65, :])
                    ins = s.copy(
                        att3T[rb:rb + 64, c * S + j * 512: c * S + (j + 1) * 512],
                        ps_o[0:64, :])
                    p.inc(ins, "sc")

                @block.gpsimd
                def _(g, h=h, j=j):
                    p.drain(g, "go")
                    p.wait(g, "sc")
                    ins = g.dma_start(
                        out=env["z2_dram"][0:1, h * S + j * 512: h * S + (j + 1) * 512],
                        in_=zrow[:, :])
                    p.inc(ins, "dmo")
                p.barrier()

        zall = p.sbuf("zall2", [16, S], F32, scqk)
        zexp = p.sbuf("zexp2", [16, 8 * 128], F32, scqk)
        ps_zb = [p.psum(f"ps_zb2{k}", [128, 512], F32, scqk) for k in range(2)]

        @block.sync
        def _(sy):
            p.drain(sy, "sy")
            p.wait(sy, "dmo")
            ins = sy.dma_start(out=zall[:, :],
                               in_=env["z2_dram"].rearrange("o (h t) -> (o h) t", h=16))
            p.inc(ins, "dmi")
            ins = sy.dma_start(out=zexp[:, :], in_=env["zexp_d"])
            p.inc(ins, "dmi")
        p.barrier()

        @block.vector
        def _(v):
            p.drain(v, "vc")
            ins = v.reciprocal(zall[:, :], zall[:, :])
            p.inc(ins, "vc")
        p.barrier()

        tn_b, vc_b = p.cnt["tn"], p.cnt["vc"]

        @block.tensor
        def _(t, vc_b=vc_b):
            p.drain(t, "tn")
            n = 0
            for c in range(ND):
                for j in range(4):
                    bk = n % 2
                    if n >= 2:
                        t.wait_ge(p.sems["vc"], vc_b + n - 1)
                    ins = t.matmul(ps_zb[bk][:, :],
                                   lhsT=zexp[:, c * 128:(c + 1) * 128],
                                   rhs=zall[:, j * 512:(j + 1) * 512],
                                   start=True, stop=True)
                    p.inc(ins, "tn")
                    n += 1

        @block.vector
        def _(v, tn_b=tn_b):
            p.drain(v, "vc")
            n = 0
            for c in range(ND):
                for j in range(4):
                    bk = n % 2
                    v.wait_ge(p.sems["tn"], tn_b + n + 1)
                    sl = att3T[:, c * S + j * 512: c * S + (j + 1) * 512]
                    ins = v.tensor_mul(sl, sl, ps_zb[bk][:, :])
                    p.inc(ins, "vc")
                    n += 1
        p.barrier()

        scqk.close()

        with ExitStack() as scx:
            wp2b = p.sbuf("wp2_b", [128, ND * D], BF16, scx)
            ph_load_weight_cast(p, block, env["wp2_d"], D, D, wp2b, "wp2",
                                col_block=512)
            g2col = p.sbuf("g2col", [128, ND], F32, scx)
            b2col = p.sbuf("b2col", [128, ND], F32, scx)

            @block.sync
            def _(sy):
                p.drain(sy, "sy")
                ins = sy.dma_start(out=g2col[:, :], in_=env["ln2g_d"])
                p.inc(ins, "dmi")
                ins = sy.dma_start(out=b2col[:, :], in_=env["ln2b_d"])
                p.inc(ins, "dmi")
            p.barrier()

            xnt2 = [p.sbuf(f"xnt2{k}", [128, D], BF16, scx) for k in range(2)]
            x2it = [p.sbuf(f"x2it{k}", [128, D], F32, scx) for k in range(2)]
            h2t = [p.sbuf(f"h2t{k}", [128, D], BF16, scx) for k in range(2)]
            sqx = p.sbuf("sqx", [128, D], F32, scx)
            s1c = p.sbuf("s1cx", [128, 1], F32, scx)
            s2c = p.sbuf("s2cx", [128, 1], F32, scx)
            mcx = p.sbuf("mcx", [128, 1], F32, scx)
            msx = p.sbuf("msx", [128, 1], F32, scx)
            vrx = p.sbuf("vrx", [128, 1], F32, scx)
            lnx = p.sbuf("lnx2", [128, 1], F32, scx)
            rsx = p.sbuf("rsx", [128, 1], F32, scx)
            ps_x = [p.psum(f"ps_x{k}", [128, 512], F32, scx) for k in range(2)]
            dmi_b, tn_b, vc_b, sc_b, dmo_b = (p.cnt[s] for s in
                                              ("dmi", "tn", "vc", "sc", "dmo"))
            NVX = 8

            @block.sync
            def _(sy):
                p.drain(sy, "sy")
                for i in range(NT):
                    if i >= 2:
                        sy.wait_ge(p.sems["vc"], vc_b + NVX * (i - 2) + NVX)
                    ins = sy.dma_start(
                        out=xnt2[i % 2][:, :],
                        in_=env["xn_dram"][i * 128:(i + 1) * 128, :])
                    p.inc(ins, "dmi")

            @block.tensor
            def _(t, vc_b=vc_b):
                p.drain(t, "tn")
                n = 0
                for i in range(NT):
                    for dh in range(2):
                        bk = n % 2
                        if n >= 2:
                            t.wait_ge(p.sems["vc"], vc_b + NVX * (n // 2) - 0)
                        ins = None
                        for cc in range(ND):
                            ins = t.matmul(
                                ps_x[bk][:, :],
                                lhsT=att3T[:, cc * S + i * 128: cc * S + (i + 1) * 128],
                                rhs=wp2b[:, cc * D + dh * 512: cc * D + (dh + 1) * 512],
                                start=(cc == 0), stop=(cc == ND - 1))
                        p.inc(ins, "tn")
                        n += 1

            @block.vector
            def _(v, tn_b=tn_b, dmi_b=dmi_b, sc_b=sc_b, dmo_b=dmo_b):
                p.drain(v, "vc")
                for i in range(NT):
                    v.wait_ge(p.sems["dmi"], dmi_b + 16 * (i + 1))
                    if i >= 2:
                        v.wait_ge(p.sems["dmo"], dmo_b + 32 * (i - 1))
                    for dh in range(2):
                        v.wait_ge(p.sems["tn"], tn_b + 2 * i + dh + 1)
                        sl = slice(dh * 512, (dh + 1) * 512)
                        ins = v.tensor_add(x2it[i % 2][:, sl], ps_x[bk_ := (2 * i + dh) % 2][:, :],
                                           xnt2[i % 2][:, sl])
                        p.inc(ins, "vc")
                    v.wait_ge(p.sems["vc"], p.cnt["vc"])
                    ins = v.reduce_sum(s1c[:, :], x2it[i % 2][:, :], axis=AXX)
                    p.inc(ins, "vc")
                    v.wait_ge(p.sems["sc"], sc_b + 3 * i + 1)
                    ins = v.reduce_sum(s2c[:, :], sqx[:, :], axis=AXX)
                    p.inc(ins, "vc")
                    v.wait_ge(p.sems["vc"], p.cnt["vc"])
                    ins = v.tensor_scalar_mul(mcx[:, :], s1c[:, :], 1.0 / D)
                    p.inc(ins, "vc")
                    v.wait_ge(p.sems["vc"], p.cnt["vc"])
                    ins = v.tensor_scalar_mul(msx[:, :], mcx[:, :], mcx[:, :])
                    p.inc(ins, "vc")
                    v.wait_ge(p.sems["vc"], p.cnt["vc"])
                    ins = v.tensor_scalar(vrx[:, :], s2c[:, :], 1.0 / D, msx[:, :],
                                          ALU.mult, ALU.subtract)
                    p.inc(ins, "vc")
                    v.wait_ge(p.sems["sc"], sc_b + 3 * i + 3)
                    ins = v.tensor_scalar(h2t[i % 2][:, :], x2it[i % 2][:, :],
                                          mcx[:, :], rsx[:, :], ALU.subtract,
                                          ALU.mult)
                    p.inc(ins, "vc")

            @block.scalar
            def _(s, vc_b=vc_b):
                p.drain(s, "sc")
                for i in range(NT):
                    s.wait_ge(p.sems["vc"], vc_b + NVX * i + 2)
                    ins = s.activation(sqx[:, :], x2it[i % 2][:, :], AF.Square)
                    p.inc(ins, "sc")
                    s.wait_ge(p.sems["vc"], vc_b + NVX * i + 7)
                    ins = s.activation(lnx[:, :], vrx[:, :], AF.Ln,
                                       bias=ccol[:, 0:1])
                    p.inc(ins, "sc")
                    s.wait_ge(p.sems["sc"], p.cnt["sc"])
                    ins = s.activation(rsx[:, :], lnx[:, :], AF.Exp,
                                       scale=ccol[:, 1:2])
                    p.inc(ins, "sc")

            @block.vector
            def _(v):
                pass

            @block.gpsimd
            def _(g, vc_b=vc_b):
                p.drain(g, "go")
                for i in range(NT):
                    g.wait_ge(p.sems["vc"], vc_b + NVX * i + NVX)
                    ins = g.dma_start(out=env["x2i_dram"][i * 128:(i + 1) * 128, :],
                                      in_=x2it[i % 2][:, :])
                    p.inc(ins, "dmo")
                    ins = g.dma_start(out=env["h2pre_dram"][i * 128:(i + 1) * 128, :],
                                      in_=h2t[i % 2][:, :])
                    p.inc(ins, "dmo")
            p.barrier()

            h2T = p.sbuf("h2T", [128, ND * S], BF16, scx)
            wgb = p.sbuf("wg_b", [128, ND * 8], BF16, scx)
            iota8 = p.sbuf("iota8b", [128, 8], F32, scx)
            logits = p.sbuf("logits2", [128, 8], F32, scx)
            amax8 = p.sbuf("amax8b", [128, 8], F32, scx)
            aidx8 = p.sbuf("aidx8b", [128, 8], U32, scx)
            nmcol = p.sbuf("nmcol", [128, 1], F32, scx)
            expb = p.sbuf("expb", [128, 8], F32, scx)
            z8 = p.sbuf("z8", [128, 1], F32, scx)
            d2c = p.sbuf("d2c", [128, 1], F32, scx)
            e1f = p.sbuf("e1f", [128, NT], F32, scx)
            e2f = p.sbuf("e2f", [128, NT], F32, scx)
            g1m = p.sbuf("g1m", [128, NT], F32, scx)
            g2m = p.sbuf("g2m", [128, NT], F32, scx)
            oh1 = p.sbuf("oh1", [128, 8], F32, scx)
            oh2 = p.sbuf("oh2", [128, 8], F32, scx)
            cnt2 = p.sbuf("cnt2", [1, 8], F32, scx)
            rowe = p.sbuf("rowe", [NT, 128], F32, scx)
            ps_l2 = [p.psum(f"ps_l2{k}", [128, 8], F32, scx) for k in range(2)]
            ps_c2 = p.psum("ps_c2", [1, 8], F32, scx)
            ps_te = p.psum("ps_te", [NT, 128], F32, scx)

            ph_load_xnT(p, block, env["h2pre_dram"], h2T)
            ph_load_weight_cast(p, block, env["wg_d"], D, 8, wgb, "wg")

            @block.sync
            def _(sy):
                p.drain(sy, "sy")
                ins = sy.dma_start(out=iota8[:, :], in_=env["iota8_d"])
                p.inc(ins, "dmi")
            p.barrier()

            @block.vector
            def _(v):
                p.drain(v, "vc")
                for c in range(ND):
                    sl = slice(c * S, (c + 1) * S)
                    ins = v.tensor_scalar(h2T[:, sl], h2T[:, sl],
                                          g2col[:, c:c + 1], b2col[:, c:c + 1],
                                          ALU.mult, ALU.add)
                    p.inc(ins, "vc")
            p.barrier()

            for i in range(NT):
                tn_b2 = p.cnt["tn"]

                @block.tensor
                def _(t, i=i):
                    p.drain(t, "tn") if i == 0 else None
                    ins = None
                    for cc in range(ND):
                        ins = t.matmul(
                            ps_l2[i % 2][:, :],
                            lhsT=h2T[:, cc * S + i * 128: cc * S + (i + 1) * 128],
                            rhs=wgb[:, cc * 8:(cc + 1) * 8],
                            start=(cc == 0), stop=(cc == ND - 1))
                    p.inc(ins, "tn")

                @block.vector
                def _(v, i=i, tn_b2=tn_b2):
                    p.drain(v, "vc") if i == 0 else None
                    v.wait_ge(p.sems["tn"], tn_b2 + 1)
                    ins = v.tensor_copy(logits[:, :], ps_l2[i % 2][:, :])
                    p.inc(ins, "vc")
                    v.wait_ge(p.sems["vc"], p.cnt["vc"])
                    ins = v.max(amax8[:, :], logits[:, :])
                    p.inc(ins, "vc")
                    v.wait_ge(p.sems["vc"], p.cnt["vc"])
                    ins = v.max_index(aidx8[:, :], amax8[:, :], logits[:, :])
                    p.inc(ins, "vc")
                    ins = v.tensor_scalar_mul(nmcol[:, :], amax8[:, 0:1], -1.0)
                    p.inc(ins, "vc")

                @block.scalar
                def _(s, i=i):
                    p.drain(s, "sc") if i == 0 else None
                    p.wait(s, "vc")
                    ins = s.activation(expb[:, :], logits[:, :], AF.Exp,
                                       bias=nmcol[:, :], accum_out=z8[:, :])
                    p.inc(ins, "sc")

                @block.vector
                def _(v, i=i):
                    p.wait(v, "sc")
                    ins = v.reciprocal(g1m[:, i:i + 1], z8[:, :])
                    p.inc(ins, "vc")
                    ins = v.tensor_scalar(d2c[:, :], amax8[:, 1:2], amax8[:, 0:1],
                                          None, ALU.subtract)
                    p.inc(ins, "vc")

                @block.scalar
                def _(s, i=i):
                    p.wait(s, "vc")
                    ins = s.activation(d2c[:, :], d2c[:, :], AF.Exp)
                    p.inc(ins, "sc")

                @block.vector
                def _(v, i=i):
                    p.wait(v, "sc")
                    ins = v.tensor_mul(g2m[:, i:i + 1], d2c[:, :], g1m[:, i:i + 1])
                    p.inc(ins, "vc")
                    ins = v.tensor_copy(e1f[:, i:i + 1], aidx8[:, 0:1])
                    p.inc(ins, "vc")
                    ins = v.tensor_copy(e2f[:, i:i + 1], aidx8[:, 1:2])
                    p.inc(ins, "vc")
                    v.wait_ge(p.sems["vc"], p.cnt["vc"])
                    ins = v.tensor_scalar(oh1[:, :], iota8[:, :], e1f[:, i:i + 1],
                                          None, ALU.is_equal)
                    p.inc(ins, "vc")
                    ins = v.tensor_scalar(oh2[:, :], iota8[:, :], e2f[:, i:i + 1],
                                          None, ALU.is_equal)
                    p.inc(ins, "vc")
                    v.wait_ge(p.sems["vc"], p.cnt["vc"])
                    ins = v.tensor_add(oh1[:, :], oh1[:, :], oh2[:, :])
                    p.inc(ins, "vc")

                @block.tensor
                def _(t, i=i):
                    p.wait(t, "vc")
                    ins = t.matmul(ps_c2[:, :], lhsT=ccol[:, 3:4], rhs=oh1[:, :],
                                   start=(i == 0), stop=(i == NT - 1))
                    p.inc(ins, "tn")
                p.barrier()

            @block.vector
            def _(v):
                p.drain(v, "vc")
                ins = v.tensor_copy(cnt2[:, :], ps_c2[:, :])
                p.inc(ins, "vc")
            p.barrier()

            for src, dstd in ((e1f, env["e1r_dram"]), (e2f, env["e2r_dram"])):
                @block.tensor
                def _(t, src=src):
                    p.drain(t, "tn")
                    ins = t.transpose(ps_te[:, :], src[:, :], ident[:, :])
                    p.inc(ins, "tn")

                @block.vector
                def _(v):
                    p.drain(v, "vc")
                    p.wait(v, "tn")
                    ins = v.tensor_copy(rowe[:, :], ps_te[:, :])
                    p.inc(ins, "vc")

                @block.gpsimd
                def _(g, dstd=dstd):
                    p.drain(g, "go")
                    p.wait(g, "vc")
                    ins = g.dma_start(out=dstd, in_=rowe[:, :])
                    p.inc(ins, "dmo")
                p.barrier()

            @block.gpsimd
            def _(g):
                p.drain(g, "go")
                ins = g.dma_start(out=env["counts_dram"], in_=cnt2[:, :])
                p.inc(ins, "dmo")
                ins = g.dma_start(out=env["g1_dram"], in_=g1m[:, :])
                p.inc(ins, "dmo")
                ins = g.dma_start(out=env["g2_dram"], in_=g2m[:, :])
                p.inc(ins, "dmo")
                g.wait_ge(p.sems["dmo"], p.cnt["dmo"])
    return p




def run_l2a(inputs, x2):
    if "l2a" not in _CACHE:
        _CACHE["l2a"] = build_l2a()
    p = _CACHE["l2a"]
    WqkvT = np.ascontiguousarray(np.asarray(inputs["L2_attn_Wqkv"], np.float32).T)
    Wqkv2 = np.asarray(inputs["L2_ffn_Wqkv"], np.float32)
    cols_q = np.concatenate([np.arange(h * 192, h * 192 + 64) for h in range(16)])
    cols_k = cols_q + 64
    cols_v = cols_q + 128
    base = {
        "Wql": WqkvT[:, 0:1024], "Wkl": WqkvT[:, 1024:2048],
        "Wvl": WqkvT[:, 2048:3072],
        "Wpl": np.asarray(inputs["L2_attn_Wproj"], np.float32).T,
        "bpl": np.asarray(inputs["L2_attn_bproj"], np.float32).reshape(1, D),
        "ln1g": np.asarray(inputs["L2_ffn_ln1_g"], np.float32).reshape(ND, 128).T,
        "ln1b": np.asarray(inputs["L2_ffn_ln1_b"], np.float32).reshape(ND, 128).T,
        "ln2g": np.asarray(inputs["L2_ffn_ln2_g"], np.float32).reshape(ND, 128).T,
        "ln2b": np.asarray(inputs["L2_ffn_ln2_b"], np.float32).reshape(ND, 128).T,
        "Wq2": Wqkv2[:, cols_q], "Wk2": Wqkv2[:, cols_k], "Wv2": Wqkv2[:, cols_v],
        "Wp2": inputs["L2_ffn_Wproj"],
        "Wg": inputs["L2_ffn_Wgate"],
    }
    base = {k: np.ascontiguousarray(np.asarray(v, np.float32))
            for k, v in base.items()}
    base.update(l2_consts())
    in_maps = []
    for c in range(NCORES):
        m = dict(base)
        m["x"] = np.ascontiguousarray(np.asarray(x2[c], np.float32))
        in_maps.append(m)
    res = run_bass_kernel_spmd(p.nc, in_maps, core_ids=list(range(NCORES)))
    return res.results




def l2b_consts(core_id):
    pm = np.zeros((8, 1), np.float32)
    pm[:core_id, 0] = 1.0
    t = np.arange(S, dtype=np.float32)
    return {
        "c_prevmask": pm,
        "c_iota8col": np.arange(8, dtype=np.float32).reshape(8, 1),
        "c_ones8col": np.ones((8, 1), np.float32),
        "c_ones1x8": np.ones((1, 8), np.float32),
        "c_bigiota": (t + float(BIGIDX)).reshape(1, S),
        "c_bigiota2": (t + float(BIGIDX) + 4096.0).reshape(1, S),
        "c_ident": np.eye(128, dtype=np.float32),
        "c_ones1x128": np.ones((1, 128), np.float32),
    }


def build_l2b():
    p = Prog()
    nc = p.nc
    x_dram = p.dram("x", [S, D], F32, kind="ExternalInput")
    s2_dram = p.dram("s2", [S, D], F32, kind="ExternalInput")
    x2i_dram = p.dram("x2i", [S, D], F32, kind="ExternalInput")
    h2_dram = p.dram("h2", [S, D], BF16, kind="ExternalInput")
    e1r_d = p.dram("e1row", [NT, 128], F32, kind="ExternalInput")
    e2r_d = p.dram("e2row", [NT, 128], F32, kind="ExternalInput")
    g1_d = p.dram("g1tm", [128, NT], F32, kind="ExternalInput")
    g2_d = p.dram("g2tm", [128, NT], F32, kind="ExternalInput")
    counts_d = p.dram("countsall", [8, 8], F32, kind="ExternalInput")
    ln2g_d = p.dram("ln2g", [128, ND], F32, kind="ExternalInput")
    ln2b_d = p.dram("ln2b", [128, ND], F32, kind="ExternalInput")
    w1_d = p.dram("W1L", [8 * D, 4096], F32, kind="ExternalInput")
    b1_d = p.dram("b1L", [8 * 128, 32], F32, kind="ExternalInput")
    w2_d = p.dram("W2L", [8 * 4096, D], F32, kind="ExternalInput")
    b2_d = p.dram("b2L", [8, D], F32, kind="ExternalInput")
    pm_d = p.dram("c_prevmask", [8, 1], F32, kind="ExternalInput")
    i8c_d = p.dram("c_iota8col", [8, 1], F32, kind="ExternalInput")
    o8c_d = p.dram("c_ones8col", [8, 1], F32, kind="ExternalInput")
    o18_d = p.dram("c_ones1x8", [1, 8], F32, kind="ExternalInput")
    bigi_d = p.dram("c_bigiota", [1, S], F32, kind="ExternalInput")
    bigi2_d = p.dram("c_bigiota2", [1, S], F32, kind="ExternalInput")
    ident_d = p.dram("c_ident", [128, 128], F32, kind="ExternalInput")
    ones_d = p.dram("c_ones1x128", [1, 128], F32, kind="ExternalInput")

    x3_dram = p.dram("x3", [S, D], F32, kind="ExternalOutput")
    moe_in = p.dram("moe_in", [8 * L2_CLOC, D], BF16)
    moe_out = p.dram("moe_out", [8 * L2_CLOC, D], BF16)

    sidx_tm = [p.sbuf(f"sidx_tm{k}", [128, NT], U32) for k in range(2)]
    ones128 = p.sbuf("ones128", [1, 128], F32)
    ident = p.sbuf("identb", [128, 128], F32)
    g2col = p.sbuf("g2colb", [128, ND], F32)
    b2colg = p.sbuf("b2colg", [128, ND], F32)

    with nc.Block() as block:
        with ExitStack() as scr:
            arow = [p.sbuf(f"arow{k}", [1, S], F32, scr) for k in range(2)]
            cnts = p.sbuf("cnts", [8, 8], F32, scr)
            pmcol = p.sbuf("pmcol", [8, 1], F32, scr)
            i8col = p.sbuf("i8col", [8, 1], F32, scr)
            o8col = p.sbuf("o8col", [8, 1], F32, scr)
            o18 = p.sbuf("o18", [1, 8], F32, scr)
            bigi = [p.sbuf(f"bigi{k}", [1, S], F32, scr) for k in range(2)]
            abc = p.sbuf("abc", [8, S], F32, scr)
            ohT = [p.sbuf(f"ohT{k}", [8, S], F32, scr) for k in range(2)]
            maskT = p.sbuf("maskT", [8, S], F32, scr)
            ones8t = p.sbuf("ones8t", [8, S], F32, scr)
            pos = p.sbuf("pos", [8, S], F32, scr)
            mp = p.sbuf("mp", [8, S], F32, scr)
            offs = p.sbuf("offs", [8, 1], F32, scr)
            pos_row = p.sbuf("pos_row", [1, S], F32, scr)
            offs_row = p.sbuf("offs_row", [1, S], F32, scr)
            keep = p.sbuf("keep", [1, S], F32, scr)
            keep2 = p.sbuf("keep2", [1, S], F32, scr)
            notk = p.sbuf("notk", [1, S], F32, scr)
            slot = p.sbuf("slot", [1, S], F32, scr)
            srow = p.sbuf("srow", [1, S], F32, scr)
            ps_a = [p.psum(f"ps_a{k}", [8, 512], F32, scr) for k in range(2)]
            ps_p = [p.psum(f"ps_p{k}", [1, 512], F32, scr) for k in range(2)]
            ps_o8 = p.psum("ps_o8", [8, 1], F32, scr)
            ps_t2 = p.psum("ps_t2", [128, 1], F32, scr)

            @block.sync
            def _(sy):
                p.drain(sy, "sy")
                ins = sy.dma_start(out=arow[0][:, :],
                                   in_=e1r_d.rearrange("i p -> (i p)"))
                p.inc(ins, "dmi")
                ins = sy.dma_start(out=arow[1][:, :],
                                   in_=e2r_d.rearrange("i p -> (i p)"))
                p.inc(ins, "dmi")
                for dst, src in ((cnts, counts_d), (pmcol, pm_d), (i8col, i8c_d),
                                 (o8col, o8c_d), (o18, o18_d),
                                 (bigi[0], bigi_d), (bigi[1], bigi2_d),
                                 (ones128, ones_d), (ident, ident_d),
                                 (g2col, ln2g_d), (b2colg, ln2b_d)):
                    ins = sy.dma_start(out=dst[:, :], in_=src)
                    p.inc(ins, "dmi")
            p.barrier()

            for k2 in range(2):
                vc_b0 = p.cnt["vc"]
                tn_b = p.cnt["tn"]

                @block.tensor
                def _(t, k2=k2, vc_b0=vc_b0):
                    p.drain(t, "tn")
                    for j in range(4):
                        if j >= 2:
                            t.wait_ge(p.sems["vc"], vc_b0 + j - 1)
                        ins = t.matmul(ps_a[j % 2][:, :], lhsT=o18[:, :],
                                       rhs=arow[k2][:, j * 512:(j + 1) * 512],
                                       start=True, stop=True)
                        p.inc(ins, "tn")

                @block.vector
                def _(v, k2=k2, tn_b=tn_b):
                    p.drain(v, "vc")
                    for j in range(4):
                        v.wait_ge(p.sems["tn"], tn_b + j + 1)
                        ins = v.tensor_copy(abc[:, j * 512:(j + 1) * 512],
                                            ps_a[j % 2][:, :])
                        p.inc(ins, "vc")
                    v.wait_ge(p.sems["vc"], p.cnt["vc"])
                    ins = v.tensor_scalar(ohT[k2][:, :], abc[:, :], i8col[:, :],
                                          None, ALU.is_equal)
                    p.inc(ins, "vc")
                p.barrier()

            @block.vector
            def _(v):
                p.drain(v, "vc")
                ins = v.tensor_add(maskT[:, :], ohT[0][:, :], ohT[1][:, :])
                p.inc(ins, "vc")
                ins = v.memset(ones8t[:, :], 1.0)
                p.inc(ins, "vc")
                v.wait_ge(p.sems["vc"], p.cnt["vc"])
                ins = v.tensor_tensor_scan(pos[:, :], ones8t[:, :], maskT[:, :],
                                           0.0, ALU.mult, ALU.add)
                p.inc(ins, "vc")
            p.barrier()

            @block.tensor
            def _(t):
                p.drain(t, "tn")
                ins = t.matmul(ps_o8[:, :], lhsT=cnts[:, :], rhs=pmcol[:, :],
                               start=True, stop=True)
                p.inc(ins, "tn")

            @block.vector
            def _(v):
                p.drain(v, "vc")
                p.wait(v, "tn")
                ins = v.tensor_copy(offs[:, :], ps_o8[:, :])
                p.inc(ins, "vc")
            p.barrier()

            for k2 in range(2):
                @block.vector
                def _(v, k2=k2):
                    p.drain(v, "vc")
                    ins = v.tensor_mul(mp[:, :], ohT[k2][:, :], pos[:, :])
                    p.inc(ins, "vc")
                    ins = v.tensor_scalar_mul(abc[:, :], ohT[k2][:, :],
                                              offs[:, :])
                    p.inc(ins, "vc")
                p.barrier()

                tn_b, vc_b = p.cnt["tn"], p.cnt["vc"]

                @block.tensor
                def _(t, vc_b=vc_b):
                    p.drain(t, "tn")
                    for n in range(8):
                        if n >= 2:
                            t.wait_ge(p.sems["vc"], vc_b + n - 1)
                        src = mp if n < 4 else abc
                        j = n % 4
                        ins = t.matmul(ps_p[n % 2][:, :], lhsT=o8col[:, :],
                                       rhs=src[:, j * 512:(j + 1) * 512],
                                       start=True, stop=True)
                        p.inc(ins, "tn")

                @block.vector
                def _(v, tn_b=tn_b):
                    p.drain(v, "vc")
                    for n in range(8):
                        v.wait_ge(p.sems["tn"], tn_b + n + 1)
                        dst = pos_row if n < 4 else offs_row
                        j = n % 4
                        ins = v.tensor_copy(dst[:, j * 512:(j + 1) * 512],
                                            ps_p[n % 2][:, :])
                        p.inc(ins, "vc")
                p.barrier()

                @block.vector
                def _(v, k2=k2):
                    p.drain(v, "vc")
                    ins = v.tensor_scalar(pos_row[:, :], pos_row[:, :], -1.0,
                                          None, ALU.add)
                    p.inc(ins, "vc")
                    v.wait_ge(p.sems["vc"], p.cnt["vc"])
                    ins = v.tensor_add(keep2[:, :], pos_row[:, :], offs_row[:, :])
                    p.inc(ins, "vc")
                    v.wait_ge(p.sems["vc"], p.cnt["vc"])
                    ins = v.tensor_scalar(keep2[:, :], keep2[:, :], 4096.0, None,
                                          ALU.is_lt)
                    p.inc(ins, "vc")
                    ins = v.tensor_scalar(keep[:, :], pos_row[:, :],
                                          float(L2_CLOC), None, ALU.is_lt)
                    p.inc(ins, "vc")
                    v.wait_ge(p.sems["vc"], p.cnt["vc"])
                    ins = v.tensor_mul(keep[:, :], keep[:, :], keep2[:, :])
                    p.inc(ins, "vc")
                    ins = v.tensor_scalar_mul(slot[:, :], arow[k2][:, :],
                                              float(L2_CLOC))
                    p.inc(ins, "vc")
                    v.wait_ge(p.sems["vc"], p.cnt["vc"])
                    ins = v.tensor_add(slot[:, :], slot[:, :], pos_row[:, :])
                    p.inc(ins, "vc")
                    ins = v.tensor_scalar(notk[:, :], keep[:, :], -1.0, 1.0,
                                          ALU.mult, ALU.add)
                    p.inc(ins, "vc")
                    v.wait_ge(p.sems["vc"], p.cnt["vc"])
                    ins = v.tensor_mul(srow[:, :], slot[:, :], keep[:, :])
                    p.inc(ins, "vc")
                    ins = v.tensor_mul(notk[:, :], notk[:, :], bigi[k2][:, :])
                    p.inc(ins, "vc")
                    v.wait_ge(p.sems["vc"], p.cnt["vc"])
                    ins = v.tensor_add(srow[:, :], srow[:, :], notk[:, :])
                    p.inc(ins, "vc")
                p.barrier()

                tn_b, vc_b = p.cnt["tn"], p.cnt["vc"]

                @block.tensor
                def _(t, vc_b=vc_b):
                    p.drain(t, "tn")
                    for i in range(NT):
                        if i >= 1:
                            t.wait_ge(p.sems["vc"], vc_b + i)
                        ins = t.transpose(ps_t2[:, :],
                                          srow[0:1, i * 128:(i + 1) * 128],
                                          ident[0:1, 0:1])
                        p.inc(ins, "tn")

                @block.vector
                def _(v, k2=k2, tn_b=tn_b):
                    p.drain(v, "vc")
                    for i in range(NT):
                        v.wait_ge(p.sems["tn"], tn_b + i + 1)
                        ins = v.tensor_copy(sidx_tm[k2][:, i:i + 1],
                                            ps_t2[:, 0:1])
                        p.inc(ins, "vc")
                p.barrier()

        breg = nc.gpsimd.alloc_register("bndreg")

        @block.gpsimd
        def _(g):
            g.reg_mov(breg, 8 * L2_CLOC - 1)

        with ExitStack() as scs:
            xnt = [p.sbuf(f"xnt{k}", [128, D], BF16, scs) for k in range(2)]
            dmi_b, dmg_b = p.cnt["dmi"], p.cnt["dmg"]

            @block.sync
            def _(sy):
                p.drain(sy, "sy")
                for i in range(NT):
                    if i >= 2:
                        sy.wait_ge(p.sems["dmg"], dmg_b + 32 * (i - 1))
                    ins = sy.dma_start(out=xnt[i % 2][:, :],
                                       in_=h2_dram[i * 128:(i + 1) * 128, :])
                    p.inc(ins, "dmi")

            @block.gpsimd
            def _(g):
                p.drain(g, "go")
                for i in range(NT):
                    g.wait_ge(p.sems["dmi"], dmi_b + 16 * (i + 1))
                    for k2 in range(2):
                        ins = g.indirect_dma_start(
                            out=moe_in, out_offset=bass.IndirectOffsetOnAxis(
                                sidx_tm[k2][:, i:i + 1], 0),
                            in_=xnt[i % 2][:, :], in_offset=None,
                            bounds_check=breg, oob_is_err=False)
                        p.inc(ins, "dmg")
            p.barrier()

        ph_moe_experts(p, block, moe_in, moe_out, w1_d, b1_d, w2_d, b2_d,
                       ones128, L2_CLOC, gain_col=g2col, gbias_col=b2colg)
        p.barrier()

        with ExitStack() as scg:
            b1t = p.sbuf("b1t", [128, D], BF16, scg)
            b2t = p.sbuf("b2t", [128, D], BF16, scg)
            g1m = p.sbuf("g1mb", [128, NT], F32, scg)
            g2m = p.sbuf("g2mb", [128, NT], F32, scg)
            xt = [p.sbuf(f"xt{k}", [128, D], F32, scg) for k in range(2)]
            x2it = [p.sbuf(f"x2it{k}", [128, D], F32, scg) for k in range(2)]
            s2t = [p.sbuf(f"s2t{k}", [128, D], F32, scg) for k in range(2)]
            moet = p.sbuf("moet", [128, D], F32, scg)
            moe2 = p.sbuf("moe2", [128, D], F32, scg)
            gsil = [p.sbuf(f"gsil{k}", [128, D], F32, scg) for k in range(2)]

            @block.sync
            def _(sy):
                p.drain(sy, "sy")
                ins = sy.dma_start(out=g1m[:, :], in_=g1_d)
                p.inc(ins, "dmi")
                ins = sy.dma_start(out=g2m[:, :], in_=g2_d)
                p.inc(ins, "dmi")
            p.barrier()

            dmi_b, dmg_b, sc_b, vc_b, dmo_b, go_b = (
                p.cnt[s] for s in ("dmi", "dmg", "sc", "vc", "dmo", "go"))

            @block.sync
            def _(sy):
                p.drain(sy, "sy")
                for i in range(NT):
                    if i >= 2:
                        sy.wait_ge(p.sems["vc"], vc_b + 6 * (i - 1))
                        sy.wait_ge(p.sems["dmo"], dmo_b + 16 * (i - 1))
                    ins = sy.dma_start(out=xt[i % 2][:, :],
                                       in_=x_dram[i * 128:(i + 1) * 128, :])
                    p.inc(ins, "dmi")
                    ins = sy.dma_start(out=x2it[i % 2][:, :],
                                       in_=x2i_dram[i * 128:(i + 1) * 128, :])
                    p.inc(ins, "dmi")
                    ins = sy.dma_start(out=s2t[i % 2][:, :],
                                       in_=s2_dram[i * 128:(i + 1) * 128, :])
                    p.inc(ins, "dmi")

            @block.gpsimd
            def _(g, vc_b=vc_b):
                p.drain(g, "go")
                def store(k):
                    g.wait_ge(p.sems["vc"], vc_b + 6 * (k + 1))
                    ins = g.dma_start(out=x3_dram[k * 128:(k + 1) * 128, :],
                                      in_=xt[k % 2][:, :])
                    p.inc(ins, "dmo")
                for i in range(NT):
                    if i >= 2:
                        store(i - 2)
                    if i >= 1:
                        g.wait_ge(p.sems["vc"], vc_b + 6 * (i - 1) + 3)
                    ins = g.memset(b1t[:, :], 0.0)
                    p.inc(ins, "go")
                    ins = g.memset(b2t[:, :], 0.0)
                    p.inc(ins, "go")
                    g.wait_ge(p.sems["go"], p.cnt["go"])
                    ins = g.indirect_dma_start(
                        out=b1t[:, :], out_offset=None,
                        in_=moe_out, in_offset=bass.IndirectOffsetOnAxis(
                            sidx_tm[0][:, i:i + 1], 0),
                        bounds_check=breg, oob_is_err=False)
                    p.inc(ins, "dmg")
                    ins = g.indirect_dma_start(
                        out=b2t[:, :], out_offset=None,
                        in_=moe_out, in_offset=bass.IndirectOffsetOnAxis(
                            sidx_tm[1][:, i:i + 1], 0),
                        bounds_check=breg, oob_is_err=False)
                    p.inc(ins, "dmg")
                store(NT - 2)
                store(NT - 1)
                g.wait_ge(p.sems["dmo"], p.cnt["dmo"])
                g.wait_ge(p.sems["dmg"], p.cnt["dmg"])

            @block.vector
            def _(v, dmg_b=dmg_b, dmi_b=dmi_b, sc_b=sc_b):
                p.drain(v, "vc")
                for i in range(NT):
                    v.wait_ge(p.sems["dmg"], dmg_b + 32 * (i + 1))
                    v.wait_ge(p.sems["dmi"], dmi_b + 48 * (i + 1))
                    ins = v.tensor_scalar_mul(moet[:, :], b1t[:, :],
                                              g1m[:, i:i + 1])
                    p.inc(ins, "vc")
                    ins = v.tensor_scalar_mul(moe2[:, :], b2t[:, :],
                                              g2m[:, i:i + 1])
                    p.inc(ins, "vc")
                    v.wait_ge(p.sems["vc"], p.cnt["vc"])
                    ins = v.tensor_add(moet[:, :], moet[:, :], moe2[:, :])
                    p.inc(ins, "vc")
                    v.wait_ge(p.sems["vc"], p.cnt["vc"])
                    ins = v.tensor_add(gsil[i % 2][:, :], moet[:, :],
                                       x2it[i % 2][:, :])
                    p.inc(ins, "vc")
                    v.wait_ge(p.sems["sc"], sc_b + i + 1)
                    ins = v.tensor_mul(gsil[i % 2][:, :], gsil[i % 2][:, :],
                                       s2t[i % 2][:, :])
                    p.inc(ins, "vc")
                    v.wait_ge(p.sems["vc"], p.cnt["vc"])
                    ins = v.tensor_add(xt[i % 2][:, :], xt[i % 2][:, :],
                                       gsil[i % 2][:, :])
                    p.inc(ins, "vc")

            @block.scalar
            def _(s, vc_b=vc_b):
                p.drain(s, "sc")
                for i in range(NT):
                    s.wait_ge(p.sems["vc"], vc_b + 6 * i + 4)
                    ins = s.activation(gsil[i % 2][:, :], gsil[i % 2][:, :],
                                       AF.Silu)
                    p.inc(ins, "sc")
    p.ctx.close()
    return p




def run_l2b(inputs, x2, l2a_results):
    if "l2b" not in _CACHE:
        _CACHE["l2b"] = build_l2b()
    p = _CACHE["l2b"]
    counts_all = np.concatenate([l2a_results[c]["counts2"] for c in range(NCORES)],
                                axis=0)
    W1 = np.asarray(inputs["L2_ffn_W1"], np.float32)
    W2 = np.asarray(inputs["L2_ffn_W2"], np.float32)
    b1 = np.asarray(inputs["L2_ffn_b1"], np.float32)
    base = {
        "W1L": W1.reshape(8 * D, 4096), "W2L": W2.reshape(8 * 4096, D),
        "b1L": b1.reshape(8, 32, 128).transpose(0, 2, 1).reshape(8 * 128, 32),
        "b2L": np.asarray(inputs["L2_ffn_b2"], np.float32),
        "countsall": counts_all,
        "ln2g": np.asarray(inputs["L2_ffn_ln2_g"], np.float32).reshape(ND, 128).T,
        "ln2b": np.asarray(inputs["L2_ffn_ln2_b"], np.float32).reshape(ND, 128).T,
    }
    base = {k: np.ascontiguousarray(np.asarray(v, np.float32))
            for k, v in base.items()}
    in_maps = []
    for c in range(NCORES):
        m = dict(base)
        m.update(l2b_consts(c))
        m["x"] = np.ascontiguousarray(np.asarray(x2[c], np.float32))
        for key in ("s2", "x2i", "h2", "e1row", "e2row", "g1tm", "g2tm"):
            m[key] = np.ascontiguousarray(l2a_results[c][key])
        in_maps.append(m)
    res = run_bass_kernel_spmd(p.nc, in_maps, core_ids=list(range(NCORES)))
    return np.stack([res.results[c]["x3"] for c in range(NCORES)])



def l3_consts():
    cc = np.zeros((128, 4), np.float32)
    cc[:, 0] = 1e-5
    cc[:, 1] = -0.5
    cc[:, 2] = 1e-6
    cc[:, 3] = 1.0
    return {"c_ccol4": cc, "c_ident": np.eye(128, dtype=np.float32)}


def build_l3(debug=False):
    p = Prog()
    nc = p.nc
    x_dram = p.dram("x", [S, D], F32, kind="ExternalInput")
    w1t_d = p.dram("w1T", [D, 4096], F32, kind="ExternalInput")
    w2t_d = p.dram("w2T", [D, 4096], F32, kind="ExternalInput")
    h1t_d = p.dram("h1T", [4096, 4096], F32, kind="ExternalInput")
    h2t_d = p.dram("h2T", [4096, 4096], F32, kind="ExternalInput")
    w3t_d = p.dram("w3T", [4096, D], F32, kind="ExternalInput")
    w1sc_d = p.dram("w1sc", [128, 32], F32, kind="ExternalInput")
    w2sc_d = p.dram("w2sc", [128, 32], F32, kind="ExternalInput")
    h1sc_d = p.dram("h1sc", [128, 32], F32, kind="ExternalInput")
    h2sc_d = p.dram("h2sc", [128, 32], F32, kind="ExternalInput")
    s3r_d = p.dram("s3row", [1, D], F32, kind="ExternalInput")
    wint_d = p.dram("winT", [D, 4096], F32, kind="ExternalInput")
    wxt_d = p.dram("wxT", [2048, 33], F32, kind="ExternalInput")
    woutt_d = p.dram("woutT", [2048, D], F32, kind="ExternalInput")
    cw_d = p.dram("cw", [128, 64], F32, kind="ExternalInput")
    cb_d = p.dram("cb", [128, 16], F32, kind="ExternalInput")
    wdt_d = p.dram("wdt", [128, 16], F32, kind="ExternalInput")
    bdt_d = p.dram("bdt", [128, 16], F32, kind="ExternalInput")
    alog_d = p.dram("alog", [128, 256], F32, kind="ExternalInput")
    dsc_d = p.dram("dsc", [128, 16], F32, kind="ExternalInput")
    g3_d = p.dram("g3", [128, 8], F32, kind="ExternalInput")
    ccol_d = p.dram("c_ccol4", [128, 4], F32, kind="ExternalInput")
    ident_d = p.dram("c_ident", [128, 128], F32, kind="ExternalInput")

    x4_dram = p.dram("x4", [S, D], F32, kind="ExternalOutput")
    dk = "ExternalOutput" if debug else "Internal"
    xn_dram = p.dram("xn3_scr", [S, D], BF16, kind=dk)
    g_dram = p.dram("gg3_scr", [S, D], BF16, kind=dk)
    z_dram = p.dram("z3_scr", [2048, S], BF16, kind=dk)
    yv_dram = p.dram("yv3_scr", [2048, S], BF16, kind=dk)
    xc_dram = p.dram("xc3_scr", [2048, S], BF16, kind=dk)
    pp_dram = p.dram("pp3_scr", [33, S], F32, kind=dk)
    nr_dram = p.dram("nr3_scr", [NT, 128], F32)

    ccol = p.sbuf("t3ccol", [128, 4], F32)
    ident = p.sbuf("t3ident", [128, 128], F32)
    ones1 = p.sbuf("t3ones1", [1, 128], F32)
    varc = p.sbuf("t3varc", [128, NT], F32)
    cwc = p.sbuf("t3cwc", [128, 64], F32)
    cbc = p.sbuf("t3cbc", [128, 16], F32)
    wdtc = p.sbuf("t3wdtc", [128, 16], F32)
    bdtc = p.sbuf("t3bdtc", [128, 16], F32)
    dcol = p.sbuf("t3dcol", [128, 16], F32)
    g3c = p.sbuf("t3g3c", [128, 8], F32)
    aneg = p.sbuf("t3aneg", [128, 256], F32)
    w1sc = p.sbuf("t3w1sc", [128, 32], F32)
    w2sc = p.sbuf("t3w2sc", [128, 32], F32)
    h1sc = p.sbuf("t3h1sc", [128, 32], F32)
    h2sc = p.sbuf("t3h2sc", [128, 32], F32)
    s3bc = p.sbuf("t3s3bc", [128, D], F32)
    s3row = p.sbuf("t3s3row", [1, D], F32)

    with nc.Block() as block:
        @block.sync
        def _(sy):
            for dst, src in ((ccol, ccol_d), (ident, ident_d), (cwc, cw_d),
                             (cbc, cb_d), (wdtc, wdt_d), (bdtc, bdt_d),
                             (dcol, dsc_d), (g3c, g3_d), (aneg, alog_d),
                             (w1sc, w1sc_d), (w2sc, w2sc_d), (h1sc, h1sc_d),
                             (h2sc, h2sc_d), (s3row, s3r_d)):
                ins = sy.dma_start(out=dst[:, :], in_=src)
                p.inc(ins, "dmi")
        ld0 = p.cnt["dmi"]

        @block.vector
        def _(v):
            ins = v.memset(ones1[:, :], 1.0)
            p.inc(ins, "vc")

        @block.scalar
        def _(s):
            s.wait_ge(p.sems["dmi"], ld0)
            ins = s.activation(aneg[:, :], aneg[:, :], AF.Exp)
            p.inc(ins, "sc")

        @block.vector
        def _(v):
            p.wait(v, "sc")
            ins = v.tensor_scalar_mul(aneg[:, :], aneg[:, :], -1.0)
            p.inc(ins, "vc")
        p.barrier()

        with ExitStack() as scb0:
            ps_s3 = p.psum("t3pss3", [128, 512], F32, scb0)
            for dh in range(2):
                @block.tensor
                def _(t, dh=dh):
                    p.drain(t, "tn")
                    if dh:
                        p.wait(t, "vc")
                    ins = t.matmul(ps_s3[:, :], lhsT=ones1[:, :],
                                   rhs=s3row[:, dh * 512:(dh + 1) * 512],
                                   start=True, stop=True)
                    p.inc(ins, "tn")

                @block.vector
                def _(v, dh=dh):
                    p.drain(v, "vc")
                    p.wait(v, "tn")
                    ins = v.tensor_copy(s3bc[:, dh * 512:(dh + 1) * 512],
                                        ps_s3[:, :])
                    p.inc(ins, "vc")
            p.barrier()

        with ExitStack() as sc0:
            ph_layernorm(p, block, x_dram, xn_dram, sc0, ccol[:, 0:1],
                         ccol[:, 1:2], keep_var=varc)
            p.barrier()

        with ExitStack() as scf:
            xnT = p.sbuf("t3xnT", [128, ND * S], BF16, scf)
            ph_load_xnT(p, block, xn_dram, xnT)
            p.barrier()
            hA = p.sbuf("t3hA", [128, 32 * 1024], BF16, scf)
            for tb in range(2):
                with ExitStack() as sca:
                    wb1 = [p.sbuf(f"t3wa1{tb}{k}", [128, 8 * 128], BF16, sca)
                           for k in range(2)]
                    wb2 = [p.sbuf(f"t3wa2{tb}{k}", [128, 8 * 128], BF16, sca)
                           for k in range(2)]
                    ps1 = [p.psum(f"t3psa1{tb}{k}", [128, 512], F32, sca)
                           for k in range(2)]
                    ps2 = [p.psum(f"t3psa2{tb}{k}", [128, 512], F32, sca)
                           for k in range(2)]
                    t1 = p.sbuf(f"t3t1{tb}", [128, 1024], F32, sca)
                    t2 = p.sbuf(f"t3t2{tb}", [128, 1024], F32, sca)
                    tn_done, dmg_done, sc_done, vc_done = {}, {}, {}, {}
                    tn_n = {}
                    for oc in range(32):
                        @block.gpsimd
                        def _(g, oc=oc):
                            p.drain(g, "go")
                            if oc >= 2:
                                g.wait_ge(p.sems["tn"], tn_done[oc - 2])
                            for wd, wb in ((w1t_d, wb1), (w2t_d, wb2)):
                                src = wd[:, oc * 128:(oc + 1) * 128]
                                ins = g.dma_start(
                                    out=wb[oc % 2][:, :].rearrange(
                                        "p (c m) -> p c m", c=8),
                                    in_=src.rearrange("(c p) m -> p c m",
                                                      p=128))
                                p.inc(ins, "dmg")
                        dmg_done[oc] = p.cnt["dmg"]

                        @block.tensor
                        def _(t, oc=oc):
                            p.drain(t, "tn")
                            t.wait_ge(p.sems["dmg"], dmg_done[oc])
                            if oc >= 1:
                                t.wait_ge(p.sems["vc"], vc_done[oc - 1])
                            tns = []
                            for ps, wb in ((ps1, wb1), (ps2, wb2)):
                                for n in range(2):
                                    ins = None
                                    for ic in range(ND):
                                        rh0 = ic * S + tb * 1024 + n * 512
                                        ins = t.matmul(
                                            ps[n][:, :],
                                            lhsT=wb[oc % 2][:, ic * 128:
                                                            (ic + 1) * 128],
                                            rhs=xnT[:, rh0:rh0 + 512],
                                            start=(ic == 0), stop=(ic == 7))
                                    p.inc(ins, "tn")
                                    tns.append(p.cnt["tn"])
                            tn_n[oc] = tns
                        tn_done[oc] = p.cnt["tn"]

                        @block.scalar
                        def _(s, oc=oc):
                            p.drain(s, "sc")
                            if oc >= 1:
                                s.wait_ge(p.sems["vc"], vc_done[oc - 1])
                            for n in range(2):
                                s.wait_ge(p.sems["tn"], tn_n[oc][n])
                                ins = s.activation(
                                    t1[:, n * 512:(n + 1) * 512],
                                    ps1[n][:, :], AF.Silu,
                                    scale=w1sc[:, oc:oc + 1])
                                p.inc(ins, "sc")
                        sc_done[oc] = p.cnt["sc"]

                        @block.vector
                        def _(v, oc=oc):
                            p.drain(v, "vc")
                            for n in range(2):
                                sl = slice(n * 512, (n + 1) * 512)
                                v.wait_ge(p.sems["tn"], tn_n[oc][2 + n])
                                ins = v.tensor_scalar_mul(
                                    t2[:, sl], ps2[n][:, :],
                                    w2sc[:, oc:oc + 1])
                                p.inc(ins, "vc")
                                v.wait_ge(p.sems["vc"], p.cnt["vc"])
                                v.wait_ge(p.sems["sc"], sc_done[oc])
                                ha0 = oc * 1024 + n * 512
                                ins = v.tensor_mul(hA[:, ha0:ha0 + 512],
                                                   t1[:, sl], t2[:, sl])
                                p.inc(ins, "vc")
                        vc_done[oc] = p.cnt["vc"]
                    p.barrier()

                def stream_bc(w_dram, src, dst, scol, tag, scope):
                    wb = [p.sbuf(f"t3wb{tag}{k}", [128, 32 * 128], BF16, scope)
                          for k in range(2)]
                    ps = [p.psum(f"t3ps{tag}{k}", [128, 512], F32, scope)
                          for k in range(2)]
                    tn_done, dmg_done, sc_done, tn_n = {}, {}, {}, {}
                    for oc in range(32):
                        @block.gpsimd
                        def _(g, oc=oc):
                            p.drain(g, "go")
                            if oc >= 2:
                                g.wait_ge(p.sems["tn"], tn_done[oc - 2])
                            src_sl = w_dram[:, oc * 128:(oc + 1) * 128]
                            ins = g.dma_start(
                                out=wb[oc % 2][:, :].rearrange(
                                    "p (c m) -> p c m", c=32),
                                in_=src_sl.rearrange("(c p) m -> p c m",
                                                     p=128))
                            p.inc(ins, "dmg")
                        dmg_done[oc] = p.cnt["dmg"]

                        @block.tensor
                        def _(t, oc=oc):
                            p.drain(t, "tn")
                            t.wait_ge(p.sems["dmg"], dmg_done[oc])
                            if oc >= 1:
                                t.wait_ge(p.sems["sc"], sc_done[oc - 1])
                            tns = []
                            for n in range(2):
                                ins = None
                                for ic in range(32):
                                    ins = t.matmul(
                                        ps[n][:, :],
                                        lhsT=wb[oc % 2][:, ic * 128:
                                                        (ic + 1) * 128],
                                        rhs=src[:, ic * 1024 + n * 512:
                                                ic * 1024 + (n + 1) * 512],
                                        start=(ic == 0), stop=(ic == 31))
                                p.inc(ins, "tn")
                                tns.append(p.cnt["tn"])
                            tn_n[oc] = tns
                        tn_done[oc] = p.cnt["tn"]

                        @block.scalar
                        def _(s, oc=oc):
                            p.drain(s, "sc")
                            for n in range(2):
                                s.wait_ge(p.sems["tn"], tn_n[oc][n])
                                ins = s.activation(
                                    dst[:, oc * 1024 + n * 512:
                                        oc * 1024 + (n + 1) * 512],
                                    ps[n][:, :], AF.Gelu,
                                    scale=scol[:, oc:oc + 1])
                                p.inc(ins, "sc")
                        sc_done[oc] = p.cnt["sc"]
                    p.barrier()

                with ExitStack() as scbc:
                    hB = p.sbuf(f"t3hB{tb}", [128, 32 * 1024], BF16, scbc)
                    with ExitStack() as scw1:
                        stream_bc(h1t_d, hA, hB, h1sc, f"h1{tb}", scw1)
                    with ExitStack() as scw2:
                        stream_bc(h2t_d, hB, hA, h2sc, f"h2{tb}", scw2)

                with ExitStack() as scd:
                    w3b = p.sbuf(f"t3w3b{tb}", [128, 32 * 1024], BF16, scd)
                    ph_load_weight_cast(p, block, w3t_d, 4096, 1024, w3b,
                                        f"w3l{tb}", col_block=256)
                    p.barrier()
                    psd = [p.psum(f"t3psd{tb}{k}", [128, 512], F32, scd)
                           for k in range(2)]
                    gt = [p.sbuf(f"t3gt{tb}{k}", [128, 1024], BF16, scd)
                          for k in range(2)]
                    gf = p.sbuf(f"t3gf{tb}", [128, 1024], F32, scd)
                    vcd, scd_done, dmo_done, tdn = {}, {}, {}, {}
                    for i in range(8):
                        @block.tensor
                        def _(t, i=i):
                            p.drain(t, "tn")
                            if i >= 1:
                                t.wait_ge(p.sems["vc"], vcd[i - 1])
                            tns = []
                            for n in range(2):
                                ins = None
                                for ic in range(32):
                                    ins = t.matmul(
                                        psd[n][:, :],
                                        lhsT=hA[:, ic * 1024 + i * 128:
                                                ic * 1024 + (i + 1) * 128],
                                        rhs=w3b[:, ic * 1024 + n * 512:
                                                ic * 1024 + (n + 1) * 512],
                                        start=(ic == 0), stop=(ic == 31))
                                p.inc(ins, "tn")
                                tns.append(p.cnt["tn"])
                            tdn[i] = tns

                        @block.vector
                        def _(v, i=i):
                            p.drain(v, "vc")
                            if i >= 1:
                                v.wait_ge(p.sems["sc"], scd_done[i - 1])
                            for n in range(2):
                                sl = slice(n * 512, (n + 1) * 512)
                                v.wait_ge(p.sems["tn"], tdn[i][n])
                                ins = v.tensor_mul(gf[:, sl], psd[n][:, :],
                                                   s3bc[:, sl])
                                p.inc(ins, "vc")
                        vcd[i] = p.cnt["vc"]

                        @block.scalar
                        def _(s, i=i):
                            p.drain(s, "sc")
                            s.wait_ge(p.sems["vc"], vcd[i])
                            if i >= 2:
                                s.wait_ge(p.sems["dmo"], dmo_done[i - 2])
                            ins = s.activation(gt[i % 2][:, :], gf[:, :],
                                               AF.Silu)
                            p.inc(ins, "sc")
                        scd_done[i] = p.cnt["sc"]

                        @block.gpsimd
                        def _(g, i=i):
                            p.drain(g, "go")
                            g.wait_ge(p.sems["sc"], scd_done[i])
                            r0 = tb * 1024 + i * 128
                            ins = g.dma_start(out=g_dram[r0:r0 + 128, :],
                                              in_=gt[i % 2][:, :])
                            p.inc(ins, "dmo")
                        dmo_done[i] = p.cnt["dmo"]
                    p.barrier()

        with ExitStack() as scs:
            xcT = p.sbuf("t3xcT", [128, 16 * S], BF16, scs)
            with ExitStack() as sci:
                xn2T = p.sbuf("t3xn2T", [128, ND * S], BF16, sci)
                with ExitStack() as scr:
                    tcol = p.sbuf("t3tcol", [128, NT], F32, scr)
                    rowbuf = p.sbuf("t3rowb", [NT, 128], F32, scr)
                    crow = p.sbuf("t3crow", [1, S], F32, scr)
                    rs_bc = p.sbuf("t3rsbc", [128, S], F32, scr)
                    xtr = [p.sbuf(f"t3xtr{k}", [128, S], BF16, scr)
                           for k in range(2)]
                    ps_ct = p.psum("t3psct", [NT, 128], F32, scr)
                    ps_cb = [p.psum(f"t3pscb{k}", [128, 512], F32, scr)
                             for k in range(2)]

                    @block.vector
                    def _(v):
                        p.drain(v, "vc")
                        ins = v.tensor_scalar(tcol[:, :], varc[:, :], 1e-5,
                                              None, ALU.add)
                        p.inc(ins, "vc")
                        v.wait_ge(p.sems["vc"], p.cnt["vc"])
                        ins = v.reciprocal(tcol[:, :], tcol[:, :])
                        p.inc(ins, "vc")
                        v.wait_ge(p.sems["vc"], p.cnt["vc"])
                        ins = v.tensor_mul(tcol[:, :], tcol[:, :], varc[:, :])
                        p.inc(ins, "vc")

                    @block.scalar
                    def _(s):
                        p.drain(s, "sc")
                        p.wait(s, "vc")
                        ins = s.activation(tcol[:, :], tcol[:, :], AF.Ln,
                                           bias=ccol[:, 2:3])
                        p.inc(ins, "sc")
                        s.wait_ge(p.sems["sc"], p.cnt["sc"])
                        ins = s.activation(tcol[:, :], tcol[:, :], AF.Exp,
                                           scale=ccol[:, 1:2])
                        p.inc(ins, "sc")

                    @block.tensor
                    def _(t):
                        p.drain(t, "tn")
                        p.wait(t, "sc")
                        ins = t.transpose(ps_ct[:, :], tcol[:, :], ident[:, :])
                        p.inc(ins, "tn")

                    @block.vector
                    def _(v):
                        p.drain(v, "vc")
                        p.wait(v, "tn")
                        ins = v.tensor_copy(rowbuf[:, :], ps_ct[:, :])
                        p.inc(ins, "vc")

                    @block.gpsimd
                    def _(g):
                        p.drain(g, "go")
                        p.wait(g, "vc")
                        ins = g.dma_start(out=nr_dram, in_=rowbuf[:, :])
                        p.inc(ins, "dmo")

                    @block.sync
                    def _(sy):
                        p.drain(sy, "sy")
                        p.wait(sy, "dmo")
                        ins = sy.dma_start(out=crow[:, :],
                                           in_=nr_dram.rearrange(
                                               "i p -> (i p)"))
                        p.inc(ins, "dmi")
                    p.barrier()

                    tn_b, vc_b = p.cnt["tn"], p.cnt["vc"]

                    @block.tensor
                    def _(t, vc_b=vc_b):
                        p.drain(t, "tn")
                        for j in range(4):
                            if j >= 2:
                                t.wait_ge(p.sems["vc"], vc_b + j - 1)
                            ins = t.matmul(ps_cb[j % 2][:, :],
                                           lhsT=ones1[:, :],
                                           rhs=crow[:, j * 512:(j + 1) * 512],
                                           start=True, stop=True)
                            p.inc(ins, "tn")

                    @block.vector
                    def _(v, tn_b=tn_b):
                        p.drain(v, "vc")
                        for j in range(4):
                            v.wait_ge(p.sems["tn"], tn_b + j + 1)
                            ins = v.tensor_copy(rs_bc[:, j * 512:(j + 1) * 512],
                                                ps_cb[j % 2][:, :])
                            p.inc(ins, "vc")
                    p.barrier()

                    dmi_b, vc_b = p.cnt["dmi"], p.cnt["vc"]

                    @block.sync
                    def _(sy, vc_b=vc_b):
                        p.drain(sy, "sy")
                        for c in range(ND):
                            if c >= 2:
                                sy.wait_ge(p.sems["vc"], vc_b + 2 * (c - 1))
                            ins = sy.dma_start_transpose(
                                xtr[c % 2][:, :],
                                xn_dram[:, c * 128:(c + 1) * 128])
                            p.inc(ins, "dmi")

                    @block.vector
                    def _(v, dmi_b=dmi_b):
                        p.drain(v, "vc")
                        for c in range(ND):
                            v.wait_ge(p.sems["dmi"], dmi_b + 16 * (c + 1))
                            sl = slice(c * S, (c + 1) * S)
                            ins = v.tensor_mul(xn2T[:, sl], xtr[c % 2][:, :],
                                               rs_bc[:, :])
                            p.inc(ins, "vc")
                            v.wait_ge(p.sems["vc"], p.cnt["vc"])
                            ins = v.tensor_scalar_mul(xn2T[:, sl], xn2T[:, sl],
                                                      g3c[:, c:c + 1])
                            p.inc(ins, "vc")
                    p.barrier()

                winb = p.sbuf("t3winb", [128, ND * 4096], BF16, sci)
                ph_load_weight_cast(p, block, wint_d, D, 4096, winb, "winl",
                                    col_block=1024)
                xsp = [p.sbuf(f"t3xsp{k}", [128, 3 + S], F32, sci)
                       for k in range(2)]
                ac = [p.sbuf(f"t3ac{k}", [128, S], F32, sci) for k in range(2)]
                zev = [p.sbuf(f"t3zev{k}", [128, 512], BF16, sci)
                       for k in range(2)]
                ps_w = [p.psum(f"t3psw{k}", [128, 512], F32, sci)
                        for k in range(2)]

                @block.vector
                def _(v):
                    p.drain(v, "vc")
                    for k in range(2):
                        ins = v.memset(xsp[k][:, 0:3], 0.0)
                        p.inc(ins, "vc")
                p.barrier()

                for c32 in range(32):
                    is_z = c32 >= 16
                    cc = c32 - 16 if is_z else c32
                    tn_b, vc_b, dmo_b = (p.cnt[s] for s in ("tn", "vc", "dmo"))

                    @block.tensor
                    def _(t, c32=c32, vc_b=vc_b):
                        p.drain(t, "tn")
                        for n in range(4):
                            if n >= 2:
                                t.wait_ge(p.sems["vc"], vc_b + n - 1)
                            ins = None
                            for ic in range(ND):
                                ins = t.matmul(
                                    ps_w[n % 2][:, :],
                                    lhsT=winb[:, ic * 4096 + c32 * 128:
                                              ic * 4096 + (c32 + 1) * 128],
                                    rhs=xn2T[:, ic * S + n * 512:
                                             ic * S + (n + 1) * 512],
                                    start=(ic == 0), stop=(ic == 7))
                            p.inc(ins, "tn")

                    if is_z:
                        @block.vector
                        def _(v, cc=cc, tn_b=tn_b, dmo_b=dmo_b):
                            p.drain(v, "vc")
                            for n in range(4):
                                v.wait_ge(p.sems["tn"], tn_b + n + 1)
                                if n >= 2:
                                    v.wait_ge(p.sems["dmo"],
                                              dmo_b + 16 * (n - 1))
                                ins = v.tensor_copy(zev[n % 2][:, :],
                                                    ps_w[n % 2][:, :])
                                p.inc(ins, "vc")

                        @block.gpsimd
                        def _(g, cc=cc, vc_b=vc_b):
                            p.drain(g, "go")
                            for n in range(4):
                                g.wait_ge(p.sems["vc"], vc_b + n + 1)
                                ins = g.dma_start(
                                    out=z_dram[cc * 128:(cc + 1) * 128,
                                               n * 512:(n + 1) * 512],
                                    in_=zev[n % 2][:, :])
                                p.inc(ins, "dmo")
                    else:
                        @block.vector
                        def _(v, cc=cc, tn_b=tn_b):
                            p.drain(v, "vc")
                            bx = xsp[cc % 2]
                            for n in range(4):
                                v.wait_ge(p.sems["tn"], tn_b + n + 1)
                                ins = v.tensor_copy(
                                    bx[:, 3 + n * 512:3 + (n + 1) * 512],
                                    ps_w[n % 2][:, :])
                                p.inc(ins, "vc")
                            v.wait_ge(p.sems["vc"], p.cnt["vc"])
                            ins = v.tensor_scalar_mul(ac[0][:, :],
                                                      bx[:, 3:3 + S],
                                                      cwc[:, cc * 4 + 3:
                                                          cc * 4 + 4])
                            p.inc(ins, "vc")
                            for k in (2, 1, 0):
                                v.wait_ge(p.sems["vc"], p.cnt["vc"])
                                src_b = ac[(2 - k) % 2]
                                dst_b = ac[(3 - k) % 2]
                                ins = v.scalar_tensor_tensor(
                                    dst_b[:, :], bx[:, k:k + S],
                                    cwc[:, cc * 4 + k:cc * 4 + k + 1],
                                    src_b[:, :], ALU.mult, ALU.add)
                                p.inc(ins, "vc")

                        @block.scalar
                        def _(s, cc=cc):
                            p.drain(s, "sc")
                            p.wait(s, "vc")
                            ins = s.activation(xcT[:, cc * S:(cc + 1) * S],
                                               ac[1][:, :], AF.Silu,
                                               bias=cbc[:, cc:cc + 1])
                            p.inc(ins, "sc")
                    p.barrier()

                if debug:
                    @block.gpsimd
                    def _(g):
                        p.drain(g, "go")
                        for cc2 in range(16):
                            ins = g.dma_start(
                                out=xc_dram[cc2 * 128:(cc2 + 1) * 128, :],
                                in_=xcT[:, cc2 * S:(cc2 + 1) * S])
                            p.inc(ins, "dmo")
                        g.wait_ge(p.sems["dmo"], p.cnt["dmo"])
                    p.barrier()

            paramsT = p.sbuf("t3paramsT", [33, S], F32, scs)
            ld_bc = p.sbuf("t3ldbc", [128, S], F32, scs)
            with ExitStack() as scp2:
                wxb = p.sbuf("t3wxb", [128, 16 * 33], BF16, scp2)
                ph_load_weight_cast(p, block, wxt_d, 2048, 33, wxb, "wxl")
                p.barrier()
                ps_par = p.psum("t3pspar", [33, 512], F32, scp2)
                for n in range(4):
                    @block.tensor
                    def _(t, n=n):
                        p.drain(t, "tn")
                        ins = None
                        for ic in range(16):
                            ins = t.matmul(
                                ps_par[:, :],
                                lhsT=wxb[:, ic * 33:(ic + 1) * 33],
                                rhs=xcT[:, ic * S + n * 512:
                                        ic * S + (n + 1) * 512],
                                start=(ic == 0), stop=(ic == 15))
                        p.inc(ins, "tn")

                    @block.vector
                    def _(v, n=n):
                        p.drain(v, "vc")
                        p.wait(v, "tn")
                        ins = v.tensor_copy(paramsT[:, n * 512:(n + 1) * 512],
                                            ps_par[:, :])
                        p.inc(ins, "vc")
                    p.barrier()

                @block.gpsimd
                def _(g):
                    p.drain(g, "go")
                    ins = g.dma_start(out=pp_dram, in_=paramsT[:, :])
                    p.inc(ins, "dmo")
                    g.wait_ge(p.sems["dmo"], p.cnt["dmo"])
                p.barrier()

            with ExitStack() as scn:
                brow_b = p.sbuf("t3browb", [1, S], F32, scn)
                brow_c = p.sbuf("t3browc", [1, S], F32, scn)
                dtp = p.sbuf("t3dtp", [128, S], F32, scn)
                dtc = [p.sbuf(f"t3dtc{k}", [128, S], F32, scn)
                       for k in range(2)]
                u = [p.sbuf(f"t3u{k}", [128, S], F32, scn) for k in range(2)]
                yacc = [p.sbuf(f"t3yac{k}", [128, S], F32, scn)
                        for k in range(2)]
                B_bc = p.sbuf("t3Bbc", [128, S], F32, scn)
                C_bc = p.sbuf("t3Cbc", [128, S], F32, scn)
                abar = p.sbuf("t3abar", [128, S], F32, scn)
                but = p.sbuf("t3but", [128, S], F32, scn)
                hbuf = p.sbuf("t3hbuf", [128, S], F32, scn)
                ztile = p.sbuf("t3ztile", [128, S], BF16, scn)
                yvt = ztile
                ps_bc = [p.psum(f"t3psbc{k}", [128, 512], F32, scn)
                         for k in range(2)]

                @block.sync
                def _(sy):
                    p.drain(sy, "sy")
                    ins = sy.dma_start(out=brow_b[:, :], in_=pp_dram[32:33, :])
                    p.inc(ins, "dmi")
                p.barrier()
                tn_b, vc_b = p.cnt["tn"], p.cnt["vc"]

                @block.tensor
                def _(t, vc_b=vc_b):
                    p.drain(t, "tn")
                    for j in range(4):
                        if j >= 2:
                            t.wait_ge(p.sems["vc"], vc_b + j - 1)
                        ins = t.matmul(ps_bc[j % 2][:, :], lhsT=ones1[:, :],
                                       rhs=brow_b[:, j * 512:(j + 1) * 512],
                                       start=True, stop=True)
                        p.inc(ins, "tn")

                @block.vector
                def _(v, tn_b=tn_b):
                    p.drain(v, "vc")
                    for j in range(4):
                        v.wait_ge(p.sems["tn"], tn_b + j + 1)
                        ins = v.tensor_copy(ld_bc[:, j * 512:(j + 1) * 512],
                                            ps_bc[j % 2][:, :])
                        p.inc(ins, "vc")
                p.barrier()

                for grp in range(8):
                    for j in range(2):
                        cc = 2 * grp + j

                        @block.vector
                        def _(v, cc=cc, j=j):
                            p.drain(v, "vc")
                            ins = v.tensor_scalar(dtp[:, :], ld_bc[:, :],
                                                  wdtc[:, cc:cc + 1],
                                                  bdtc[:, cc:cc + 1],
                                                  ALU.mult, ALU.add)
                            p.inc(ins, "vc")

                        @block.scalar
                        def _(s, j=j):
                            p.drain(s, "sc")
                            p.wait(s, "vc")
                            ins = s.activation(dtp[:, :], dtp[:, :], AF.Exp)
                            p.inc(ins, "sc")
                            s.wait_ge(p.sems["sc"], p.cnt["sc"])
                            ins = s.activation(dtc[j][:, :], dtp[:, :], AF.Ln,
                                               bias=ccol[:, 3:4])
                            p.inc(ins, "sc")

                        @block.vector
                        def _(v, cc=cc, j=j):
                            p.drain(v, "vc")
                            p.wait(v, "sc")
                            ins = v.tensor_mul(u[j][:, :], dtc[j][:, :],
                                               xcT[:, cc * S:(cc + 1) * S])
                            p.inc(ins, "vc")
                        p.barrier()

                    for st in range(16):
                        @block.gpsimd
                        def _(g, st=st):
                            p.drain(g, "go")
                            ins = g.dma_start(out=brow_b[:, :],
                                              in_=pp_dram[st:st + 1, :])
                            p.inc(ins, "dmg")
                            ins = g.dma_start(out=brow_c[:, :],
                                              in_=pp_dram[16 + st:17 + st, :])
                            p.inc(ins, "dmg")
                        dmg_n = p.cnt["dmg"]
                        tn_b, vc_b = p.cnt["tn"], p.cnt["vc"]

                        @block.tensor
                        def _(t, dmg_n=dmg_n, vc_b=vc_b):
                            p.drain(t, "tn")
                            t.wait_ge(p.sems["dmg"], dmg_n)
                            for q, brow in ((0, brow_b), (1, brow_c)):
                                for n in range(4):
                                    k = q * 4 + n
                                    if k >= 2:
                                        t.wait_ge(p.sems["vc"], vc_b + k - 1)
                                    ins = t.matmul(
                                        ps_bc[k % 2][:, :], lhsT=ones1[:, :],
                                        rhs=brow[:, n * 512:(n + 1) * 512],
                                        start=True, stop=True)
                                    p.inc(ins, "tn")

                        @block.vector
                        def _(v, tn_b=tn_b):
                            p.drain(v, "vc")
                            for q, dst in ((0, B_bc), (1, C_bc)):
                                for n in range(4):
                                    k = q * 4 + n
                                    v.wait_ge(p.sems["tn"], tn_b + k + 1)
                                    ins = v.tensor_copy(
                                        dst[:, n * 512:(n + 1) * 512],
                                        ps_bc[k % 2][:, :])
                                    p.inc(ins, "vc")
                        p.barrier()

                        for j in range(2):
                            cc = 2 * grp + j

                            @block.scalar
                            def _(s, cc=cc, j=j, st=st):
                                p.drain(s, "sc")
                                if j == 1:
                                    s.wait_ge(p.sems["vc"], p.cnt["vc"] - 2)
                                col = cc * 16 + st
                                ins = s.activation(abar[:, :], dtc[j][:, :],
                                                   AF.Exp,
                                                   scale=aneg[:, col:col + 1])
                                p.inc(ins, "sc")
                            sc_ab = p.cnt["sc"]

                            @block.vector
                            def _(v, j=j, st=st, sc_ab=sc_ab):
                                p.drain(v, "vc")
                                ins = v.tensor_mul(but[:, :], u[j][:, :],
                                                   B_bc[:, :])
                                p.inc(ins, "vc")
                                v.wait_ge(p.sems["vc"], p.cnt["vc"])
                                v.wait_ge(p.sems["sc"], sc_ab)
                                ins = v.tensor_tensor_scan(
                                    hbuf[:, :], abar[:, :], but[:, :],
                                    0.0, ALU.mult, ALU.add)
                                p.inc(ins, "vc")
                                v.wait_ge(p.sems["vc"], p.cnt["vc"])
                                ins = v.tensor_mul(but[:, :], hbuf[:, :],
                                                   C_bc[:, :])
                                p.inc(ins, "vc")
                                v.wait_ge(p.sems["vc"], p.cnt["vc"])
                                if st == 0:
                                    ins = v.tensor_copy(yacc[j][:, :],
                                                        but[:, :])
                                else:
                                    ins = v.tensor_add(yacc[j][:, :],
                                                       yacc[j][:, :],
                                                       but[:, :])
                                p.inc(ins, "vc")
                        p.barrier()

                    for j in range(2):
                        cc = 2 * grp + j

                        @block.sync
                        def _(sy, cc=cc):
                            p.drain(sy, "sy")
                            ins = sy.dma_start(
                                out=ztile[:, :],
                                in_=z_dram[cc * 128:(cc + 1) * 128, :])
                            p.inc(ins, "dmi")
                        dmi_z = p.cnt["dmi"]

                        @block.scalar
                        def _(s, dmi_z=dmi_z):
                            p.drain(s, "sc")
                            s.wait_ge(p.sems["dmi"], dmi_z)
                            ins = s.activation(dtp[:, :], ztile[:, :], AF.Silu)
                            p.inc(ins, "sc")
                        sc_z = p.cnt["sc"]

                        @block.vector
                        def _(v, cc=cc, j=j, sc_z=sc_z):
                            p.drain(v, "vc")
                            ins = v.tensor_scalar_mul(
                                but[:, :], xcT[:, cc * S:(cc + 1) * S],
                                dcol[:, cc:cc + 1])
                            p.inc(ins, "vc")
                            v.wait_ge(p.sems["vc"], p.cnt["vc"])
                            ins = v.tensor_add(yacc[j][:, :], yacc[j][:, :],
                                               but[:, :])
                            p.inc(ins, "vc")
                            v.wait_ge(p.sems["vc"], p.cnt["vc"])
                            v.wait_ge(p.sems["sc"], sc_z)
                            ins = v.tensor_mul(yvt[:, :], yacc[j][:, :],
                                               dtp[:, :])
                            p.inc(ins, "vc")

                        @block.gpsimd
                        def _(g, cc=cc):
                            p.drain(g, "go")
                            p.wait(g, "vc")
                            ins = g.dma_start(
                                out=yv_dram[cc * 128:(cc + 1) * 128, :],
                                in_=yvt[:, :])
                            p.inc(ins, "dmo")
                        p.barrier()

        with ExitStack() as sco:
            woutb = p.sbuf("t3woutb", [128, 16 * D], BF16, sco)
            ph_load_weight_cast(p, block, woutt_d, 2048, 1024, woutb, "woutl",
                                col_block=256)
            yvti = p.sbuf("t3yvti", [128, 16 * 128], BF16, sco)
            xnt = p.sbuf("t3xnt", [128, D], BF16, sco)
            gtt = p.sbuf("t3gtt", [128, D], BF16, sco)
            xtt = p.sbuf("t3xtt", [128, D], F32, sco)
            xnf = p.sbuf("t3xnf", [128, D], F32, sco)
            gff = p.sbuf("t3gff", [128, D], F32, sco)
            s3t = p.sbuf("t3s3t", [128, D], F32, sco)
            ot = p.sbuf("t3ot", [128, D], F32, sco)
            ps_u = [p.psum(f"t3psu{k}", [128, 512], F32, sco)
                    for k in range(2)]
            p.barrier()
            for i in range(NT):
                @block.sync
                def _(sy, i=i):
                    p.drain(sy, "sy")
                    for cc in range(16):
                        ins = sy.dma_start(
                            out=yvti[:, cc * 128:(cc + 1) * 128],
                            in_=yv_dram[cc * 128:(cc + 1) * 128,
                                        i * 128:(i + 1) * 128])
                        p.inc(ins, "dmi")
                    for dst, src in ((xnt, xn_dram), (gtt, g_dram),
                                     (xtt, x_dram)):
                        ins = sy.dma_start(out=dst[:, :],
                                           in_=src[i * 128:(i + 1) * 128, :])
                        p.inc(ins, "dmi")
                dmi_i = p.cnt["dmi"]
                tn_b = p.cnt["tn"]

                @block.tensor
                def _(t, dmi_i=dmi_i):
                    p.drain(t, "tn")
                    t.wait_ge(p.sems["dmi"], dmi_i)
                    for n in range(2):
                        ins = None
                        for cc in range(16):
                            ins = t.matmul(
                                ps_u[n][:, :],
                                lhsT=yvti[:, cc * 128:(cc + 1) * 128],
                                rhs=woutb[:, cc * D + n * 512:
                                          cc * D + (n + 1) * 512],
                                start=(cc == 0), stop=(cc == 15))
                        p.inc(ins, "tn")

                @block.vector
                def _(v, i=i, dmi_i=dmi_i, tn_b=tn_b):
                    p.drain(v, "vc")
                    v.wait_ge(p.sems["dmi"], dmi_i)
                    ins = v.tensor_copy(xnf[:, :], xnt[:, :])
                    p.inc(ins, "vc")
                    ins = v.tensor_copy(gff[:, :], gtt[:, :])
                    p.inc(ins, "vc")
                    for n in range(2):
                        sl = slice(n * 512, (n + 1) * 512)
                        v.wait_ge(p.sems["tn"], tn_b + n + 1)
                        v.wait_ge(p.sems["vc"], p.cnt["vc"])
                        ins = v.tensor_add(s3t[:, sl], ps_u[n][:, :],
                                           xnf[:, sl])
                        p.inc(ins, "vc")
                    v.wait_ge(p.sems["vc"], p.cnt["vc"])
                    ins = v.tensor_mul(s3t[:, :], s3t[:, :], gff[:, :])
                    p.inc(ins, "vc")
                    v.wait_ge(p.sems["vc"], p.cnt["vc"])
                    ins = v.tensor_add(ot[:, :], xtt[:, :], s3t[:, :])
                    p.inc(ins, "vc")

                @block.gpsimd
                def _(g, i=i):
                    p.drain(g, "go")
                    p.wait(g, "vc")
                    ins = g.dma_start(out=x4_dram[i * 128:(i + 1) * 128, :],
                                      in_=ot[:, :])
                    p.inc(ins, "dmo")
                p.barrier()

            @block.gpsimd
            def _(g):
                p.drain(g, "go")
                g.wait_ge(p.sems["dmo"], p.cnt["dmo"])
    p.ctx.close()
    return p


def run_l3(inputs, x3, debug=False):
    key = "l3dbg" if debug else "l3"
    if key not in _CACHE:
        _CACHE[key] = build_l3(debug=debug)
    p = _CACHE[key]
    J = lambda k: np.asarray(inputs[k], np.float32)
    base = {
        "w1T": J("L3_ffn_w1_w").T, "w2T": J("L3_ffn_w2_w").T,
        "h1T": J("L3_ffn_h1_w").T, "h2T": J("L3_ffn_h2_w").T,
        "w3T": J("L3_ffn_w3_w").T,
        "w1sc": J("L3_ffn_w1_s").reshape(32, 128).T,
        "w2sc": J("L3_ffn_w2_s").reshape(32, 128).T,
        "h1sc": J("L3_ffn_h1_s").reshape(32, 128).T,
        "h2sc": J("L3_ffn_h2_s").reshape(32, 128).T,
        "s3row": J("L3_ffn_w3_s").reshape(1, D),
        "winT": J("L3_attn_W_in").T,
        "wxT": J("L3_attn_W_x").T,
        "woutT": J("L3_attn_W_out").T,
        "cw": J("L3_attn_conv_w").reshape(16, 128, 4).transpose(1, 0, 2)
              .reshape(128, 64),
        "cb": J("L3_attn_conv_b").reshape(16, 128).T,
        "wdt": J("L3_attn_W_dt").reshape(16, 128).T,
        "bdt": J("L3_attn_b_dt").reshape(16, 128).T,
        "alog": J("L3_attn_A_log").reshape(16, 128, 16).transpose(1, 0, 2)
                .reshape(128, 256),
        "dsc": J("L3_attn_D").reshape(16, 128).T,
        "g3": J("L3_attn_g").reshape(8, 128).T,
    }
    base.update(l3_consts())
    base = {k: np.ascontiguousarray(np.asarray(v, np.float32))
            for k, v in base.items()}
    in_maps = []
    for c in range(NCORES):
        m = dict(base)
        m["x"] = np.ascontiguousarray(np.asarray(x3[c], np.float32))
        in_maps.append(m)
    res = run_bass_kernel_spmd(p.nc, in_maps, core_ids=list(range(NCORES)))
    if debug:
        return res
    return np.stack([res.results[c]["x4"] for c in range(NCORES)])


def kernel(**inputs):
    x1 = run_l0(inputs)
    r1 = run_l1a(inputs, x1)
    x2 = run_l1b(inputs, x1, r1)
    r2 = run_l2a(inputs, x2)
    x3 = run_l2b(inputs, x2, r2)
    x4 = run_l3(inputs, x3)
    return np.ascontiguousarray(np.asarray(x4, np.float32))
